# version 11
# baseline (speedup 1.0000x reference)
import numpy as np
import ml_dtypes
import concourse.bass as bass
import concourse.mybir as mybir
from concourse.bass_utils import run_bass_kernel_spmd
from contextlib import ExitStack

F32 = mybir.dt.float32
BF16 = mybir.dt.bfloat16
AF = mybir.ActivationFunctionType
ALU = mybir.AluOpType
AX = mybir.AxisListType

T = 2048
D = 1024
NCH = 16
NFC = 8
NEG = -1.0e30
NH_A, DH_A = 16, 64
HEADS_F, DH_F, INNER, WINDOW, HID = 8, 64, 512, 512, 2730
QK_SCALE = 8.0

STAGE = 4


def _bf(x):
    return np.ascontiguousarray(np.asarray(x, np.float32)).astype(ml_dtypes.bfloat16)


def host_constants():
    c = {}
    ii = np.arange(128)
    c["id128_bf"] = _bf(np.eye(128))
    c["triSI_bf"] = _bf((ii[:, None] <= ii[None, :]).astype(np.float32))
    c["triL_f32"] = np.where(ii[None, :] < ii[:, None], np.float32(NEG), np.float32(0)).astype(np.float32)
    c["triR_f32"] = np.where(ii[None, :] > ii[:, None], np.float32(NEG), np.float32(0)).astype(np.float32)
    _s2 = np.stack([np.r_[np.ones(64), np.zeros(64)], np.r_[np.zeros(64), np.ones(64)]])
    c["sel2_bf"] = _bf(_s2)
    c["sel2_f32"] = _s2.astype(np.float32)
    c["ones_row_bf"] = _bf(np.ones((1, 128)))
    c["id16_f32"] = np.eye(16, dtype=np.float32)
    half = np.arange(0, DH_F, 2, dtype=np.float64)
    inv_freq = 1.0 / 10000.0 ** (half / DH_F)
    t = np.arange(T, dtype=np.float64)
    freqs = np.concatenate([t[:, None] * inv_freq[None, :]] * 2, axis=-1)
    base = (half + 0.4 * DH_F) / (1.4 * DH_F)
    power = (t - T // 2) / float(WINDOW // 2)
    scale = np.concatenate([base[None, :] ** power[:, None]] * 2, axis=-1)
    cos, sin = np.cos(freqs), np.sin(freqs)
    sgn = np.concatenate([-np.ones(32), np.ones(32)])[None, :]
    c["qC_bf"] = _bf(cos * scale * QK_SCALE)
    c["qS_bf"] = _bf(sin * scale * sgn * QK_SCALE)
    c["kC_bf"] = _bf(cos / scale)
    c["kS_bf"] = _bf(sin / scale * sgn)
    return c


CONSTS = host_constants()


class Prog:
    ENGS = ("tensor", "vector", "scalar", "sync", "gpsimd")

    def __init__(self, nc, ctx):
        self.nc = nc
        self.ctx = ctx
        self.buf = {e: [] for e in self.ENGS}
        self.semh = {}
        self.cnt = {}
        self.desc = {}

    def sem(self, name):
        if name not in self.semh:
            self.semh[name] = self.ctx.enter_context(self.nc.semaphore(name))
            self.cnt[name] = 0
        return self.semh[name]

    def now(self, name):
        self.sem(name)
        return self.cnt[name]

    def _emit(self, eng, fn, inc, by, waits):
        for w in waits:
            if w is None:
                continue
            (s, v) = w
            if v is None or v == 0:
                continue
            h = self.sem(s)
            self.buf[eng].append(lambda e, h=h, v=v: e.wait_ge(h, v))
        if inc is not None:
            h = self.sem(inc)
            self.cnt[inc] += by
            tgt = self.cnt[inc]
            self.buf[eng].append(lambda e, fn=fn, h=h, by=by: fn(e).then_inc(h, by))
            return tgt
        self.buf[eng].append(fn)
        return None

    def t(self, fn, waits=()):
        return self._emit("tensor", fn, "te", 1, waits)

    def v(self, fn, waits=()):
        return self._emit("vector", fn, "ve", 1, waits)

    def s(self, fn, waits=()):
        return self._emit("scalar", fn, "sc", 1, waits)

    def g(self, fn, waits=()):
        return self._emit("gpsimd", fn, "gp", 1, waits)

    def dma(self, q, out, in_, waits=(), slow=False):
        w = list(waits) + [("q_" + q, self.now("q_" + q))]
        return self._emit(q, lambda e, o=out, i=in_: e.dma_start(
            out=o, in_=i, allow_slow_non_contiguous=slow), "q_" + q, 16, w)

    def dmat(self, out, in_, waits=()):
        w = list(waits) + [("q_sync", self.now("q_sync"))]
        return self._emit("sync", lambda e, o=out, i=in_: e.dma_start_transpose(out=o, in_=i),
                          "q_sync", 16, w)

    def barrier(self, engs=None):
        engs = engs or self.ENGS
        for (e, q) in (("sync", "q_sync"), ("scalar", "q_scalar"), ("gpsimd", "q_gpsimd")):
            if e in engs:
                h = self.sem(q)
                v = self.cnt[q]
                if v:
                    self.buf[e].append(lambda x, h=h, v=v: x.wait_ge(h, v))
        toks = {}
        for e in engs:
            toks[e] = self._emit(e, lambda x: x.drain(), "bar_" + e, 1, ())
        for e in engs:
            for o in engs:
                if o != e:
                    h = self.sem("bar_" + o)
                    self.buf[e].append(lambda x, h=h, v=toks[o]: x.wait_ge(h, v))

    def finalize(self, block):
        for e in self.ENGS:
            lst = self.buf[e]
            getattr(block, e)(lambda eng, lst=lst: [f(eng) for f in lst])


class Arena:
    def __init__(self, tensor, cols):
        self.t = tensor
        self.cols = cols
        self.off = 0
        self.hw = 0
        self.marks = []

    def alloc(self, n, align=16):
        o = (self.off + align - 1) // align * align
        assert o + n <= self.cols, f"arena overflow {o}+{n}>{self.cols}"
        self.off = o + n
        self.hw = max(self.hw, self.off)
        return o

    def push(self):
        self.marks.append(self.off)

    def pop(self):
        self.off = self.marks.pop()

    @property
    def pstride(self):
        return self.t[:, 0:1].ap[0][0]

    def ap(self, off, dims, p=128, base_p=0):
        dd = [[self.pstride, p]] + [list(d) for d in dims]
        a = bass.AP(self.t[:, 0:1].tensor, off, dd)
        if base_p:
            a = bass.AP(a.tensor, off + base_p * 0, dd)
        return a

    def r(self, off, n, p0=0, p1=128):
        return self.t[p0:p1, off:off + n]


def build_program():
    nc = bass.Bass("TRN2", target_bir_lowering=False, debug=False)
    I = {}

    def din(name, shape, dtype=F32):
        I[name] = nc.dram_tensor(name, list(shape), dtype, kind="ExternalInput").ap()

    din("x", (T, D))
    out = nc.dram_tensor("out", [T, D], F32, kind="ExternalOutput").ap()
    wscr = nc.dram_tensor("wscr", [3 * D, D], BF16, kind="Internal").ap()
    bscr = nc.dram_tensor("bscr", [3 * T, D], BF16, kind="Internal").ap()

    din("a_WqT", (D, D)); din("a_WkT", (D, D)); din("a_WvT", (D, D)); din("a_WoT", (D, D))
    din("a_bwT", (D, NH_A))
    din("a_rin", (128, 8)); din("a_rk", (128, 8)); din("a_rv", (128, 8))
    din("a_bo_row", (1, D)); din("a_lnw_bc", (128, D)); din("a_lnb_bc", (128, D))
    din("a_bb_col", (NH_A, 1)); din("a_bbl_col", (NH_A, 1))
    for l in range(2):
        din(f"f_wqkv{l}", (D, 3 * INNER))
        din(f"f_wo{l}", (INNER, D))
        din(f"f_w1{l}", (D, 2 * HID))
        din(f"f_w2a{l}", (HID + 1, D))
        din(f"f_b1a{l}", (128, 22)); din(f"f_b1g{l}", (128, 22))
        din(f"f_lng_row{l}", (1, D)); din(f"f_lnb_row{l}", (1, D))
        din(f"f_qs{l}_bc", (128, 64)); din(f"f_ks{l}_bc", (128, 64))
        din(f"f_qs{l}_sw", (128, 64)); din(f"f_ks{l}_sw", (128, 64))
    din("b_WqT", (D, D)); din("b_WkT", (D, D)); din("b_WvT", (D, D))
    din("b_bq_row", (1, D)); din("b_bk_row", (1, D)); din("b_bv_row", (1, D))
    din("b_g_bc", (128, D)); din("b_b_bc", (128, D))
    for k, v in CONSTS.items():
        din(k, v.shape, BF16 if v.dtype == ml_dtypes.bfloat16 else F32)

    ctx = ExitStack()
    with ctx:
        af_t = ctx.enter_context(nc.sbuf_tensor("af", [128, 21804], F32))
        ab_t = ctx.enter_context(nc.sbuf_tensor("ab", [128, 57480], BF16))
        PS = ctx.enter_context(nc.psum_tensor("PS", [128, 4096], F32))
        A4 = Arena(af_t, 21804)
        A2 = Arena(ab_t, 57480)
        p = Prog(nc, ctx)

        def named(name, shape, dt=F32):
            return ctx.enter_context(nc.sbuf_tensor(name, shape, dt))

        id128 = named("id128", [128, 128], BF16)
        triSI = named("triSI", [128, 128], BF16)
        triL = named("triL", [128, 128], F32)
        triR = named("triR", [128, 128], F32)
        sel2 = named("sel2", [2, 128], BF16)
        sel2f = named("sel2f", [2, 128], F32)
        ones_row = named("ones_rowX", [1, 128], BF16)
        id16 = named("id16", [16, 16], F32)
        ones16c = named("ones16c", [16, 1], F32)
        cvals = named("cvals", [128, 8], F32)

        for (sb, nm) in ((id128, "id128_bf"), (triSI, "triSI_bf"), (triL, "triL_f32"),
                         (triR, "triR_f32"), (sel2, "sel2_bf"), (sel2f, "sel2_f32"),
                         (ones_row, "ones_row_bf"),
                         (id16, "id16_f32")):
            p.dma("scalar", sb[:], I[nm])
        p.v(lambda e: e.memset(ones16c[:], 1.0))
        for (k, val) in enumerate((1e-5, 1e-6, 1.0 / D, -1.0, 1.0, 0.0, 2.0, 0.5)):
            p.v(lambda e, k=k, val=val: e.memset(cvals[:, k:k + 1], val))
            nc.const_aps.register(F32, val, cvals[:, k:k + 1])
        p.barrier()

        def bank(b, p0, p1, c0, n):
            return PS[p0:p1, 512 * b + c0:512 * b + c0 + n]

        BN6 = A4.alloc(12)
        bn6 = af_t[:, BN6:BN6 + 12]
        MV = A4.alloc(32)
        NMU = A4.alloc(16)
        RSG = A4.alloc(16)
        SC0 = A4.alloc(16)
        TSa = A4.alloc(16)
        TSb = A4.alloc(16)
        NMU2 = A4.alloc(16)
        RSG2 = A4.alloc(16)

        def mv_view(which):
            return bass.AP(af_t[:, 0:1].tensor, MV + which, [[A4.pstride, 128], [2, 16]])

        def tm_stats(src_ap, dst2_ap, waits):
            p.v(lambda e: e.bn_stats(out=bn6[:, 0:6], in_=src_ap[:, 0:512]),
                waits=list(waits) + [("ve", p.now("ve"))])
            t2 = p.v(lambda e: e.bn_stats(out=bn6[:, 6:12], in_=src_ap[:, 512:1024]))
            return p.v(lambda e: e.bn_aggr(out=dst2_ap, in_=bn6), waits=[("ve", t2)])

        def stats_rowmath(last_tok, with_rms0, nmu_off=NMU, rsg_off=RSG, c0=0, c1=16):
            n = c1 - c0
            mvv = lambda which: bass.AP(af_t[:, 0:1].tensor, MV + 2 * c0 + which,
                                        [[A4.pstride, 128], [2, n]])
            p.v(lambda e: e.tensor_scalar_mul(out=af_t[:, nmu_off + c0:nmu_off + c1],
                                              in0=mvv(0),
                                              scalar1=-1.0), waits=[("ve", last_tok)])
            a1 = p.s(lambda e: e.activation(out=af_t[:, TSa + c0:TSa + c1], in_=mvv(1),
                                            func=AF.Sqrt,
                                            bias=1e-5), waits=[("ve", last_tok)])
            r1 = p.v(lambda e: e.reciprocal(out=af_t[:, rsg_off + c0:rsg_off + c1],
                                            in_=af_t[:, TSa + c0:TSa + c1]), waits=[("sc", a1)])
            if not with_rms0:
                return r1
            q1 = p.v(lambda e: e.tensor_mul(out=af_t[:, TSa:TSa + 16], in0=af_t[:, rsg_off:rsg_off + 16],
                                            in1=af_t[:, rsg_off:rsg_off + 16]), waits=[("ve", r1)])
            q2 = p.v(lambda e: e.tensor_scalar(out=af_t[:, TSa:TSa + 16], in0=af_t[:, TSa:TSa + 16],
                                               scalar1=-1e-5, scalar2=1.0 + 1e-6,
                                               op0=ALU.mult, op1=ALU.add), waits=[("ve", q1)])
            q3 = p.s(lambda e: e.activation(out=af_t[:, TSb:TSb + 16], in_=af_t[:, TSa:TSa + 16],
                                            func=AF.Sqrt), waits=[("ve", q2)])
            q4 = p.v(lambda e: e.reciprocal(out=af_t[:, TSa:TSa + 16], in_=af_t[:, TSb:TSb + 16]),
                     waits=[("sc", q3)])
            return p.v(lambda e: e.tensor_mul(out=af_t[:, SC0:SC0 + 16],
                                              in0=af_t[:, rsg_off:rsg_off + 16],
                                              in1=af_t[:, TSa:TSa + 16]), waits=[("ve", q4)])

        def emit_attn_a(x_src, x_dst):
            A4.push(); A2.push()
            rin = named("a_rin_sb", [128, 8]); rkv = named("a_rk_sb", [128, 8])
            rvv = named("a_rv_sb", [128, 8])
            l1 = p.dma("scalar", rin[:], I["a_rin"])
            l2 = p.dma("scalar", rkv[:], I["a_rk"])
            l3 = p.dma("scalar", rvv[:], I["a_rv"])
            s1 = p.v(lambda e: e.tensor_mul(out=rkv[:], in0=rkv[:], in1=rin[:]),
                     waits=[("q_scalar", l3)])
            s2 = p.v(lambda e: e.tensor_mul(out=rvv[:], in0=rvv[:], in1=rin[:]), waits=[("ve", s1)])
            WBUF = A2.alloc(2 * NFC * D)
            WTMP = A4.alloc(2 * D)
            prev = None
            st_tok = None
            for (wi, (nm, scl)) in enumerate((("a_WqT", rin), ("a_WkT", rkv), ("a_WvT", rvv))):
                for f in range(NFC):
                    slot = WTMP + D * (f % 2)
                    lw = p.dma("scalar", af_t[:, slot:slot + D], I[nm][128 * f:128 * (f + 1), :],
                               waits=[("ve", prev)])
                    prev = p.v(lambda e, f=f, scl=scl, slot=slot: e.tensor_scalar_mul(
                        out=ab_t[:, WBUF + D * f:WBUF + D * (f + 1)],
                        in0=af_t[:, slot:slot + D], scalar1=scl[:, f:f + 1]),
                        waits=[("q_scalar", lw), ("ve", s2),
                               ("q_gpsimd", st_tok)])
                    st_tok = p.dma("gpsimd", wscr[D * wi + 128 * f:D * wi + 128 * (f + 1), :],
                                   ab_t[:, WBUF + D * f:WBUF + D * (f + 1)],
                                   waits=[("ve", prev)])
            XCH = A4.alloc(4 * D)
            lnt = None
            for qr in range(4):
                ld = p.dma("scalar",
                           bass.AP(af_t[:, 0:1].tensor, XCH, [[A4.pstride, 128], [D, 4], [1, D]]),
                           x_src[512 * qr:512 * (qr + 1), :].rearrange("(j p) d -> p j d", p=128),
                           waits=[("ve", lnt)])
                for j in range(4):
                    c = 4 * qr + j
                    lnt = tm_stats(af_t[:, XCH + D * j:XCH + D * (j + 1)],
                                   af_t[:, MV + 2 * c:MV + 2 * c + 2],
                                   [("q_scalar", ld)] if j == 0 else [])
            sc_tok = stats_rowmath(lnt, True)
            BWT = A2.alloc(NFC * 16)
            for f in range(NFC):
                slot = WTMP + D * (f % 2)
                lb = p.dma("scalar", af_t[:, slot:slot + 16], I["a_bwT"][128 * f:128 * (f + 1), :],
                           waits=[("ve", prev)])
                prev = p.v(lambda e, f=f, slot=slot: e.tensor_scalar_mul(
                    out=ab_t[:, BWT + 16 * f:BWT + 16 * (f + 1)],
                    in0=af_t[:, slot:slot + 16], scalar1=rin[:, f:f + 1]),
                    waits=[("q_scalar", lb)])
            RIN2 = A2.alloc(8)
            for f in range(NFC):
                prev = p.v(lambda e, f=f: e.tensor_mul(out=ab_t[:, RIN2 + f:RIN2 + f + 1],
                                                       in0=rin[:, f:f + 1], in1=rin[:, f:f + 1]),
                           waits=[("q_scalar", l1), ("ve", prev)])
            p.barrier()

            XN0TM = A2.alloc(4 * D)
            XN0FM = A2.alloc(NFC * 512)

            def make_xn0(qr):
                ld = p.dma("scalar",
                           bass.AP(af_t[:, 0:1].tensor, XCH, [[A4.pstride, 128], [D, 4], [1, D]]),
                           x_src[512 * qr:512 * (qr + 1), :].rearrange("(j p) d -> p j d", p=128),
                           waits=[("ve", p.now("ve"))])
                tvs = []
                for j in range(4):
                    c = 4 * qr + j
                    tvs.append(p.v(lambda e, c=c, j=j: e.tensor_scalar(
                        out=ab_t[:, XN0TM + D * j:XN0TM + D * (j + 1)],
                        in0=af_t[:, XCH + D * j:XCH + D * (j + 1)],
                        scalar1=af_t[:, NMU + c:NMU + c + 1], scalar2=af_t[:, SC0 + c:SC0 + c + 1],
                        op0=ALU.add, op1=ALU.mult), waits=[("q_scalar", ld)] if j == 0 else ()))
                in_ap = bass.AP(ab_t[:, 0:1].tensor, XN0TM, [[A2.pstride, 128], [128, 32], [1, 128]])
                out_ap = bass.AP(ab_t[:, 0:1].tensor, XN0FM, [[A2.pstride, 128], [128, 32], [1, 128]])
                return p.dmat(out_ap, in_ap, waits=[("ve", tvs[-1]), ("te", p.now("te"))])

            def xn0fm(f):
                return bass.AP(ab_t[:, 0:1].tensor, XN0FM + 128 * f,
                               [[A2.pstride, 128], [D, 4], [1, 128]])

            def xn0blk(tt, f):
                return ab_t[:, XN0FM + D * tt + 128 * f:XN0FM + D * tt + 128 * (f + 1)]

            XSQ = A2.alloc(512)
            for qr in range(4):
                tq = make_xn0(qr)
                sqv = None
                for f in range(NFC):
                    p.t(lambda e, qr=qr, f=f: e.matmul(
                        out=bank(qr, 0, 16, 0, 512),
                        lhsT=ab_t[:, BWT + 16 * f:BWT + 16 * (f + 1)],
                        rhs=xn0fm(f), start=(f == 0), stop=(f == NFC - 1)),
                        waits=[("q_sync", tq)] if f == 0 else ())
                    sqv = p.s(lambda e, f=f: e.activation(out=ab_t[:, XSQ:XSQ + 512], in_=xn0fm(f),
                                                          func=AF.Square),
                              waits=[("q_sync", tq), ("te", p.now("te"))])
                    p.t(lambda e, qr=qr, f=f: e.matmul(
                        out=bank(4 + qr, 0, 1, 0, 512),
                        lhsT=ab_t[:, RIN2 + f:RIN2 + f + 1],
                        rhs=ab_t[:, XSQ:XSQ + 512], start=(f == 0), stop=(f == NFC - 1)),
                        waits=[("sc", sqv)])
            mm_all = p.now("te")

            BET = A4.alloc(T); LNB = A4.alloc(T); DEN = A4.alloc(T); CCS = A4.alloc(T)
            GRW = A4.alloc(T); FQr = A4.alloc(T)
            EIr = BET
            EPr = LNB
            RM2 = A4.alloc(T)
            bbc = named("a_bb_sb", [16, 1]); bbl = named("a_bbl_sb", [16, 1])
            sbase = named("a_sbase", [16, 1])
            l4 = p.dma("scalar", bbc[:], I["a_bb_col"])
            l5 = p.dma("scalar", bbl[:], I["a_bbl_col"])
            t1 = p.s(lambda e: e.activation(out=af_t[0:16, BET:BET + T], in_=PS[0:16, 0:2048],
                                            func=AF.Sigmoid, bias=bbc[:]),
                     waits=[("te", mm_all), ("q_scalar", l4)])
            t2 = p.s(lambda e: e.activation(out=sbase[:], in_=bbl[:], func=AF.Sigmoid),
                     waits=[("q_scalar", l5)])
            t3 = p.v(lambda e: e.tensor_scalar_mul(out=af_t[0:16, BET:BET + T],
                                                   in0=af_t[0:16, BET:BET + T], scalar1=sbase[:]),
                     waits=[("sc", t1), ("sc", t2)])
            t4 = p.v(lambda e: e.tensor_scalar(out=af_t[0:16, BET:BET + T],
                                               in0=af_t[0:16, BET:BET + T],
                                               scalar1=BETA_MIN_, scalar2=BETA_MAX_,
                                               op0=ALU.max, op1=ALU.min), waits=[("ve", t3)])
            t5 = p.s(lambda e: e.activation(out=af_t[0:16, LNB:LNB + T], in_=af_t[0:16, BET:BET + T],
                                            func=AF.Ln), waits=[("ve", t4)])
            t6 = p.v(lambda e: e.tensor_tensor_scan(out=af_t[0:16, DEN:DEN + T],
                                                    data0=af_t[0:16, BET:BET + T],
                                                    data1=ones16c[:].broadcast_to((16, T)),
                                                    initial=0.0, op0=ALU.mult, op1=ALU.add),
                     waits=[("ve", t4)])
            t7 = p.v(lambda e: e.tensor_scalar_add(out=af_t[0:16, DEN:DEN + T],
                                                   in0=af_t[0:16, DEN:DEN + T], scalar1=1e-6),
                     waits=[("ve", t6)])
            t8 = p.v(lambda e: e.reciprocal(out=af_t[0:16, DEN:DEN + T], in_=af_t[0:16, DEN:DEN + T]),
                     waits=[("ve", t7)])
            tc = None
            for c in range(NCH):
                tc = p.v(lambda e, c=c: e.tensor_tensor_scan(
                    out=af_t[0:16, CCS + 128 * c:CCS + 128 * (c + 1)],
                    data0=af_t[0:16, LNB + 128 * c:LNB + 128 * (c + 1)],
                    data1=af_t[0:16, LNB + 128 * c:LNB + 128 * (c + 1)],
                    initial=0.0, op0=ALU.add, op1=ALU.bypass),
                    waits=[("sc", t5)] if c == 0 else ())
            t9 = p.s(lambda e: e.activation(out=af_t[0:16, GRW:GRW + T], in_=af_t[0:16, CCS:CCS + T],
                                            func=AF.Exp), waits=[("ve", tc)])
            t10 = p.v(lambda e: e.tensor_mul(out=af_t[0:16, FQr:FQr + T], in0=af_t[0:16, GRW:GRW + T],
                                             in1=af_t[0:16, DEN:DEN + T]),
                      waits=[("sc", t9), ("ve", t8)])
            t11 = p.s(lambda e: e.activation(out=af_t[0:16, EIr:EIr + T], in_=af_t[0:16, CCS:CCS + T],
                                             func=AF.Exp, scale=-1.0), waits=[("ve", tc)])
            clast = bass.AP(af_t[:, 0:1].tensor, CCS + 127, [[A4.pstride, 16], [128, 16], [0, 128]])
            t12 = p.v(lambda e: e.tensor_sub(
                out=af_t[0:16, EPr:EPr + T].rearrange("p (a b) -> p a b", a=16),
                in0=clast,
                in1=af_t[0:16, CCS:CCS + T].rearrange("p (a b) -> p a b", a=16)),
                waits=[("ve", tc)])
            t13 = p.s(lambda e: e.activation(out=af_t[0:16, EPr:EPr + T], in_=af_t[0:16, EPr:EPr + T],
                                             func=AF.Exp), waits=[("ve", t12)])
            t14 = p.s(lambda e: e.activation(out=af_t[0:1, RM2:RM2 + T], in_=PS[0:1, 2048:4096],
                                             func=AF.Sqrt, bias=1e-6, scale=1.0 / D),
                      waits=[("te", mm_all)])
            t15 = p.v(lambda e: e.reciprocal(out=af_t[0:1, RM2:RM2 + T], in_=af_t[0:1, RM2:RM2 + T]),
                      waits=[("sc", t14)])
            p.barrier()
            FQT = A4.alloc(256); EIT = A4.alloc(256); EPT = A4.alloc(256); RM2T = A4.alloc(16)
            GBIG = A4.alloc(128)
            for (src, dst, bnk) in ((FQr, FQT, 0), (EIr, EIT, 1), (EPr, EPT, 2)):
                for c in range(NCH):
                    p.t(lambda e, src=src, c=c, bnk=bnk: e.matmul(
                        out=bank(bnk, 0, 128, 16 * c, 16),
                        lhsT=af_t[0:16, src + 128 * c:src + 128 * (c + 1)],
                        rhs=id16[:], start=True, stop=True))
            for c in range(NCH):
                p.t(lambda e, c=c: e.matmul(
                    out=bank(3, 0, 128, c, 1),
                    lhsT=af_t[0:1, RM2 + 128 * c:RM2 + 128 * (c + 1)],
                    rhs=id16[0:1, 0:1], start=True, stop=True))
            tmm = p.now("te")
            e1 = p.v(lambda e: e.tensor_copy(out=af_t[:, FQT:FQT + 256], in_=PS[:, 0:256]),
                     waits=[("te", tmm)])
            e2 = p.v(lambda e: e.tensor_copy(out=af_t[:, EIT:EIT + 256], in_=PS[:, 512:768]))
            e3 = p.v(lambda e: e.tensor_copy(out=af_t[:, EPT:EPT + 256], in_=PS[:, 1024:1280]))
            e4 = p.v(lambda e: e.tensor_copy(out=af_t[:, RM2T:RM2T + 16], in_=PS[:, 1536:1552]))
            for c in range(NCH):
                p.v(lambda e, c=c: e.tensor_scalar_mul(out=af_t[:, EIT + 16 * c:EIT + 16 * (c + 1)],
                                                       in0=af_t[:, EIT + 16 * c:EIT + 16 * (c + 1)],
                                                       scalar1=af_t[:, RM2T + c:RM2T + c + 1]),
                    waits=[("ve", e4)] if c == 0 else ())
                p.v(lambda e, c=c: e.tensor_scalar_mul(out=af_t[:, EPT + 16 * c:EPT + 16 * (c + 1)],
                                                       in0=af_t[:, EPT + 16 * c:EPT + 16 * (c + 1)],
                                                       scalar1=af_t[:, RM2T + c:RM2T + c + 1]))
            GST = A4.alloc(128)
            rst = []
            for hp in range(8):
                src_ap = bass.AP(af_t[:, 0:1].tensor, GRW + 127 + A4.pstride * 2 * hp,
                                 [[A4.pstride, 2], [128, 16]])
                rst.append(p.dma("sync", af_t[0:2, GST + 16 * hp:GST + 16 * (hp + 1)], src_ap,
                                 waits=[("sc", p.now("sc")), ("ve", p.now("ve"))] if hp == 0 else (),
                                 slow=True))
            tg = p.t(lambda e: e.matmul(out=bank(4, 0, 128, 0, 128), lhsT=sel2f[:],
                                        rhs=af_t[0:2, GST:GST + 128], start=True, stop=True),
                     waits=[("q_sync", rst[-1])])
            p.v(lambda e: e.tensor_copy(out=af_t[:, GBIG:GBIG + 128], in_=PS[:, 2048:2176]),
                waits=[("te", tg)])
            p.barrier()

            QTM = A2.alloc(4 * D)
            KRTM = A2.alloc(4 * D)
            KSTM = A2.alloc(4 * D)
            VPTM = A2.alloc(4 * D)
            QFM = A2.alloc(NFC * 512)
            KFM = A2.alloc(NFC * 512)
            OFM = A2.alloc(NFC * 512)
            TTB = A2.alloc(16 * 128)
            MPADe = A2.alloc(512)
            MPADo = A2.alloc(512)
            MF = DEN
            MTMP = DEN + 512
            XIN = FQr
            XIN2 = FQr + D
            YY = GRW
            YT = CCS
            lnw_bc = named("a_lnw_sb", [128, D], BF16)
            lnb_bc = named("a_lnb_sb", [128, D], BF16)
            bo_row = named("a_bo_sb", [1, D], BF16)
            p.dma("gpsimd", lnw_bc[:], I["a_lnw_bc"])
            p.dma("gpsimd", lnb_bc[:], I["a_lnb_bc"])
            p.dma("gpsimd", bo_row[:], I["a_bo_row"])
            mz1 = p.v(lambda e: e.memset(ab_t[:, MPADe:MPADe + 512], 0.0))
            mz2 = p.v(lambda e: e.memset(ab_t[:, MPADo:MPADo + 512], 0.0))
            mz3 = p.v(lambda e: e.memset(af_t[:, MF:MF + 512], 0.0))
            p.barrier()

            wmm_hist = {}
            for qr in range(4):
                tq = make_xn0(qr)
                for (wi, kind) in ((0, "q"), (1, "k"), (2, "v")):
                    wslot = WBUF + (NFC * D) * (wi % 2)
                    wld = p.dma("gpsimd",
                                bass.AP(ab_t[:, 0:1].tensor, wslot,
                                        [[A2.pstride, 128], [D, NFC], [1, D]]),
                                wscr[D * wi:D * (wi + 1), :].rearrange("(f p) d -> p f d", p=128),
                                waits=[("te", wmm_hist.get(wi % 2, 0))])
                    for tt in range(4):
                        c = 4 * qr + tt
                        mmt = None
                        for oseg in range(2):
                            for f in range(NFC):
                                mmt = p.t(lambda e, oseg=oseg, f=f, tt=tt, wslot=wslot: e.matmul(
                                    out=bank(oseg, 0, 128, 0, 512),
                                    lhsT=xn0blk(tt, f),
                                    rhs=ab_t[:, wslot + D * f + 512 * oseg:wslot + D * f + 512 * (oseg + 1)],
                                    start=(f == 0), stop=(f == NFC - 1)),
                                    waits=([("q_sync", tq), ("q_gpsimd", wld),
                                            ("ve", p.now("ve")), ("sc", p.now("sc"))]
                                           if (f == 0 and oseg == 0) else ()))
                        if kind == "q":
                            p.v(lambda e, tt=tt, c=c: e.tensor_tensor(
                                out=ab_t[:, QTM + D * tt:QTM + D * (tt + 1)].rearrange(
                                    "p (a b) -> p a b", a=16),
                                in0=PS[:, 0:1024].rearrange("p (a b) -> p a b", a=16),
                                in1=bass.AP(af_t[:, 0:1].tensor, FQT + 16 * c,
                                            [[A4.pstride, 128], [1, 16], [0, 64]]),
                                op=ALU.mult), waits=[("te", mmt)])
                        elif kind == "k":
                            krc = p.s(lambda e, tt=tt: e.copy(
                                out=ab_t[:, KRTM + D * tt:KRTM + D * (tt + 1)],
                                in_=PS[:, 0:1024]), waits=[("te", mmt)])
                            p.v(lambda e, tt=tt, c=c: e.tensor_tensor(
                                out=ab_t[:, KSTM + D * tt:KSTM + D * (tt + 1)].rearrange(
                                    "p (a b) -> p a b", a=16),
                                in0=PS[:, 0:1024].rearrange("p (a b) -> p a b", a=16),
                                in1=bass.AP(af_t[:, 0:1].tensor, EPT + 16 * c,
                                            [[A4.pstride, 128], [1, 16], [0, 64]]),
                                op=ALU.mult), waits=[("te", mmt), ("sc", krc)])
                        else:
                            p.v(lambda e, tt=tt, c=c: e.tensor_scalar_mul(
                                out=ab_t[:, VPTM + D * tt:VPTM + D * (tt + 1)],
                                in0=PS[:, 0:1024],
                                scalar1=af_t[:, RM2T + c:RM2T + c + 1]), waits=[("te", mmt)])
                    wmm_hist[wi % 2] = p.now("te")
                sc_now = p.now("sc")
                ve_now = p.now("ve")
                trq = []
                for (SRC, DST) in ((QTM, QFM), (KRTM, KFM)):
                    in_ap = bass.AP(ab_t[:, 0:1].tensor, SRC, [[A2.pstride, 128], [128, 32], [1, 128]])
                    out_ap = bass.AP(ab_t[:, 0:1].tensor, DST, [[A2.pstride, 128], [128, 32], [1, 128]])
                    trq.append(p.dmat(out_ap, in_ap,
                                      waits=([("ve", ve_now), ("sc", sc_now)]
                                             if SRC == QTM else ())))
                tr_done = trq[-1]

                for tt in range(4):
                    c = 4 * qr + tt
                    st_toks = {}
                    for hp in range(8):
                        for par in range(2):
                            h = 2 * hp + par
                            sb_ = 6 + (h % 2)
                            mm = p.t(lambda e, hp=hp, par=par, tt=tt, sb_=sb_: e.matmul(
                                out=bank(sb_, 0, 128, 0, 128),
                                lhsT=ab_t[64 * par:64 * par + 64,
                                          KFM + D * tt + 128 * hp:KFM + D * tt + 128 * (hp + 1)],
                                rhs=ab_t[64 * par:64 * par + 64,
                                         QFM + D * tt + 128 * hp:QFM + D * tt + 128 * (hp + 1)],
                                start=True, stop=True),
                                waits=([("q_sync", tr_done), ("ve", p.now("ve"))] if h == 0 else
                                       [("ve", st_toks.get(h - 2, None))]))
                            st_toks[h] = p.v(lambda e, h=h, c=c, sb_=sb_: e.scalar_tensor_tensor(
                                out=ab_t[:, TTB + 128 * h:TTB + 128 * (h + 1)],
                                in0=bank(sb_, 0, 128, 0, 128),
                                scalar=af_t[:, EIT + 16 * c + h:EIT + 16 * c + h + 1],
                                in1=triSI[:], op0=ALU.mult, op1=ALU.mult),
                                waits=[("te", mm)])
                    otok = None
                    for hp in range(8):
                        for par in range(2):
                            h = 2 * hp + par
                            tp = (0, 64 * par)
                            MP = MPADe if par == 0 else MPADo
                            if c > 0:
                                p.t(lambda e, hp=hp, par=par, tt=tt, tp=tp, MP=MP: e.matmul(
                                    out=PS[64 * par:64 * par + 64, 128 * hp:128 * hp + 128],
                                    lhsT=ab_t[:, MP + 64 * hp:MP + 64 * (hp + 1)],
                                    rhs=ab_t[:, QFM + D * tt + 128 * hp:QFM + D * tt + 128 * (hp + 1)],
                                    start=True, stop=False, tile_position=tp),
                                    waits=[("ve", st_toks[h])] if True else ())
                            otok = p.t(lambda e, hp=hp, par=par, tt=tt, h=h, tp=tp, c=c: e.matmul(
                                out=PS[64 * par:64 * par + 64, 128 * hp:128 * hp + 128],
                                lhsT=ab_t[:, VPTM + D * tt + 64 * h:VPTM + D * tt + 64 * (h + 1)],
                                rhs=ab_t[:, TTB + 128 * h:TTB + 128 * (h + 1)],
                                start=(c == 0), stop=True, tile_position=tp),
                                waits=[("ve", st_toks[h])])
                            p.t(lambda e, hp=hp, par=par, tt=tt, h=h, tp=tp: e.matmul(
                                out=PS[64 * par:64 * par + 64, 1024 + 64 * hp:1024 + 64 * (hp + 1)],
                                lhsT=ab_t[:, KSTM + D * tt + 64 * h:KSTM + D * tt + 64 * (h + 1)],
                                rhs=ab_t[:, VPTM + D * tt + 64 * h:VPTM + D * tt + 64 * (h + 1)],
                                start=True, stop=True, tile_position=tp))
                    kv_tok = p.now("te")
                    oev = p.s(lambda e, tt=tt: e.copy(
                        out=bass.AP(ab_t[:, 0:1].tensor, OFM + 128 * tt,
                                    [[A2.pstride, 128], [512, 8], [1, 128]]),
                        in_=PS[:, 0:1024].rearrange("p (a b) -> p a b", a=8)),
                        waits=[("te", kv_tok)])
                    u1 = p.v(lambda e, c=c: e.tensor_tensor(
                        out=af_t[:, MTMP:MTMP + 512], in0=af_t[:, MF:MF + 512],
                        in1=bass.AP(af_t[:, 0:1].tensor, GBIG + c,
                                    [[A4.pstride, 128], [16, 8], [0, 64]]),
                        op=ALU.mult), waits=[("te", kv_tok), ("ve", mz3)])
                    u2 = p.v(lambda e: e.tensor_add(out=af_t[:, MF:MF + 512],
                                                    in0=af_t[:, MTMP:MTMP + 512],
                                                    in1=PS[:, 1024:1536]), waits=[("ve", u1)])
                    u3 = p.v(lambda e: e.tensor_copy(out=ab_t[0:64, MPADe:MPADe + 512],
                                                     in_=af_t[0:64, MF:MF + 512]),
                             waits=[("ve", u2), ("ve", mz1)])
                    u4 = p.v(lambda e: e.tensor_copy(out=ab_t[64:128, MPADo:MPADo + 512],
                                                     in_=af_t[64:128, MF:MF + 512]),
                             waits=[("ve", u2), ("ve", mz2)])
                    p.t(lambda e: e.nop(), waits=[("ve", u4), ("sc", oev)])

                wold = p.dma("gpsimd",
                             bass.AP(ab_t[:, 0:1].tensor, WBUF + NFC * D,
                                     [[A2.pstride, 128], [D, NFC], [1, D]]),
                             I["a_WoT"].rearrange("(f p) d -> p f d", p=128),
                             waits=[("te", wmm_hist.get(1, 0))])
                for tt in range(4):
                    c = 4 * qr + tt
                    mmt = None
                    for oseg in range(2):
                        for f in range(NFC):
                            mmt = p.t(lambda e, oseg=oseg, f=f, tt=tt: e.matmul(
                                out=bank(2 + oseg, 0, 128, 0, 512),
                                lhsT=ab_t[:, OFM + 512 * f + 128 * tt:OFM + 512 * f + 128 * (tt + 1)],
                                rhs=ab_t[:, WBUF + NFC * D + D * f + 512 * oseg:WBUF + NFC * D + D * f + 512 * (oseg + 1)],
                                start=(f == 0), stop=False),
                                waits=(([("q_gpsimd", wold)] if tt == 0 else [])
                                       + ([("ve", p.now("ve")), ("sc", p.now("sc"))]
                                          if (f == 0 and oseg == 0) else [])))
                        mmt = p.t(lambda e, oseg=oseg, tt=tt: e.matmul(
                            out=bank(2 + oseg, 0, 128, 0, 512),
                            lhsT=ones_row[0:1, 0:128],
                            rhs=bo_row[0:1, 512 * oseg:512 * (oseg + 1)],
                            start=False, stop=True))
                    ld = p.dma("scalar", af_t[:, XIN:XIN + D], x_src[128 * c:128 * (c + 1), :],
                               waits=[("ve", p.now("ve"))])
                    w1_ = p.v(lambda e, c=c: e.tensor_scalar(
                        out=af_t[:, XIN2:XIN2 + D], in0=af_t[:, XIN:XIN + D],
                        scalar1=af_t[:, NMU + c:NMU + c + 1], scalar2=af_t[:, RSG + c:RSG + c + 1],
                        op0=ALU.add, op1=ALU.mult), waits=[("q_scalar", ld)])
                    w2_ = p.v(lambda e: e.tensor_add(out=af_t[:, YY:YY + D],
                                                     in0=af_t[:, XIN2:XIN2 + D], in1=PS[:, 1024:2048]),
                              waits=[("ve", w1_), ("te", mmt)])
                    bnt = tm_stats(af_t[:, YY:YY + D], af_t[:, MV + 2 * c:MV + 2 * c + 2],
                                   [("ve", w2_)])
                    a1 = p.s(lambda e, c=c: e.activation(out=af_t[:, TSa:TSa + 1],
                                                         in_=af_t[:, MV + 2 * c + 1:MV + 2 * c + 2],
                                                         func=AF.Sqrt, bias=1e-5),
                             waits=[("ve", bnt)])
                    a2 = p.v(lambda e: e.reciprocal(out=af_t[:, TSb:TSb + 1], in_=af_t[:, TSa:TSa + 1]),
                             waits=[("sc", a1)])
                    a3 = p.v(lambda e, c=c: e.tensor_scalar(
                        out=af_t[:, YT:YT + D], in0=af_t[:, YY:YY + D],
                        scalar1=af_t[:, MV + 2 * c:MV + 2 * c + 1], scalar2=af_t[:, TSb:TSb + 1],
                        op0=ALU.subtract, op1=ALU.mult),
                        waits=[("ve", a2), ("q_scalar", p.now("q_scalar"))])
                    a4 = p.v(lambda e: e.tensor_mul(out=af_t[:, YT:YT + D], in0=af_t[:, YT:YT + D],
                                                    in1=lnw_bc[:]), waits=[("ve", a3)])
                    a5 = p.v(lambda e: e.tensor_add(out=af_t[:, YT:YT + D], in0=af_t[:, YT:YT + D],
                                                    in1=lnb_bc[:]), waits=[("ve", a4)])
                    a6 = p.v(lambda e: e.tensor_add(out=af_t[:, YT:YT + D], in0=af_t[:, YT:YT + D],
                                                    in1=af_t[:, XIN:XIN + D]),
                             waits=[("ve", a5)])
                    p.dma("scalar", x_dst[128 * c:128 * (c + 1), :], af_t[:, YT:YT + D],
                          waits=[("ve", a6)])
                wmm_hist[1] = p.now("te")
            p.barrier()
            A4.pop(); A2.pop()

        BETA_MIN_, BETA_MAX_ = 0.9, 0.9995

        def emit_ffn(x_src, x_dst, nm_sfx):
            A4.push(); A2.push()
            XF = A4.alloc(16 * D)
            WTMP0 = A4.alloc(4 * D)
            RNC = A4.alloc(16)
            NM_ = A4.alloc(8)
            B1A = A4.alloc(22)
            B1G = A4.alloc(22)

            def xf(c):
                return af_t[:, XF + D * c:XF + D * (c + 1)]

            lnt = None
            app = {}
            for ehalf in (0, 1):
                for qr in (2 * ehalf, 2 * ehalf + 1):
                    ld = p.dma("scalar",
                               bass.AP(af_t[:, 0:1].tensor, XF + 4 * D * qr,
                                       [[A4.pstride, 128], [D, 4], [1, D]]),
                               x_src[512 * qr:512 * (qr + 1), :].rearrange("(j p) d -> p j d", p=128))
                    for j in range(4):
                        c = 4 * qr + j
                        lnt = tm_stats(xf(c), af_t[:, MV + 2 * c:MV + 2 * c + 2],
                                       [("q_scalar", ld)] if j == 0 else [])
                rm_e = stats_rowmath(lnt, False, c0=8 * ehalf, c1=8 * ehalf + 8)
                ap_last = None
                for c in range(8 * ehalf, 8 * ehalf + 8):
                    ap_last = p.v(lambda e, c=c: e.tensor_scalar(
                        out=xf(c), in0=xf(c),
                        scalar1=af_t[:, NMU + c:NMU + c + 1], scalar2=af_t[:, RSG + c:RSG + c + 1],
                        op0=ALU.add, op1=ALU.mult),
                        waits=[("ve", rm_e)] if c == 8 * ehalf else ())
                app[ehalf] = ap_last

            def build_shift_full(XBA, XF0, HSH, c0=0, c1=NCH, extra0=()):
                n = c1 - c0
                w0 = list(extra0) + [("q_sync", p.now("q_sync")),
                                     ("q_scalar", p.now("q_scalar"))]
                dA = p.dma("gpsimd",
                           bass.AP(ab_t[:, 0:1].tensor, XBA + 512 * c0,
                                   [[A2.pstride, 128], [512, n], [1, 512]]),
                           bass.AP(af_t[:, 0:1].tensor, XF0 + D * c0,
                                   [[A4.pstride, 128], [D, n], [1, 512]]), waits=w0)
                mz = None
                if c0 == 0:
                    mz = p.v(lambda e: e.memset(ab_t[0:1, HSH:HSH + 512], 0.0),
                             waits=list(extra0) + [("q_gpsimd", p.now("q_gpsimd")),
                                    ("q_scalar", p.now("q_scalar")),
                                    ("q_sync", p.now("q_sync"))])
                p.dma("gpsimd",
                      bass.AP(ab_t[:, 0:1].tensor, HSH + 512 * c0 + A2.pstride,
                              [[A2.pstride, 64], [512, n], [1, 512]]),
                      bass.AP(af_t[:, 0:1].tensor, XF0 + D * c0 + 512,
                              [[A4.pstride, 64], [D, n], [1, 512]]))
                dB = p.dma("gpsimd",
                           bass.AP(ab_t[:, 0:1].tensor, HSH + 512 * c0 + 65 * A2.pstride,
                                   [[A2.pstride, 63], [512, n], [1, 512]]),
                           bass.AP(af_t[:, 0:1].tensor, XF0 + D * c0 + 512 + 64 * A4.pstride,
                                   [[A4.pstride, 63], [D, n], [1, 512]]))
                cb0 = max(c0, 1)
                dC = p.dma("gpsimd",
                           bass.AP(ab_t[:, 0:1].tensor, HSH + 512 * cb0,
                                   [[A2.pstride, 1], [512, c1 - cb0], [1, 512]]),
                           bass.AP(af_t[:, 0:1].tensor, XF0 + 512 + D * (cb0 - 1) + A4.pstride * 127,
                                   [[A4.pstride, 1], [D, c1 - cb0], [1, 512]]))
                return dA, dC, mz

            def shift_builder(stage_off, hsh_off, consume, extra0=()):
                cts = {}
                dms = {}
                for c in range(NCH):
                    sl = stage_off + D * (c % 2)
                    w = list(extra0) + [("ve", p.now("ve")), ("q_sync", p.now("q_sync")),
                                        ("q_scalar", p.now("q_scalar")),
                                        ("te", p.now("te")), ("sc", p.now("sc"))] if c == 0 else \
                        [("q_sync", p.now("q_sync")), ("q_scalar", p.now("q_scalar"))]
                    cts[c] = p.v(lambda e, c=c, sl=sl: e.tensor_copy(out=ab_t[:, sl:sl + D],
                                                                    in_=xf(c)), waits=w)
                    if c == 0:
                        mz = p.v(lambda e: e.memset(ab_t[0:1, hsh_off:hsh_off + 512], 0.0))
                        d1 = p.dma("scalar", bass.AP(ab_t[:, 0:1].tensor, hsh_off + A2.pstride,
                                                     [[A2.pstride, 127], [1, 512]]),
                                   bass.AP(ab_t[:, 0:1].tensor, sl + 512,
                                           [[A2.pstride, 127], [1, 512]]),
                                   waits=[("ve", cts[0])])
                    else:
                        d1 = p.dma("scalar", bass.AP(ab_t[:, 0:1].tensor, hsh_off + A2.pstride,
                                                     [[A2.pstride, 127], [1, 512]]),
                                   bass.AP(ab_t[:, 0:1].tensor, sl + 512,
                                           [[A2.pstride, 127], [1, 512]]),
                                   waits=[("ve", cts[c])])
                        d1 = p.dma("scalar", ab_t[0:1, hsh_off:hsh_off + 512],
                                   bass.AP(ab_t[:, 0:1].tensor,
                                           stage_off + D * ((c - 1) % 2) + 512 + A2.pstride * 127,
                                           [[A2.pstride, 1], [1, 512]]))
                    dms[c] = consume(c, sl, hsh_off, [("q_scalar", d1), ("ve", cts[c])] +
                                     ([("ve", mz)] if c == 0 else []))
                return dms

            HHALF = NCH * 512
            for l in range(2):
                A2.push()
                QFM = A2.alloc(NCH * 512)
                KFM = A2.alloc(NCH * 512)
                VTM = A2.alloc(NCH * 512)
                QSB = A2.alloc(128)
                A2.push()
                HFM = A2.alloc(NCH * D)
                A2.push()
                XBA = A2.alloc(NCH * 512)
                HSH1 = A2.alloc(NCH * 512)
                if l == 0:
                    e0a, e0b = [("ve", app[0])], [("ve", app[1])]
                else:
                    e0a = e0b = [("ve", fadd_last)]
                sh_ct, sh_d2, sh_mz = build_shift_full(XBA, XF, HSH1, 0, 8, extra0=e0a)
                sh_ct1, sh_d21, _ = build_shift_full(XBA, XF, HSH1, 8, NCH, extra0=e0b)
                trA = p.dmat(bass.AP(ab_t[:, 0:1].tensor, HFM,
                                     [[A2.pstride, 128], [128, 32], [1, 128]]),
                             bass.AP(ab_t[:, 0:1].tensor, XBA,
                                     [[A2.pstride, 128], [128, 32], [1, 128]]),
                             waits=[("q_gpsimd", sh_ct), ("te", p.now("te"))])
                trB = p.dmat(bass.AP(ab_t[:, 0:1].tensor, HFM + HHALF,
                                     [[A2.pstride, 128], [128, 32], [1, 128]]),
                             bass.AP(ab_t[:, 0:1].tensor, HSH1,
                                     [[A2.pstride, 128], [128, 32], [1, 128]]),
                             waits=[("q_gpsimd", sh_d2), ("ve", sh_mz)])
                trA1 = p.dmat(bass.AP(ab_t[:, 0:1].tensor, HFM + 4096,
                                      [[A2.pstride, 128], [128, 32], [1, 128]]),
                              bass.AP(ab_t[:, 0:1].tensor, XBA + 4096,
                                      [[A2.pstride, 128], [128, 32], [1, 128]]),
                              waits=[("q_gpsimd", sh_ct1)])
                trB1 = p.dmat(bass.AP(ab_t[:, 0:1].tensor, HFM + HHALF + 4096,
                                      [[A2.pstride, 128], [128, 32], [1, 128]]),
                              bass.AP(ab_t[:, 0:1].tensor, HSH1 + 4096,
                                      [[A2.pstride, 128], [128, 32], [1, 128]]),
                              waits=[("q_gpsimd", sh_d21)])
                sh_tok0 = trB
                sh_tok = trB1
                A2.pop()
                WSL = A2.alloc(NFC * 512)
                TBS = {k: A2.alloc(NCH * 64) for k in ("qC", "qS", "kC", "kS")}
                SQS = A2.alloc(512); QH = A2.alloc(512); T1 = A2.alloc(512); T2 = A2.alloc(512)
                QST = A2.alloc(2 * 512)

                tbl_hist = {}
                for (ti, (dst, nm, bc)) in enumerate((("qC", "qC_bf", f"f_qs{l}_bc"),
                                                      ("qS", "qS_bf", f"f_qs{l}_sw"),
                                                      ("kC", "kC_bf", f"f_ks{l}_bc"),
                                                      ("kS", "kS_bf", f"f_ks{l}_sw"))):
                    in_ap = bass.AP(I[nm].tensor, 0, [[64, 128], [128 * 64, NCH], [1, 64]])
                    out_ap = bass.AP(ab_t[:, 0:1].tensor, TBS[dst],
                                     [[A2.pstride, 128], [64, NCH], [1, 64]])
                    lt = p.dma("scalar", out_ap, in_ap,
                               waits=[("ve", p.now("ve")), ("q_sync", trA1)])
                    qsl = QSB + 64 * (ti % 2)
                    lq = p.dma("gpsimd", ab_t[:, qsl:qsl + 64], I[bc],
                               waits=[("ve", tbl_hist.get(ti - 2,
                                             fadd_last if l == 1 else 0))])
                    p.v(lambda e, dst=dst, qsl=qsl: e.tensor_tensor(
                        out=bass.AP(ab_t[:, 0:1].tensor, TBS[dst],
                                    [[A2.pstride, 128], [64, NCH], [1, 64]]),
                        in0=bass.AP(ab_t[:, 0:1].tensor, TBS[dst],
                                    [[A2.pstride, 128], [64, NCH], [1, 64]]),
                        in1=bass.AP(ab_t[:, 0:1].tensor, qsl,
                                    [[A2.pstride, 128], [0, NCH], [1, 64]]),
                        op=ALU.mult), waits=[("q_scalar", lt), ("q_gpsimd", lq)])
                    tbl_hist[ti] = p.now("ve")

                prev_kind_last = trB

                def hfm_blk(c, f):
                    o = HFM + (HHALF if f >= 4 else 0) + 512 * c + 128 * (f % 4)
                    return ab_t[:, o:o + 128]

                for (oi, kind) in ((0, "q"), (1, "k"), (2, "v")):
                    wld = p.dma("gpsimd", ab_t[:, WSL:WSL + NFC * 512].rearrange(
                                    "p (f d) -> p f d", f=NFC),
                                I[f"f_wqkv{l}"][:, 512 * oi:512 * (oi + 1)].rearrange(
                                    "(f p) d -> p f d", p=128),
                                waits=[("te", p.now("te")), ("q_sync", trA)])
                    cp_toks = {}
                    tr_toks = {}
                    kind_sync0 = prev_kind_last
                    for c in range(NCH):
                        bnk = c % 2
                        mmt = None
                        for f in range(NFC):
                            mmt = p.t(lambda e, c=c, f=f, bnk=bnk: e.matmul(
                                out=bank(bnk, 0, 128, 0, 512),
                                lhsT=hfm_blk(c, f),
                                rhs=ab_t[:, WSL + 512 * f:WSL + 512 * (f + 1)],
                                start=(f == 0), stop=(f == NFC - 1)),
                                waits=([("q_sync", sh_tok0), ("q_gpsimd", wld),
                                        ("sc", p.now("sc")), ("ve", p.now("ve"))]
                                       if (c == 0 and f == 0) else
                                       (([("q_sync", sh_tok)] if c == 8 else []) +
                                        [("sc", cp_toks.get(c - 2, 0)),
                                         ("ve", cp_toks.get(c - 2, 0) and 0)] if f == 0 else ())))
                        if kind == "v":
                            cp_toks[c] = p.s(lambda e, c=c: e.copy(
                                out=ab_t[:, VTM + 512 * c:VTM + 512 * (c + 1)],
                                in_=bank(c % 2, 0, 128, 0, 512)), waits=[("te", mmt)])
                            continue
                        sl2 = QST + 512 * (c % 2)
                        cp_toks[c] = p.s(lambda e, c=c, sl2=sl2: e.copy(
                            out=ab_t[:, sl2:sl2 + 512],
                            in_=bank(c % 2, 0, 128, 0, 512)),
                            waits=[("te", mmt), ("q_sync", tr_toks.get(c - 2, kind_sync0))])
                        sq = p.s(lambda e, sl2=sl2: e.activation(
                            out=ab_t[:, SQS:SQS + 512], in_=ab_t[:, sl2:sl2 + 512],
                            func=AF.Square), waits=[("sc", cp_toks[c]), ("ve", p.now("ve"))])
                        r1 = p.v(lambda e: e.reduce_sum(
                            out=af_t[:, RNC:RNC + 8],
                            in_=ab_t[:, SQS:SQS + 512].rearrange("p (h d) -> p h d", h=8),
                            axis=AX.X), waits=[("sc", sq)])
                        r2 = p.s(lambda e: e.activation(out=af_t[:, RNC + 8:RNC + 16],
                                                        in_=af_t[:, RNC:RNC + 8], func=AF.Sqrt),
                                 waits=[("ve", r1)])
                        r3 = p.v(lambda e: e.tensor_scalar_max(out=af_t[:, RNC + 8:RNC + 16],
                                                               in0=af_t[:, RNC + 8:RNC + 16],
                                                               scalar1=1e-12), waits=[("sc", r2)])
                        r4 = p.v(lambda e: e.reciprocal(out=af_t[:, RNC:RNC + 8],
                                                        in_=af_t[:, RNC + 8:RNC + 16]),
                                 waits=[("ve", r3)])
                        Ct, St = ("qC", "qS") if kind == "q" else ("kC", "kS")
                        h1 = p.v(lambda e, sl2=sl2: e.tensor_tensor(
                            out=ab_t[:, QH:QH + 512].rearrange("p (h d) -> p h d", h=8),
                            in0=ab_t[:, sl2:sl2 + 512].rearrange("p (h d) -> p h d", h=8),
                            in1=bass.AP(af_t[:, 0:1].tensor, RNC,
                                        [[A4.pstride, 128], [1, 8], [0, 64]]),
                            op=ALU.mult), waits=[("ve", r4)])
                        t1_ = p.v(lambda e, c=c, Ct=Ct: e.tensor_tensor(
                            out=ab_t[:, T1:T1 + 512].rearrange("p (h d) -> p h d", h=8),
                            in0=ab_t[:, QH:QH + 512].rearrange("p (h d) -> p h d", h=8),
                            in1=bass.AP(ab_t[:, 0:1].tensor, TBS[Ct] + 64 * c,
                                        [[A2.pstride, 128], [0, 8], [1, 64]]),
                            op=ALU.mult), waits=[("ve", h1)])
                        swap_ap = bass.AP(ab_t[:, 0:1].tensor, QH + 32,
                                          [[A2.pstride, 128], [64, 8], [-32, 2], [1, 32]])
                        t2_ = p.v(lambda e, c=c, St=St, swap_ap=swap_ap: e.tensor_tensor(
                            out=ab_t[:, T2:T2 + 512].rearrange("p (h a b) -> p h a b", h=8, a=2),
                            in0=swap_ap,
                            in1=bass.AP(ab_t[:, 0:1].tensor, TBS[St] + 64 * c,
                                        [[A2.pstride, 128], [0, 8], [32, 2], [1, 32]]),
                            op=ALU.mult), waits=[("ve", t1_)])
                        t3_ = p.v(lambda e, sl2=sl2: e.tensor_add(
                            out=ab_t[:, sl2:sl2 + 512],
                            in0=ab_t[:, T1:T1 + 512], in1=ab_t[:, T2:T2 + 512]),
                            waits=[("ve", t2_)])
                        DSTF = QFM if kind == "q" else KFM
                        tr_toks[c] = p.dmat(
                            bass.AP(ab_t[:, 0:1].tensor, DSTF + 512 * c,
                                    [[A2.pstride, 128], [128, 4], [1, 128]]),
                            bass.AP(ab_t[:, 0:1].tensor, sl2,
                                    [[A2.pstride, 128], [128, 4], [1, 128]]),
                            waits=[("ve", t3_)])
                        if c == NCH - 1:
                            prev_kind_last = tr_toks[c]
                A2.pop()
                tr_all = p.now("q_sync")

                A2.push()
                OFM = A2.alloc(NCH * 512)
                PBF = A2.alloc(2 * 4 * 640)
                ABF = A2.alloc(8 * 512)
                DQB = A2.alloc(2 * 4 * 128)
                iters = [(h, w) for h in range(HEADS_F) for w in range(4)]
                NI = len(iters)
                p_toks = {}; dq_toks = {}; s_toks = {}; tr_tok = {}
                etok = {}; tok_trlast = {}; av_last = {}
                sc_attn0 = p.now("sc"); ve_attn0 = p.now("ve")

                def pb(j):
                    return PBF + 2560 * (j % 2)

                def db(j):
                    return DQB + 512 * (j % 2)

                def nmo(j):
                    return NM_ + 4 * (j % 2)

                def tr_qcs(w, kc):
                    if w > 0:
                        return [q_ for q_ in range(4) if 0 <= kc - q_ <= 4]
                    return [q_ for q_ in range(4) if q_ >= kc]

                def emit_S(j, qc):
                    (h, w) = iters[j]
                    hp, par = h // 2, h % 2
                    cq = 4 * w + qc
                    if w == 0:
                        width = 128 * (qc + 1)
                        kc0g = 0
                    else:
                        width = 640
                        kc0g = 4 * (w - 1) + qc
                    nb = width // 128
                    qb = ab_t[64 * par:64 * par + 64,
                              QFM + 512 * cq + 128 * hp:QFM + 512 * cq + 128 * (hp + 1)]
                    rhs1 = bass.AP(ab_t[:, 0:1].tensor,
                                   KFM + 512 * kc0g + 128 * hp + A2.pstride * 64 * par,
                                   [[A2.pstride, 64], [512, min(nb, 4)], [1, 128]])
                    if j == 0 and qc == 0:
                        w0 = [("q_sync", tr_all), ("sc", sc_attn0), ("ve", ve_attn0)]
                    else:
                        w0 = [("sc", p_toks[(j, qc - 1)] if qc > 0 else p_toks[(j - 1, 3)])]
                    m1 = p.t(lambda e, qb=qb, rhs1=rhs1, n=min(nb, 4): e.matmul(
                        out=bank(2, 0, 128, 0, 128 * n), lhsT=qb, rhs=rhs1,
                        start=True, stop=True), waits=w0)
                    if nb > 4:
                        rhs2 = bass.AP(ab_t[:, 0:1].tensor,
                                       KFM + 512 * (kc0g + 4) + 128 * hp + A2.pstride * 64 * par,
                                       [[A2.pstride, 64], [512, 1], [1, 128]])
                        m1 = p.t(lambda e, qb=qb, rhs2=rhs2: e.matmul(
                            out=bank(3, 0, 128, 0, 128), lhsT=qb, rhs=rhs2,
                            start=True, stop=True))
                    s_toks[(j, qc)] = m1

                def emit_soft(j, qc):
                    (h, w) = iters[j]
                    width = 640 if w > 0 else 128 * (qc + 1)
                    if w == 0:
                        mk = p.v(lambda e, qc=qc: e.tensor_add(
                            out=bank(2, 0, 128, 128 * qc, 128),
                            in0=bank(2, 0, 128, 128 * qc, 128), in1=triR[:]),
                            waits=[("te", s_toks[(j, qc)])])
                    else:
                        p.v(lambda e: e.tensor_add(
                            out=bank(2, 0, 128, 0, 128),
                            in0=bank(2, 0, 128, 0, 128), in1=triL[:]),
                            waits=[("te", s_toks[(j, qc)])])
                        mk = p.v(lambda e: e.tensor_add(
                            out=bank(3, 0, 128, 0, 128),
                            in0=bank(3, 0, 128, 0, 128), in1=triR[:]))
                    no = nmo(j)
                    mx = p.v(lambda e, width=width, no=no: e.reduce_max(
                        out=af_t[:, no:no + 1], in_=PS[:, 1024:1024 + width],
                        axis=AX.X, negate=True), waits=[("ve", mk)])
                    ex = p.s(lambda e, j=j, qc=qc, width=width, no=no: e.activation(
                        out=ab_t[:, pb(j) + 640 * qc:pb(j) + 640 * qc + width],
                        in_=PS[:, 1024:1024 + width], func=AF.Exp,
                        bias=af_t[:, no:no + 1],
                        accum_out=af_t[:, no + 1:no + 2]),
                        waits=[("ve", mx), ("te", tok_trlast.get(j - 2, 0))])
                    rc = p.v(lambda e, no=no: e.reciprocal(
                        out=af_t[:, no + 2:no + 3], in_=af_t[:, no + 1:no + 2]),
                        waits=[("sc", ex)])
                    dqt = p.v(lambda e, j=j, qc=qc, no=no: e.tensor_scalar_mul(
                        out=ab_t[:, db(j) + 128 * qc:db(j) + 128 * (qc + 1)],
                        in0=id128[:], scalar1=af_t[:, no + 2:no + 3]),
                        waits=[("ve", rc), ("te", tok_trlast.get(j - 2, 0))])
                    p_toks[(j, qc)] = ex
                    dq_toks[(j, qc)] = dqt

                def emit_tr(i, kc):
                    (h, w) = iters[i]
                    qcs = tr_qcs(w, kc)
                    qmax = max(qcs)
                    wl = [("sc", p_toks[(i, qmax)]), ("ve", dq_toks[(i, qmax)])]
                    if kc >= 2:
                        wl.append(etok[(i, kc - 2)])
                    elif i > 0:
                        pk = etok.get((i - 1, 6 + kc)) or etok.get((i - 1, 2 + kc))
                        if pk is not None:
                            wl.append(pk)
                    tt = None
                    for (qi, qc) in enumerate(qcs):
                        sb = (kc - qc) if w > 0 else kc
                        tt = p.t(lambda e, i=i, qc=qc, sb=sb, kc=kc: e.matmul(
                            out=bank(kc % 2, 0, 128, 128 * qc, 128),
                            lhsT=ab_t[:, pb(i) + 640 * qc + 128 * sb:pb(i) + 640 * qc + 128 * (sb + 1)],
                            rhs=ab_t[:, db(i) + 128 * qc:db(i) + 128 * (qc + 1)],
                            start=True, stop=True), waits=wl if qi == 0 else ())
                    tr_tok[(i, kc)] = tt

                def emit_evac(i, kc):
                    (h, w) = iters[i]
                    qcs = tr_qcs(w, kc)
                    span0, span1 = 128 * min(qcs), 128 * (max(qcs) + 1)
                    if kc % 2 == 0:
                        ev = p.s(lambda e, kc=kc, span0=span0, span1=span1: e.copy(
                            out=ab_t[:, ABF + 512 * kc + span0:ABF + 512 * kc + span1],
                            in_=bank(kc % 2, 0, 128, span0, span1 - span0)),
                            waits=[("te", tr_tok[(i, kc)])])
                        etok[(i, kc)] = ("sc", ev)
                    else:
                        ev = p.v(lambda e, kc=kc, span0=span0, span1=span1: e.tensor_copy(
                            out=ab_t[:, ABF + 512 * kc + span0:ABF + 512 * kc + span1],
                            in_=bank(kc % 2, 0, 128, span0, span1 - span0)),
                            waits=[("te", tr_tok[(i, kc)])])
                        etok[(i, kc)] = ("ve", ev)

                def emit_AV(i, qc):
                    (h, w) = iters[i]
                    if w > 0:
                        kcs = list(range(qc, qc + 5))
                    else:
                        kcs = list(range(0, qc + 1))
                    es = max([etok[(i, kc)][1] for kc in kcs if kc % 2 == 0], default=0)
                    ev = max([etok[(i, kc)][1] for kc in kcs if kc % 2 == 1], default=0)
                    last = None
                    for (ki, kc) in enumerate(kcs):
                        vglob = (4 * (w - 1) + kc) if w > 0 else kc
                        last = p.t(lambda e, qc=qc, kc=kc, vglob=vglob, h=h,
                                   st_=(ki == 0), sp_=(ki == len(kcs) - 1): e.matmul(
                            out=bank(4 + qc, 0, 64, 0, 128),
                            lhsT=ab_t[:, VTM + 512 * vglob + 64 * h:VTM + 512 * vglob + 64 * (h + 1)],
                            rhs=ab_t[:, ABF + 512 * kc + 128 * qc:ABF + 512 * kc + 128 * (qc + 1)],
                            start=st_, stop=sp_),
                            waits=[("sc", es), ("ve", ev)] if ki == 0 else ())
                    return last

                for qc in range(4):
                    emit_S(0, qc)
                    emit_soft(0, qc)
                for i in range(NI):
                    (h, w) = iters[i]
                    hp, par = h // 2, h % 2
                    nkc = 8 if w > 0 else 4
                    have_next = i + 1 < NI
                    for blk in range(4):
                        k0 = 2 * blk
                        if k0 < nkc:
                            emit_tr(i, k0)
                            emit_tr(i, k0 + 1)
                            emit_evac(i, k0)
                            emit_evac(i, k0 + 1)
                        if have_next:
                            if blk >= 1:
                                emit_soft(i + 1, blk - 1)
                            if blk < 4:
                                emit_S(i + 1, blk)
                    if have_next:
                        emit_soft(i + 1, 3)
                    tok_trlast[i] = tr_tok[(i, nkc - 1)]
                    avl = None
                    for qc in range(4):
                        avl = emit_AV(i, qc)
                    av_last[i] = avl
                    oev = p.s(lambda e, h=h, w=w, hp=hp, par=par: e.copy(
                        out=bass.AP(ab_t[:, 0:1].tensor,
                                    OFM + 512 * 4 * w + 128 * hp + A2.pstride * 64 * par,
                                    [[A2.pstride, 64], [512, 4], [1, 128]]),
                        in_=bass.AP(PS[:, 0:1].tensor, 512 * 4,
                                    [[4096, 64], [512, 4], [1, 128]])),
                        waits=[("te", av_last[i])])

                WSL2 = A2.alloc(4 * D)
                wld = p.dma("gpsimd", ab_t[:, WSL2:WSL2 + 4 * D].rearrange("p (f d) -> p f d", f=4),
                            I[f"f_wo{l}"].rearrange("(f p) d -> p f d", p=128),
                            waits=[("te", p.now("te"))])
                radd = {}
                for c in range(NCH):
                    mmt = None
                    b0 = 2 * (c % 2)
                    for oseg in range(2):
                        for f in range(4):
                            mmt = p.t(lambda e, c=c, oseg=oseg, f=f, b0=b0: e.matmul(
                                out=bank(b0 + oseg, 0, 128, 0, 512),
                                lhsT=ab_t[:, OFM + 512 * c + 128 * f:OFM + 512 * c + 128 * (f + 1)],
                                rhs=ab_t[:, WSL2 + D * f + 512 * oseg:WSL2 + D * f + 512 * (oseg + 1)],
                                start=(f == 0), stop=(f == 3)),
                                waits=([("q_gpsimd", wld), ("ve", p.now("ve")),
                                        ("sc", p.now("sc"))]
                                       if (c == 0 and oseg == 0 and f == 0) else
                                       ([("ve", radd.get(c - 2, 0))]
                                        if (oseg == 0 and f == 0) else ())))
                    radd[c] = p.v(lambda e, c=c, b0=b0: e.tensor_add(
                        out=xf(c), in0=xf(c), in1=PS[:, 512 * b0:512 * b0 + 1024]),
                        waits=[("te", mmt)])
                A2.pop()
                A2.pop()

                A2.push()
                HFM2 = A2.alloc(NCH * D)
                LNGB = A2.alloc(D)
                LNBB = A2.alloc(D)
                A2.push()
                XBA2 = A2.alloc(NCH * 512)
                HSH2 = A2.alloc(NCH * 512)
                XNS = A2.alloc(4 * 512)
                l1_ = p.dma("gpsimd", ab_t[:, LNGB:LNGB + D],
                            bass.AP(I[f"f_lng_row{l}"].tensor, 0, [[0, 128], [1, D]]),
                            waits=[("ve", radd[3])])
                l2_ = p.dma("gpsimd", ab_t[:, LNBB:LNBB + D],
                            bass.AP(I[f"f_lnb_row{l}"].tensor, 0, [[0, 128], [1, D]]))
                sh_ct2, sh_d22, sh_mz2 = build_shift_full(
                    XBA2, XF, HSH2, 0, 8, extra0=[("ve", radd[11])])
                sh_ct2b, sh_d22b, _ = build_shift_full(
                    XBA2, XF, HSH2, 8, NCH, extra0=[("ve", radd[15])])
                lnt2 = None
                xn_done = None
                xn_th = {}
                for chalf in (0, 1):
                    for c in range(8 * chalf, 8 * chalf + 8):
                        p.v(lambda e, c=c: e.bn_stats(out=bn6[:, 0:6],
                                                      in_=ab_t[:, XBA2 + 512 * c:XBA2 + 512 * (c + 1)]),
                            waits=([("ve", radd[11]), ("q_gpsimd", sh_ct2)] if c == 0
                                   else ([("ve", lnt2), ("q_gpsimd", sh_ct2b)] if c == 8
                                         else [("ve", lnt2)])))
                        t2b = p.v(lambda e, c=c: e.bn_stats(
                            out=bn6[:, 6:12], in_=ab_t[:, HSH2 + 512 * c:HSH2 + 512 * (c + 1)]),
                            waits=([("q_gpsimd", sh_d22), ("ve", sh_mz2)] if c == 0
                                   else ([("q_gpsimd", sh_d22b)] if c == 8 else ())))
                        lnt2 = p.v(lambda e, c=c: e.bn_aggr(out=af_t[:, MV + 2 * c:MV + 2 * c + 2],
                                                            in_=bn6), waits=[("ve", t2b)])
                    rm2 = stats_rowmath(lnt2, False, nmu_off=NMU2, rsg_off=RSG2,
                                        c0=8 * chalf, c1=8 * chalf + 8)
                    for qr4 in (2 * chalf, 2 * chalf + 1):
                        for (half, SRCB, dsto) in ((0, XBA2, 0), (1, HSH2, HHALF)):
                            last = None
                            for j in range(4):
                                c = 4 * qr4 + j
                                t_ts = p.v(lambda e, c=c, j=j, SRCB=SRCB: e.tensor_scalar(
                                    out=ab_t[:, XNS + 512 * j:XNS + 512 * (j + 1)],
                                    in0=ab_t[:, SRCB + 512 * c:SRCB + 512 * (c + 1)],
                                    scalar1=af_t[:, NMU2 + c:NMU2 + c + 1],
                                    scalar2=af_t[:, RSG2 + c:RSG2 + c + 1],
                                    op0=ALU.add, op1=ALU.mult),
                                    waits=([("q_sync", xn_done), ("q_gpsimd", l2_),
                                            ("ve", rm2)] if j == 0 else []))
                                t_tm = p.v(lambda e, j=j, half=half: e.tensor_mul(
                                    out=ab_t[:, XNS + 512 * j:XNS + 512 * (j + 1)],
                                    in0=ab_t[:, XNS + 512 * j:XNS + 512 * (j + 1)],
                                    in1=ab_t[:, LNGB + 512 * half:LNGB + 512 * (half + 1)]),
                                    waits=[("ve", t_ts)])
                                last = p.v(lambda e, j=j, half=half: e.tensor_add(
                                    out=ab_t[:, XNS + 512 * j:XNS + 512 * (j + 1)],
                                    in0=ab_t[:, XNS + 512 * j:XNS + 512 * (j + 1)],
                                    in1=ab_t[:, LNBB + 512 * half:LNBB + 512 * (half + 1)]),
                                    waits=[("ve", t_tm)])
                            xn_done = p.dmat(
                                bass.AP(ab_t[:, 0:1].tensor, HFM2 + dsto + 2048 * qr4,
                                        [[A2.pstride, 128], [128, 16], [1, 128]]),
                                bass.AP(ab_t[:, 0:1].tensor, XNS,
                                        [[A2.pstride, 128], [128, 16], [1, 128]]),
                                waits=[("ve", last)])
                            if (qr4, half) in ((1, 1), (3, 1)):
                                xn_th[qr4 // 2] = xn_done
                A2.pop()

                A2.push()
                HB = A2.alloc(22 * D)
                HBL = A2.alloc(D)
                GSC = A2.alloc(4 * 512)
                WSL3 = A2.alloc(2 * NFC * 256)
                b1a = af_t[:, B1A:B1A + 22]
                b1g = af_t[:, B1G:B1G + 22]
                lb1 = p.dma("scalar", b1a, I[f"f_b1a{l}"], waits=[("sc", p.now("sc")),
                                                                  ("ve", p.now("ve"))])
                lb2 = p.dma("scalar", b1g, I[f"f_b1g{l}"])
                mzo = p.v(lambda e: e.memset(ab_t[32:64, HBL:HBL + D], 1.0))
                for th in range(2):
                    wga = {}
                    hv_toks = {}
                    glu_hist = {}
                    for j in range(22):
                        rows = 128 if j < 21 else 42
                        mz_w = None
                        if j == 21:
                            mz_w = p.v(lambda e, j=j: e.memset(
                                ab_t[:, WSL3 + 4096 * (j % 2):WSL3 + 4096 * (j % 2) + 4096], 0.0),
                                waits=[("te", wga.get(j - 2, p.now("te")))])
                        wa = p.dma("gpsimd",
                                   bass.AP(ab_t[:, 0:1].tensor, WSL3 + 4096 * (j % 2),
                                           [[A2.pstride, 128], [256, NFC], [1, rows]]),
                                   bass.AP(I[f"f_w1{l}"].tensor, 128 * j,
                                           [[2 * HID, 128], [128 * 2 * HID, NFC], [1, rows]]),
                                   waits=[("te", wga.get(j - 2, p.now("te"))), ("ve", mz_w)])
                        wg = p.dma("gpsimd",
                                   bass.AP(ab_t[:, 0:1].tensor, WSL3 + 4096 * (j % 2) + 128,
                                           [[A2.pstride, 128], [256, NFC], [1, rows]]),
                                   bass.AP(I[f"f_w1{l}"].tensor, HID + 128 * j,
                                           [[2 * HID, 128], [128 * 2 * HID, NFC], [1, rows]]))
                        mm_l = None
                        for seg in range(2):
                            for (which, b0) in (("a", 0), ("g", 2)):
                                for f in range(NFC):
                                    rhs_ap = bass.AP(
                                        ab_t[:, 0:1].tensor,
                                        HFM2 + (HHALF if f >= 4 else 0)
                                        + 512 * (8 * th + 4 * seg) + 128 * (f % 4),
                                        [[A2.pstride, 128], [512, 4], [1, 128]])
                                    wofs = WSL3 + 4096 * (j % 2) + 256 * f + (0 if which == "a" else 128)
                                    mm_l = p.t(lambda e, rhs_ap=rhs_ap, wofs=wofs,
                                               b0=b0, seg=seg, f=f: e.matmul(
                                        out=bank(b0 + seg, 0, 128, 0, 512),
                                        lhsT=ab_t[:, wofs:wofs + 128],
                                        rhs=rhs_ap, start=(f == 0), stop=(f == NFC - 1)),
                                        waits=([("q_gpsimd", wg), ("q_sync", xn_th[th]),
                                                ("sc", glu_hist.get((j - 1, seg), 0)),
                                                ("ve", hv_toks.get((j - 1, seg), 0))]
                                               if (f == 0 and which == "a") else ()))
                        wga[j] = mm_l
                        for seg in range(2):
                            gslot = GSC + 512 * (2 * (j % 2) + seg)
                            gl = p.s(lambda e, j=j, seg=seg, rows=rows, gslot=gslot: e.activation(
                                out=ab_t[0:rows, gslot:gslot + 512], in_=bank(2 + seg, 0, rows, 0, 512),
                                func=AF.Gelu, bias=af_t[0:rows, B1G + j:B1G + j + 1]),
                                waits=[("te", wga[j]), ("q_scalar", lb2),
                                       ("ve", hv_toks.get((j - 2, seg), 0))])
                            dst = (HB + D * j) if j < 21 else HBL
                            hv_toks[(j, seg)] = p.v(
                                lambda e, seg=seg, rows=rows, dst=dst, j=j, gslot=gslot: e.scalar_tensor_tensor(
                                    out=ab_t[0:rows, dst + 512 * seg:dst + 512 * (seg + 1)],
                                    in0=bank(0 + seg, 0, rows, 0, 512),
                                    scalar=af_t[0:rows, B1A + j:B1A + j + 1],
                                    in1=ab_t[0:rows, gslot:gslot + 512],
                                    op0=ALU.add, op1=ALU.mult),
                                waits=[("sc", gl), ("q_scalar", lb1), ("ve", mzo)] +
                                      ([("q_sync", xn_done)] if j in (16, 17) else []))
                            glu_hist[(j, seg)] = gl
                    hb_done = p.now("ve")
                    for grp in range(2):
                        for oseg in range(2):
                            mmw = {}
                            for j in range(22):
                                rows = 128 if j < 21 else 43
                                w2l = p.dma("gpsimd",
                                            ab_t[0:rows, WSL3 + 4096 * (j % 2):WSL3 + 4096 * (j % 2) + 512],
                                            I[f"f_w2a{l}"][128 * j:128 * j + rows,
                                                           512 * oseg:512 * (oseg + 1)],
                                            waits=[("te", mmw.get(j - 2, p.now("te"))),
                                                   ("ve", hv_toks.get((j, 1), hb_done))])
                                for tt in range(4):
                                    if j < 21:
                                        lhsT = ab_t[:, HB + D * j + 512 * grp + 128 * tt:
                                                    HB + D * j + 512 * grp + 128 * (tt + 1)]
                                    else:
                                        lhsT = ab_t[0:43, HBL + 512 * grp + 128 * tt:
                                                    HBL + 512 * grp + 128 * (tt + 1)]
                                    mmw[j] = p.t(lambda e, j=j, tt=tt, rows=rows, lhsT=lhsT: e.matmul(
                                        out=bank(4 + tt, 0, 128, 0, 512),
                                        lhsT=lhsT,
                                        rhs=ab_t[0:rows, WSL3 + 4096 * (j % 2):WSL3 + 4096 * (j % 2) + 512],
                                        start=(j == 0), stop=(j == 21)),
                                        waits=(([("q_gpsimd", w2l)] +
                                                ([("ve", p.now("ve"))] if j == 0 else []))
                                               if tt == 0 else ()))
                            for tt in range(4):
                                cg = 8 * th + 4 * grp + tt
                                fadd_last = p.v(lambda e, cg=cg, tt=tt, oseg=oseg: e.tensor_add(
                                    out=af_t[:, XF + D * cg + 512 * oseg:XF + D * cg + 512 * (oseg + 1)],
                                    in0=af_t[:, XF + D * cg + 512 * oseg:XF + D * cg + 512 * (oseg + 1)],
                                    in1=bank(4 + tt, 0, 128, 0, 512)),
                                    waits=[("te", mmw[21])])
                A2.pop()
                A2.pop()
            for qr in range(4):
                ld = p.dma("scalar",
                           bass.AP(af_t[:, 0:1].tensor, WTMP0,
                                   [[A4.pstride, 128], [D, 4], [1, D]]),
                           x_src[512 * qr:512 * (qr + 1), :].rearrange("(j p) d -> p j d", p=128))
                ads = []
                for j in range(4):
                    c = 4 * qr + j
                    ads.append(p.v(lambda e, c=c, j=j: e.tensor_add(
                        out=xf(c), in0=af_t[:, WTMP0 + D * j:WTMP0 + D * (j + 1)], in1=xf(c)),
                        waits=([("q_scalar", ld)] +
                               ([("ve", fadd_last)] if qr == 0 else []))
                        if j == 0 else ()))
                p.dma("scalar",
                      x_dst[512 * qr:512 * (qr + 1), :].rearrange("(j p) d -> p j d", p=128),
                      bass.AP(af_t[:, 0:1].tensor, XF + 4 * D * qr,
                              [[A4.pstride, 128], [D, 4], [1, D]]),
                      waits=[("ve", ads[-1])])
            p.barrier()
            A4.pop(); A2.pop()

        def emit_attn_b(x_src, x_dst):
            A4.push(); A2.push()
            RNC = A4.alloc(40)
            RCPT = A4.alloc(256)
            YSC = A4.alloc(D)
            XQ = A4.alloc(16 * D)
            lnt = None
            for qr in range(4):
                sl = XQ + 4 * D * qr
                ld = p.dma("scalar",
                           bass.AP(af_t[:, 0:1].tensor, sl, [[A4.pstride, 128], [D, 4], [1, D]]),
                           x_src[512 * qr:512 * (qr + 1), :].rearrange("(j p) d -> p j d", p=128))
                for j in range(4):
                    c = 4 * qr + j
                    lnt = tm_stats(af_t[:, sl + D * j:sl + D * (j + 1)],
                                   af_t[:, MV + 2 * c:MV + 2 * c + 2],
                                   [("q_scalar", ld)] if j == 0 else [])
            stats_rowmath(lnt, False)
            p.barrier()
            A2.push()
            XNTM = A2.alloc(2 * D)
            XNFM = A2.alloc(NCH * D)
            WB = A2.alloc(NFC * D)
            BRW = A2.alloc(D)
            EVS = A2.alloc(2 * D)
            tr_xn = {}
            for c in range(NCH):
                sl = XNTM + D * (c % 2)
                tv = p.v(lambda e, c=c, sl=sl: e.tensor_scalar(
                    out=ab_t[:, sl:sl + D], in0=af_t[:, XQ + D * c:XQ + D * (c + 1)],
                    scalar1=af_t[:, NMU + c:NMU + c + 1], scalar2=af_t[:, RSG + c:RSG + c + 1],
                    op0=ALU.add, op1=ALU.mult),
                    waits=[("q_sync", tr_xn.get(c - 2, 0))])
                tr_xn[c] = p.dmat(bass.AP(ab_t[:, 0:1].tensor, XNFM + D * c,
                                          [[A2.pstride, 128], [128, 8], [1, 128]]),
                                  bass.AP(ab_t[:, 0:1].tensor, sl,
                                          [[A2.pstride, 128], [128, 8], [1, 128]]),
                                  waits=[("ve", tv)])
            for (wi, (wn, bn_, relu)) in enumerate((("b_WqT", "b_bq_row", True),
                                                    ("b_WkT", "b_bk_row", True),
                                                    ("b_WvT", "b_bv_row", False))):
                wld = p.dma("gpsimd", ab_t[:, WB:WB + NFC * D].rearrange("p (f d) -> p f d", f=NFC),
                            I[wn].rearrange("(f p) d -> p f d", p=128),
                            waits=[("te", p.now("te"))])
                bld = p.dma("gpsimd", ab_t[0:1, BRW:BRW + D], I[bn_])
                evt = {}
                svt = {}
                for c in range(NCH):
                    bnk = 2 * (c % 2)
                    mmt = None
                    for oseg in range(2):
                        for f in range(NFC):
                            mmt = p.t(lambda e, c=c, f=f, oseg=oseg, bnk=bnk: e.matmul(
                                out=bank(bnk + oseg, 0, 128, 0, 512),
                                lhsT=ab_t[:, XNFM + D * c + 128 * f:XNFM + D * c + 128 * (f + 1)],
                                rhs=ab_t[:, WB + D * f + 512 * oseg:WB + D * f + 512 * (oseg + 1)],
                                start=(f == 0), stop=False),
                                waits=([("q_gpsimd", bld), ("q_sync", tr_xn[NCH - 1]),
                                        ("ve", p.now("ve")), ("sc", p.now("sc"))]
                                       if (c == 0 and oseg == 0 and f == 0) else
                                       ([("ve", evt.get(c - 2, 0))]
                                        if (oseg == 0 and f == 0) else ())))
                            if f == NFC - 1:
                                mmt = p.t(lambda e, c=c, oseg=oseg, bnk=bnk: e.matmul(
                                    out=bank(bnk + oseg, 0, 128, 0, 512),
                                    lhsT=ones_row[0:1, 0:128],
                                    rhs=ab_t[0:1, BRW + 512 * oseg:BRW + 512 * (oseg + 1)],
                                    start=False, stop=True))
                    sl = EVS + D * (c % 2)
                    if relu:
                        evt[c] = p.v(lambda e, sl=sl, bnk=bnk: e.tensor_scalar_max(
                            out=ab_t[:, sl:sl + D], in0=PS[:, 512 * bnk:512 * bnk + 1024],
                            scalar1=0.0), waits=[("te", mmt), ("q_gpsimd", svt.get(c - 2, 0))])
                    else:
                        evt[c] = p.v(lambda e, sl=sl, bnk=bnk: e.tensor_copy(
                            out=ab_t[:, sl:sl + D], in_=PS[:, 512 * bnk:512 * bnk + 1024]),
                            waits=[("te", mmt), ("q_gpsimd", svt.get(c - 2, 0))])
                    svt[c] = p.dma("gpsimd", bscr[T * wi + 128 * c:T * wi + 128 * (c + 1), :],
                                   ab_t[:, sl:sl + D], waits=[("ve", evt[c])])
            p.barrier()
            A2.pop()
            A2.push()
            KST = A2.alloc(4 * 1024)
            VST = A2.alloc(4 * 544)
            KVB = A2.alloc(16 * 65)
            KVB2 = A2.alloc(16 * 65)
            KSR = A2.alloc(D)
            KBC = A2.alloc(D)
            mz1 = p.v(lambda e: e.memset(
                bass.AP(ab_t[:, 0:1].tensor, VST + 64, [[A2.pstride, 128], [544, 4], [65, 8], [1, 1]]), 1.0))
            mzk1 = p.v(lambda e: e.memset(ab_t[:, KVB:KVB + 16 * 65], 0.0))
            mzk2 = p.v(lambda e: e.memset(ab_t[:, KVB2:KVB2 + 16 * 65], 0.0))
            kv_ev = {}
            mms = {}
            te0b = p.now("te")
            for ps_ in range(2):
                for c in range(NCH):
                    ci = NCH * ps_ + c
                    sk = KST + 1024 * (ci % 4)
                    sv = VST + 544 * (ci % 4)
                    lk = p.dma("scalar", ab_t[:, sk:sk + 512],
                               bscr[T + 128 * c:T + 128 * (c + 1), 512 * ps_:512 * (ps_ + 1)],
                               waits=[("te", mms.get(ci - 4, te0b))])
                    lv = p.dma("sync",
                               bass.AP(ab_t[:, 0:1].tensor, sv, [[A2.pstride, 128], [65, 8], [1, 64]]),
                               bscr[2 * T + 128 * c:2 * T + 128 * (c + 1),
                                    512 * ps_:512 * (ps_ + 1)].rearrange("p (h d) -> p h d", h=8),
                               waits=[("ve", mz1), ("te", mms.get(ci - 4, te0b))])
                    for hh in range(8):
                        mms[ci] = p.t(lambda e, c=c, hh=hh, sk=sk, sv=sv: e.matmul(
                            out=bank(hh, 0, 64, 0, 65),
                            lhsT=ab_t[:, sk + 64 * hh:sk + 64 * (hh + 1)],
                            rhs=ab_t[:, sv + 65 * hh:sv + 65 * (hh + 1)],
                            start=(c == 0), stop=(c == NCH - 1)),
                            waits=([("q_scalar", lk), ("q_sync", lv)] if hh == 0 else ()))
                for hh in range(8):
                    h = 8 * ps_ + hh
                    kv_ev[h] = p.s(lambda e, h=h, hh=hh: e.copy(
                        out=ab_t[0:64, KVB + 65 * h:KVB + 65 * (h + 1)],
                        in_=bank(hh, 0, 64, 0, 65)),
                        waits=[("te", mms[NCH * ps_ + NCH - 1]), ("ve", mzk1)])
            QS2 = A2.alloc(2 * D)
            QF2 = A2.alloc(2 * D)
            TT2 = A2.alloc(D)
            GBC = A2.alloc(D)
            BBC = A2.alloc(D)
            OSC = A2.alloc(D)
            lg1 = p.dma("gpsimd", ab_t[:, GBC:GBC + D], I["b_g_bc"])
            lg2 = p.dma("gpsimd", ab_t[:, BBC:BBC + D], I["b_b_bc"])
            lqs = {}
            trq = {}
            for c in (0, 1):
                sq = QS2 + D * (c % 2)
                qf = QF2 + D * (c % 2)
                lqs[c] = p.dma("scalar", ab_t[:, sq:sq + D], bscr[128 * c:128 * (c + 1), :],
                               waits=[("te", te0b)])
                trq[c] = p.dmat(bass.AP(ab_t[:, 0:1].tensor, qf,
                                        [[A2.pstride, 128], [128, 8], [1, 128]]),
                                bass.AP(ab_t[:, 0:1].tensor, sq,
                                        [[A2.pstride, 128], [128, 8], [1, 128]]),
                                waits=[("q_scalar", lqs[c])])
            ktr = None
            for h in range(16):
                ktr = p.t(lambda e, h=h: e.matmul(
                    out=PS[0:1, 64 * h:64 * (h + 1)],
                    lhsT=ab_t[0:64, KVB + 65 * h + 64:KVB + 65 * h + 65],
                    rhs=id128[0:64, 0:64], start=True, stop=True),
                    waits=[("sc", p.now("sc"))] if h == 0 else ())
            kse = p.v(lambda e: e.tensor_copy(out=ab_t[0:1, KSR:KSR + D], in_=PS[0:1, 0:1024]),
                      waits=[("te", ktr)])
            kcp = p.dma("sync", bass.AP(ab_t[:, 0:1].tensor, KVB2 + A2.pstride * 64,
                                        [[A2.pstride, 64], [1, 16 * 65]]),
                        ab_t[0:64, KVB:KVB + 16 * 65],
                        waits=[("sc", kv_ev[15]), ("ve", mzk2)])
            bc1 = p.t(lambda e: e.matmul(out=bank(4, 0, 128, 0, 512),
                                         lhsT=ones_row[0:1, 0:128], rhs=ab_t[0:1, KSR:KSR + 512],
                                         start=True, stop=True), waits=[("ve", kse)])
            bc2 = p.t(lambda e: e.matmul(out=bank(5, 0, 128, 0, 512),
                                         lhsT=ones_row[0:1, 0:128],
                                         rhs=ab_t[0:1, KSR + 512:KSR + 1024],
                                         start=True, stop=True))
            kbe = p.v(lambda e: e.tensor_copy(out=ab_t[:, KBC:KBC + D], in_=PS[:, 2048:3072]),
                      waits=[("te", bc2)])
            p.barrier()
            st_prev = {}
            stb = {}
            for c in range(NCH):
                sq = QS2 + D * (c % 2)
                qf = QF2 + D * (c % 2)
                if c < 2:
                    lq = lqs[c]
                else:
                    lq = p.dma("scalar", ab_t[:, sq:sq + D], bscr[128 * c:128 * (c + 1), :],
                               waits=[("ve", st_prev.get(c - 2, 0)), ("q_sync", trq.get(c - 2, 0))])
                    trq[c] = p.dmat(bass.AP(ab_t[:, 0:1].tensor, qf,
                                            [[A2.pstride, 128], [128, 8], [1, 128]]),
                                    bass.AP(ab_t[:, 0:1].tensor, sq,
                                            [[A2.pstride, 128], [128, 8], [1, 128]]),
                                    waits=[("q_scalar", lq)])
                d1 = p.v(lambda e, sq=sq: e.tensor_tensor(
                    out=ab_t[:, TT2:TT2 + D], in0=ab_t[:, sq:sq + D], in1=ab_t[:, KBC:KBC + D],
                    op=ALU.mult), waits=[("q_scalar", lq), ("ve", kbe),
                                         ("q_gpsimd", lg2), ("ve", st_prev.get(c - 1, 0))])
                d2 = p.v(lambda e: e.reduce_sum(
                    out=af_t[:, RNC:RNC + 16],
                    in_=ab_t[:, TT2:TT2 + D].rearrange("p (h d) -> p h d", h=16),
                    axis=AX.X), waits=[("ve", d1)])
                d3 = p.v(lambda e: e.tensor_scalar_max(out=af_t[:, RNC:RNC + 16],
                                                       in0=af_t[:, RNC:RNC + 16], scalar1=1e-6),
                         waits=[("ve", d2)])
                d4 = p.v(lambda e, c=c: e.reciprocal(out=af_t[:, RCPT + 16 * c:RCPT + 16 * (c + 1)],
                                                     in_=af_t[:, RNC:RNC + 16]), waits=[("ve", d3)])
                b01 = 4 + 2 * (c % 2)
                mmc = None
                for h in range(16):
                    hp, par = h // 2, h % 2
                    kvsrc = (ab_t[:, KVB + 65 * h:KVB + 65 * h + 64] if par == 0 else
                             ab_t[:, KVB2 + 65 * h:KVB2 + 65 * h + 64])
                    mmc = p.t(lambda e, h=h, hp=hp, par=par, qf=qf, b01=b01, kvsrc=kvsrc: e.matmul(
                        out=bank(b01 + (h >= 8), 0, 128, 64 * (h % 8), 64),
                        lhsT=ab_t[:, qf + 128 * hp:qf + 128 * (hp + 1)],
                        rhs=kvsrc,
                        start=(h % 8 == 0), stop=(h % 8 == 7)),
                        waits=([("q_sync", trq[c]), ("ve", st_prev.get(c - 2, 0)),
                                ("sc", p.now("sc")), ("q_sync", kcp)] if h == 0 else ()))
                o1 = p.v(lambda e, c=c, b01=b01: e.tensor_tensor(
                    out=ab_t[:, OSC:OSC + D].rearrange("p (h d) -> p h d", h=16),
                    in0=PS[:, 512 * b01:512 * b01 + 1024].rearrange("p (h d) -> p h d", h=16),
                    in1=bass.AP(af_t[:, 0:1].tensor, RCPT + 16 * c,
                                [[A4.pstride, 128], [1, 16], [0, 64]]),
                    op=ALU.mult), waits=[("te", mmc), ("ve", d4)])
                x1 = p.v(lambda e, c=c: e.tensor_scalar(
                    out=af_t[:, YSC:YSC + D], in0=af_t[:, XQ + D * c:XQ + D * (c + 1)],
                    scalar1=af_t[:, NMU + c:NMU + c + 1], scalar2=af_t[:, RSG + c:RSG + c + 1],
                    op0=ALU.add, op1=ALU.mult),
                    waits=[("ve", o1)] +
                          ([("q_scalar", stb[c - 1])] if c == 1 else []))
                y1 = p.v(lambda e: e.tensor_add(out=af_t[:, YSC:YSC + D],
                                                in0=af_t[:, YSC:YSC + D],
                                                in1=ab_t[:, OSC:OSC + D]), waits=[("ve", x1)])
                bnt = tm_stats(af_t[:, YSC:YSC + D], af_t[:, MV + 2 * c:MV + 2 * c + 2],
                               [("ve", y1)])
                a1 = p.s(lambda e, c=c: e.activation(out=af_t[:, TSa:TSa + 1],
                                                     in_=af_t[:, MV + 2 * c + 1:MV + 2 * c + 2],
                                                     func=AF.Sqrt, bias=1e-5), waits=[("ve", bnt)])
                a2 = p.v(lambda e: e.reciprocal(out=af_t[:, TSb:TSb + 1], in_=af_t[:, TSa:TSa + 1]),
                         waits=[("sc", a1)])
                a3 = p.v(lambda e, c=c: e.tensor_scalar(
                    out=af_t[:, YSC:YSC + D], in0=af_t[:, YSC:YSC + D],
                    scalar1=af_t[:, MV + 2 * c:MV + 2 * c + 1], scalar2=af_t[:, TSb:TSb + 1],
                    op0=ALU.subtract, op1=ALU.mult), waits=[("ve", a2)])
                a4 = p.v(lambda e: e.tensor_mul(out=af_t[:, YSC:YSC + D],
                                                in0=af_t[:, YSC:YSC + D], in1=ab_t[:, GBC:GBC + D]),
                         waits=[("ve", a3)])
                a5 = p.v(lambda e: e.tensor_add(out=af_t[:, YSC:YSC + D],
                                                in0=af_t[:, YSC:YSC + D], in1=ab_t[:, BBC:BBC + D]),
                         waits=[("ve", a4)])
                a6 = p.v(lambda e, c=c: e.tensor_add(out=af_t[:, YSC:YSC + D],
                                                in0=af_t[:, YSC:YSC + D],
                                                in1=af_t[:, XQ + D * c:XQ + D * (c + 1)]),
                         waits=[("ve", a5)])
                st_prev[c] = a6
                stb[c] = p.dma("scalar", x_dst[128 * c:128 * (c + 1), :], af_t[:, YSC:YSC + D],
                               waits=[("ve", a6)])
            p.barrier()
            A2.pop(); A4.pop(); A2.pop()

        emit_attn_a(I["x"], out)
        if STAGE >= 2:
            emit_ffn(out, out, "a")
        if STAGE >= 3:
            emit_attn_b(out, out)
        if STAGE >= 4:
            emit_ffn(out, out, "b")

        for q in ("q_scalar", "q_sync", "q_gpsimd"):
            p.sem(q)
        p.g(lambda e: e.nop(), waits=[("q_scalar", p.now("q_scalar")),
                                      ("q_sync", p.now("q_sync")),
                                      ("q_gpsimd", p.now("q_gpsimd"))])
        with nc.Block() as block:
            p.finalize(block)
    return nc


_NC_CACHE = {}


def _get_nc():
    if "nc" not in _NC_CACHE:
        _NC_CACHE["nc"] = build_program()
    return _NC_CACHE["nc"]


def _cc(a):
    return np.ascontiguousarray(np.asarray(a, np.float32))


def kernel(**inp):
    nc = _get_nc()
    sh = {}
    sh["a_WqT"] = _cc(np.asarray(inp["L0_attn_a_Wq"]).T)
    sh["a_WkT"] = _cc(np.asarray(inp["L0_attn_a_Wk"]).T)
    sh["a_WvT"] = _cc(np.asarray(inp["L0_attn_a_Wv"]).T)
    sh["a_WoT"] = _cc(np.asarray(inp["L0_attn_a_Wo"]).T)
    sh["a_bwT"] = _cc(np.asarray(inp["L0_attn_a_beta_w"]).T)
    sh["a_rin"] = _cc(np.asarray(inp["L0_attn_a_rin_w"]).reshape(8, 128).T)
    sh["a_rk"] = _cc(np.asarray(inp["L0_attn_a_rk_w"]).reshape(8, 128).T)
    sh["a_rv"] = _cc(np.asarray(inp["L0_attn_a_rv_w"]).reshape(8, 128).T)
    sh["a_bo_row"] = _cc(np.asarray(inp["L0_attn_a_bo"])[None, :])
    sh["a_lnw_bc"] = _cc(np.broadcast_to(np.asarray(inp["L0_attn_a_ln_w"]), (128, D)))
    sh["a_lnb_bc"] = _cc(np.broadcast_to(np.asarray(inp["L0_attn_a_ln_b"]), (128, D)))
    sh["a_bb_col"] = _cc(np.asarray(inp["L0_attn_a_beta_b"])[:, None])
    sh["a_bbl_col"] = _cc(np.asarray(inp["L0_attn_a_base_beta_logit"])[:, None])
    sw = np.r_[np.arange(32, 64), np.arange(0, 32)]
    for l in range(2):
        sh[f"f_wqkv{l}"] = _cc(np.asarray(inp["L0_ffn_wqkv"])[l])
        sh[f"f_wo{l}"] = _cc(np.asarray(inp["L0_ffn_wo"])[l])
        sh[f"f_w1{l}"] = _cc(np.asarray(inp["L0_ffn_w1"])[l])
        w2 = np.asarray(inp["L0_ffn_w2"])[l]
        b2 = np.asarray(inp["L0_ffn_b2"])[l]
        sh[f"f_w2a{l}"] = _cc(np.concatenate([w2, b2[None, :]], axis=0))
        b1 = np.asarray(inp["L0_ffn_b1"])[l]
        b1a = np.zeros((22 * 128,), np.float32); b1a[:HID] = b1[:HID]
        b1g = np.zeros((22 * 128,), np.float32); b1g[:HID] = b1[HID:]
        sh[f"f_b1a{l}"] = _cc(b1a.reshape(22, 128).T)
        sh[f"f_b1g{l}"] = _cc(b1g.reshape(22, 128).T)
        sh[f"f_lng_row{l}"] = _cc(np.asarray(inp["L0_ffn_ln_g"])[l][None, :])
        sh[f"f_lnb_row{l}"] = _cc(np.asarray(inp["L0_ffn_ln_b"])[l][None, :])
        qs = np.asarray(inp["L0_ffn_q_scale"])[l]; ks = np.asarray(inp["L0_ffn_k_scale"])[l]
        sh[f"f_qs{l}_bc"] = _cc(np.broadcast_to(qs, (128, 64)))
        sh[f"f_ks{l}_bc"] = _cc(np.broadcast_to(ks, (128, 64)))
        sh[f"f_qs{l}_sw"] = _cc(np.broadcast_to(qs[sw], (128, 64)))
        sh[f"f_ks{l}_sw"] = _cc(np.broadcast_to(ks[sw], (128, 64)))
    sh["b_WqT"] = _cc(np.asarray(inp["L0_attn_b_Wq"]).T)
    sh["b_WkT"] = _cc(np.asarray(inp["L0_attn_b_Wk"]).T)
    sh["b_WvT"] = _cc(np.asarray(inp["L0_attn_b_Wv"]).T)
    sh["b_bq_row"] = _cc(np.asarray(inp["L0_attn_b_bq"])[None, :])
    sh["b_bk_row"] = _cc(np.asarray(inp["L0_attn_b_bk"])[None, :])
    sh["b_bv_row"] = _cc(np.asarray(inp["L0_attn_b_bv"])[None, :])
    sh["b_g_bc"] = _cc(np.broadcast_to(np.asarray(inp["L0_attn_b_gamma"]), (128, D)))
    sh["b_b_bc"] = _cc(np.broadcast_to(np.asarray(inp["L0_attn_b_beta"]), (128, D)))
    for k, v in CONSTS.items():
        sh[k] = v
    x = np.asarray(inp["L0_attn_a_x"])
    in_maps = [dict(sh, x=_cc(x[b])) for b in range(8)]
    res = run_bass_kernel_spmd(nc, in_maps, core_ids=list(range(8)))
    return np.stack([r["out"] for r in res.results], axis=0)


# version 12
# speedup vs baseline: 1.0057x; 1.0057x over previous
import numpy as np
import ml_dtypes
import concourse.bass as bass
import concourse.mybir as mybir
from concourse.bass_utils import run_bass_kernel_spmd
from contextlib import ExitStack

F32 = mybir.dt.float32
BF16 = mybir.dt.bfloat16
AF = mybir.ActivationFunctionType
ALU = mybir.AluOpType
AX = mybir.AxisListType

T = 2048
D = 1024
NCH = 16
NFC = 8
NEG = -1.0e30
NH_A, DH_A = 16, 64
HEADS_F, DH_F, INNER, WINDOW, HID = 8, 64, 512, 512, 2730
QK_SCALE = 8.0

STAGE = 4


def _bf(x):
    return np.ascontiguousarray(np.asarray(x, np.float32)).astype(ml_dtypes.bfloat16)


def host_constants():
    c = {}
    ii = np.arange(128)
    c["id128_bf"] = _bf(np.eye(128))
    c["triSI_bf"] = _bf((ii[:, None] <= ii[None, :]).astype(np.float32))
    c["triL_f32"] = np.where(ii[None, :] < ii[:, None], np.float32(NEG), np.float32(0)).astype(np.float32)
    c["triR_f32"] = np.where(ii[None, :] > ii[:, None], np.float32(NEG), np.float32(0)).astype(np.float32)
    _s2 = np.stack([np.r_[np.ones(64), np.zeros(64)], np.r_[np.zeros(64), np.ones(64)]])
    c["sel2_bf"] = _bf(_s2)
    c["sel2_f32"] = _s2.astype(np.float32)
    c["ones_row_bf"] = _bf(np.ones((1, 128)))
    c["id16_f32"] = np.eye(16, dtype=np.float32)
    half = np.arange(0, DH_F, 2, dtype=np.float64)
    inv_freq = 1.0 / 10000.0 ** (half / DH_F)
    t = np.arange(T, dtype=np.float64)
    freqs = np.concatenate([t[:, None] * inv_freq[None, :]] * 2, axis=-1)
    base = (half + 0.4 * DH_F) / (1.4 * DH_F)
    power = (t - T // 2) / float(WINDOW // 2)
    scale = np.concatenate([base[None, :] ** power[:, None]] * 2, axis=-1)
    cos, sin = np.cos(freqs), np.sin(freqs)
    sgn = np.concatenate([-np.ones(32), np.ones(32)])[None, :]
    c["qC_bf"] = _bf(cos * scale * QK_SCALE)
    c["qS_bf"] = _bf(sin * scale * sgn * QK_SCALE)
    c["kC_bf"] = _bf(cos / scale)
    c["kS_bf"] = _bf(sin / scale * sgn)
    return c


CONSTS = host_constants()


class Prog:
    ENGS = ("tensor", "vector", "scalar", "sync", "gpsimd")

    def __init__(self, nc, ctx):
        self.nc = nc
        self.ctx = ctx
        self.buf = {e: [] for e in self.ENGS}
        self.semh = {}
        self.cnt = {}
        self.desc = {}

    def sem(self, name):
        if name not in self.semh:
            self.semh[name] = self.ctx.enter_context(self.nc.semaphore(name))
            self.cnt[name] = 0
        return self.semh[name]

    def now(self, name):
        self.sem(name)
        return self.cnt[name]

    def _emit(self, eng, fn, inc, by, waits):
        for w in waits:
            if w is None:
                continue
            (s, v) = w
            if v is None or v == 0:
                continue
            h = self.sem(s)
            self.buf[eng].append(lambda e, h=h, v=v: e.wait_ge(h, v))
        if inc is not None:
            h = self.sem(inc)
            self.cnt[inc] += by
            tgt = self.cnt[inc]
            self.buf[eng].append(lambda e, fn=fn, h=h, by=by: fn(e).then_inc(h, by))
            return tgt
        self.buf[eng].append(fn)
        return None

    def t(self, fn, waits=()):
        return self._emit("tensor", fn, "te", 1, waits)

    def v(self, fn, waits=()):
        return self._emit("vector", fn, "ve", 1, waits)

    def s(self, fn, waits=()):
        return self._emit("scalar", fn, "sc", 1, waits)

    def g(self, fn, waits=()):
        return self._emit("gpsimd", fn, "gp", 1, waits)

    def dma(self, q, out, in_, waits=(), slow=False):
        w = list(waits) + [("q_" + q, self.now("q_" + q))]
        return self._emit(q, lambda e, o=out, i=in_: e.dma_start(
            out=o, in_=i, allow_slow_non_contiguous=slow), "q_" + q, 16, w)

    def dmat(self, out, in_, waits=()):
        w = list(waits) + [("q_sync", self.now("q_sync"))]
        return self._emit("sync", lambda e, o=out, i=in_: e.dma_start_transpose(out=o, in_=i),
                          "q_sync", 16, w)

    def barrier(self, engs=None):
        engs = engs or self.ENGS
        for (e, q) in (("sync", "q_sync"), ("scalar", "q_scalar"), ("gpsimd", "q_gpsimd")):
            if e in engs:
                h = self.sem(q)
                v = self.cnt[q]
                if v:
                    self.buf[e].append(lambda x, h=h, v=v: x.wait_ge(h, v))
        toks = {}
        for e in engs:
            toks[e] = self._emit(e, lambda x: x.drain(), "bar_" + e, 1, ())
        for e in engs:
            for o in engs:
                if o != e:
                    h = self.sem("bar_" + o)
                    self.buf[e].append(lambda x, h=h, v=toks[o]: x.wait_ge(h, v))

    def finalize(self, block):
        for e in self.ENGS:
            lst = self.buf[e]
            getattr(block, e)(lambda eng, lst=lst: [f(eng) for f in lst])


class Arena:
    def __init__(self, tensor, cols):
        self.t = tensor
        self.cols = cols
        self.off = 0
        self.hw = 0
        self.marks = []

    def alloc(self, n, align=16):
        o = (self.off + align - 1) // align * align
        assert o + n <= self.cols, f"arena overflow {o}+{n}>{self.cols}"
        self.off = o + n
        self.hw = max(self.hw, self.off)
        return o

    def push(self):
        self.marks.append(self.off)

    def pop(self):
        self.off = self.marks.pop()

    @property
    def pstride(self):
        return self.t[:, 0:1].ap[0][0]

    def ap(self, off, dims, p=128, base_p=0):
        dd = [[self.pstride, p]] + [list(d) for d in dims]
        a = bass.AP(self.t[:, 0:1].tensor, off, dd)
        if base_p:
            a = bass.AP(a.tensor, off + base_p * 0, dd)
        return a

    def r(self, off, n, p0=0, p1=128):
        return self.t[p0:p1, off:off + n]


def build_program():
    nc = bass.Bass("TRN2", target_bir_lowering=False, debug=False)
    I = {}

    def din(name, shape, dtype=F32):
        I[name] = nc.dram_tensor(name, list(shape), dtype, kind="ExternalInput").ap()

    din("x", (T, D))
    out = nc.dram_tensor("out", [T, D], F32, kind="ExternalOutput").ap()
    wscr = nc.dram_tensor("wscr", [3 * D, D], BF16, kind="Internal").ap()
    bscr = nc.dram_tensor("bscr", [3 * T, D], BF16, kind="Internal").ap()

    din("a_WqT", (D, D)); din("a_WkT", (D, D)); din("a_WvT", (D, D)); din("a_WoT", (D, D))
    din("a_bwT", (D, NH_A))
    din("a_rin", (128, 8)); din("a_rk", (128, 8)); din("a_rv", (128, 8))
    din("a_bo_row", (1, D)); din("a_lnw_bc", (128, D)); din("a_lnb_bc", (128, D))
    din("a_bb_col", (NH_A, 1)); din("a_bbl_col", (NH_A, 1))
    for l in range(2):
        din(f"f_wqkv{l}", (D, 3 * INNER))
        din(f"f_wo{l}", (INNER, D))
        din(f"f_w1{l}", (D, 2 * HID))
        din(f"f_w2a{l}", (HID + 1, D))
        din(f"f_b1a{l}", (128, 22)); din(f"f_b1g{l}", (128, 22))
        din(f"f_lng_row{l}", (1, D)); din(f"f_lnb_row{l}", (1, D))
        din(f"f_qs{l}_bc", (128, 64)); din(f"f_ks{l}_bc", (128, 64))
        din(f"f_qs{l}_sw", (128, 64)); din(f"f_ks{l}_sw", (128, 64))
    din("b_WqT", (D, D)); din("b_WkT", (D, D)); din("b_WvT", (D, D))
    din("b_bq_row", (1, D)); din("b_bk_row", (1, D)); din("b_bv_row", (1, D))
    din("b_g_bc", (128, D)); din("b_b_bc", (128, D))
    for k, v in CONSTS.items():
        din(k, v.shape, BF16 if v.dtype == ml_dtypes.bfloat16 else F32)

    ctx = ExitStack()
    with ctx:
        af_t = ctx.enter_context(nc.sbuf_tensor("af", [128, 21804], F32))
        ab_t = ctx.enter_context(nc.sbuf_tensor("ab", [128, 57480], BF16))
        PS = ctx.enter_context(nc.psum_tensor("PS", [128, 4096], F32))
        A4 = Arena(af_t, 21804)
        A2 = Arena(ab_t, 57480)
        p = Prog(nc, ctx)

        def named(name, shape, dt=F32):
            return ctx.enter_context(nc.sbuf_tensor(name, shape, dt))

        id128 = named("id128", [128, 128], BF16)
        triSI = named("triSI", [128, 128], BF16)
        triL = named("triL", [128, 128], F32)
        triR = named("triR", [128, 128], F32)
        sel2 = named("sel2", [2, 128], BF16)
        sel2f = named("sel2f", [2, 128], F32)
        ones_row = named("ones_rowX", [1, 128], BF16)
        id16 = named("id16", [16, 16], F32)
        ones16c = named("ones16c", [16, 1], F32)
        cvals = named("cvals", [128, 8], F32)

        for (sb, nm) in ((id128, "id128_bf"), (triSI, "triSI_bf"), (triL, "triL_f32"),
                         (triR, "triR_f32"), (sel2, "sel2_bf"), (sel2f, "sel2_f32"),
                         (ones_row, "ones_row_bf"),
                         (id16, "id16_f32")):
            p.dma("scalar", sb[:], I[nm])
        p.v(lambda e: e.memset(ones16c[:], 1.0))
        for (k, val) in enumerate((1e-5, 1e-6, 1.0 / D, -1.0, 1.0, 0.0, 2.0, 0.5)):
            p.v(lambda e, k=k, val=val: e.memset(cvals[:, k:k + 1], val))
            nc.const_aps.register(F32, val, cvals[:, k:k + 1])
        p.barrier()

        def bank(b, p0, p1, c0, n):
            return PS[p0:p1, 512 * b + c0:512 * b + c0 + n]

        BN6 = A4.alloc(12)
        bn6 = af_t[:, BN6:BN6 + 12]
        MV = A4.alloc(32)
        NMU = A4.alloc(16)
        RSG = A4.alloc(16)
        SC0 = A4.alloc(16)
        TSa = A4.alloc(16)
        TSb = A4.alloc(16)
        NMU2 = A4.alloc(16)
        RSG2 = A4.alloc(16)

        def mv_view(which):
            return bass.AP(af_t[:, 0:1].tensor, MV + which, [[A4.pstride, 128], [2, 16]])

        def tm_stats(src_ap, dst2_ap, waits):
            p.v(lambda e: e.bn_stats(out=bn6[:, 0:6], in_=src_ap[:, 0:512]),
                waits=list(waits) + [("ve", p.now("ve"))])
            t2 = p.v(lambda e: e.bn_stats(out=bn6[:, 6:12], in_=src_ap[:, 512:1024]))
            return p.v(lambda e: e.bn_aggr(out=dst2_ap, in_=bn6), waits=[("ve", t2)])

        def stats_rowmath(last_tok, with_rms0, nmu_off=NMU, rsg_off=RSG, c0=0, c1=16):
            n = c1 - c0
            mvv = lambda which: bass.AP(af_t[:, 0:1].tensor, MV + 2 * c0 + which,
                                        [[A4.pstride, 128], [2, n]])
            p.v(lambda e: e.tensor_scalar_mul(out=af_t[:, nmu_off + c0:nmu_off + c1],
                                              in0=mvv(0),
                                              scalar1=-1.0), waits=[("ve", last_tok)])
            a1 = p.s(lambda e: e.activation(out=af_t[:, TSa + c0:TSa + c1], in_=mvv(1),
                                            func=AF.Sqrt,
                                            bias=1e-5), waits=[("ve", last_tok)])
            r1 = p.v(lambda e: e.reciprocal(out=af_t[:, rsg_off + c0:rsg_off + c1],
                                            in_=af_t[:, TSa + c0:TSa + c1]), waits=[("sc", a1)])
            if not with_rms0:
                return r1
            q1 = p.v(lambda e: e.tensor_mul(out=af_t[:, TSa:TSa + 16], in0=af_t[:, rsg_off:rsg_off + 16],
                                            in1=af_t[:, rsg_off:rsg_off + 16]), waits=[("ve", r1)])
            q2 = p.v(lambda e: e.tensor_scalar(out=af_t[:, TSa:TSa + 16], in0=af_t[:, TSa:TSa + 16],
                                               scalar1=-1e-5, scalar2=1.0 + 1e-6,
                                               op0=ALU.mult, op1=ALU.add), waits=[("ve", q1)])
            q3 = p.s(lambda e: e.activation(out=af_t[:, TSb:TSb + 16], in_=af_t[:, TSa:TSa + 16],
                                            func=AF.Sqrt), waits=[("ve", q2)])
            q4 = p.v(lambda e: e.reciprocal(out=af_t[:, TSa:TSa + 16], in_=af_t[:, TSb:TSb + 16]),
                     waits=[("sc", q3)])
            return p.v(lambda e: e.tensor_mul(out=af_t[:, SC0:SC0 + 16],
                                              in0=af_t[:, rsg_off:rsg_off + 16],
                                              in1=af_t[:, TSa:TSa + 16]), waits=[("ve", q4)])

        def emit_attn_a(x_src, x_dst):
            A4.push(); A2.push()
            rin = named("a_rin_sb", [128, 8]); rkv = named("a_rk_sb", [128, 8])
            rvv = named("a_rv_sb", [128, 8])
            l1 = p.dma("scalar", rin[:], I["a_rin"])
            l2 = p.dma("scalar", rkv[:], I["a_rk"])
            l3 = p.dma("scalar", rvv[:], I["a_rv"])
            s1 = p.v(lambda e: e.tensor_mul(out=rkv[:], in0=rkv[:], in1=rin[:]),
                     waits=[("q_scalar", l3)])
            s2 = p.v(lambda e: e.tensor_mul(out=rvv[:], in0=rvv[:], in1=rin[:]), waits=[("ve", s1)])
            WBUF = A2.alloc(2 * NFC * D)
            WTMP = A4.alloc(2 * D)
            prev = None
            st_tok = None
            for (wi, (nm, scl)) in enumerate((("a_WqT", rin), ("a_WkT", rkv), ("a_WvT", rvv))):
                for f in range(NFC):
                    slot = WTMP + D * (f % 2)
                    lw = p.dma("scalar", af_t[:, slot:slot + D], I[nm][128 * f:128 * (f + 1), :],
                               waits=[("ve", prev)])
                    prev = p.v(lambda e, f=f, scl=scl, slot=slot: e.tensor_scalar_mul(
                        out=ab_t[:, WBUF + D * f:WBUF + D * (f + 1)],
                        in0=af_t[:, slot:slot + D], scalar1=scl[:, f:f + 1]),
                        waits=[("q_scalar", lw), ("ve", s2),
                               ("q_gpsimd", st_tok)])
                    st_tok = p.dma("gpsimd", wscr[D * wi + 128 * f:D * wi + 128 * (f + 1), :],
                                   ab_t[:, WBUF + D * f:WBUF + D * (f + 1)],
                                   waits=[("ve", prev)])
            XCH = A4.alloc(4 * D)
            lnt = None
            for qr in range(4):
                ld = p.dma("scalar",
                           bass.AP(af_t[:, 0:1].tensor, XCH, [[A4.pstride, 128], [D, 4], [1, D]]),
                           x_src[512 * qr:512 * (qr + 1), :].rearrange("(j p) d -> p j d", p=128),
                           waits=[("ve", lnt)])
                for j in range(4):
                    c = 4 * qr + j
                    lnt = tm_stats(af_t[:, XCH + D * j:XCH + D * (j + 1)],
                                   af_t[:, MV + 2 * c:MV + 2 * c + 2],
                                   [("q_scalar", ld)] if j == 0 else [])
            sc_tok = stats_rowmath(lnt, True)
            BWT = A2.alloc(NFC * 16)
            for f in range(NFC):
                slot = WTMP + D * (f % 2)
                lb = p.dma("scalar", af_t[:, slot:slot + 16], I["a_bwT"][128 * f:128 * (f + 1), :],
                           waits=[("ve", prev)])
                prev = p.v(lambda e, f=f, slot=slot: e.tensor_scalar_mul(
                    out=ab_t[:, BWT + 16 * f:BWT + 16 * (f + 1)],
                    in0=af_t[:, slot:slot + 16], scalar1=rin[:, f:f + 1]),
                    waits=[("q_scalar", lb)])
            RIN2 = A2.alloc(8)
            for f in range(NFC):
                prev = p.v(lambda e, f=f: e.tensor_mul(out=ab_t[:, RIN2 + f:RIN2 + f + 1],
                                                       in0=rin[:, f:f + 1], in1=rin[:, f:f + 1]),
                           waits=[("q_scalar", l1), ("ve", prev)])
            p.barrier()

            XN0TM = A2.alloc(4 * D)
            XN0FM = A2.alloc(NFC * 512)

            def make_xn0(qr):
                ld = p.dma("scalar",
                           bass.AP(af_t[:, 0:1].tensor, XCH, [[A4.pstride, 128], [D, 4], [1, D]]),
                           x_src[512 * qr:512 * (qr + 1), :].rearrange("(j p) d -> p j d", p=128),
                           waits=[("ve", p.now("ve"))])
                tvs = []
                for j in range(4):
                    c = 4 * qr + j
                    tvs.append(p.v(lambda e, c=c, j=j: e.tensor_scalar(
                        out=ab_t[:, XN0TM + D * j:XN0TM + D * (j + 1)],
                        in0=af_t[:, XCH + D * j:XCH + D * (j + 1)],
                        scalar1=af_t[:, NMU + c:NMU + c + 1], scalar2=af_t[:, SC0 + c:SC0 + c + 1],
                        op0=ALU.add, op1=ALU.mult), waits=[("q_scalar", ld)] if j == 0 else ()))
                in_ap = bass.AP(ab_t[:, 0:1].tensor, XN0TM, [[A2.pstride, 128], [128, 32], [1, 128]])
                out_ap = bass.AP(ab_t[:, 0:1].tensor, XN0FM, [[A2.pstride, 128], [128, 32], [1, 128]])
                return p.dmat(out_ap, in_ap, waits=[("ve", tvs[-1]), ("te", p.now("te"))])

            def xn0fm(f):
                return bass.AP(ab_t[:, 0:1].tensor, XN0FM + 128 * f,
                               [[A2.pstride, 128], [D, 4], [1, 128]])

            def xn0blk(tt, f):
                return ab_t[:, XN0FM + D * tt + 128 * f:XN0FM + D * tt + 128 * (f + 1)]

            XSQ = A2.alloc(512)
            for qr in range(4):
                tq = make_xn0(qr)
                sqv = None
                for f in range(NFC):
                    p.t(lambda e, qr=qr, f=f: e.matmul(
                        out=bank(qr, 0, 16, 0, 512),
                        lhsT=ab_t[:, BWT + 16 * f:BWT + 16 * (f + 1)],
                        rhs=xn0fm(f), start=(f == 0), stop=(f == NFC - 1)),
                        waits=[("q_sync", tq)] if f == 0 else ())
                    sqv = p.s(lambda e, f=f: e.activation(out=ab_t[:, XSQ:XSQ + 512], in_=xn0fm(f),
                                                          func=AF.Square),
                              waits=[("q_sync", tq), ("te", p.now("te"))])
                    p.t(lambda e, qr=qr, f=f: e.matmul(
                        out=bank(4 + qr, 0, 1, 0, 512),
                        lhsT=ab_t[:, RIN2 + f:RIN2 + f + 1],
                        rhs=ab_t[:, XSQ:XSQ + 512], start=(f == 0), stop=(f == NFC - 1)),
                        waits=[("sc", sqv)])
            mm_all = p.now("te")

            BET = A4.alloc(T); LNB = A4.alloc(T); DEN = A4.alloc(T); CCS = A4.alloc(T)
            GRW = A4.alloc(T); FQr = A4.alloc(T)
            EIr = BET
            EPr = LNB
            RM2 = A4.alloc(T)
            bbc = named("a_bb_sb", [16, 1]); bbl = named("a_bbl_sb", [16, 1])
            sbase = named("a_sbase", [16, 1])
            l4 = p.dma("scalar", bbc[:], I["a_bb_col"])
            l5 = p.dma("scalar", bbl[:], I["a_bbl_col"])
            t1 = p.s(lambda e: e.activation(out=af_t[0:16, BET:BET + T], in_=PS[0:16, 0:2048],
                                            func=AF.Sigmoid, bias=bbc[:]),
                     waits=[("te", mm_all), ("q_scalar", l4)])
            t2 = p.s(lambda e: e.activation(out=sbase[:], in_=bbl[:], func=AF.Sigmoid),
                     waits=[("q_scalar", l5)])
            t3 = p.v(lambda e: e.tensor_scalar_mul(out=af_t[0:16, BET:BET + T],
                                                   in0=af_t[0:16, BET:BET + T], scalar1=sbase[:]),
                     waits=[("sc", t1), ("sc", t2)])
            t4 = p.v(lambda e: e.tensor_scalar(out=af_t[0:16, BET:BET + T],
                                               in0=af_t[0:16, BET:BET + T],
                                               scalar1=BETA_MIN_, scalar2=BETA_MAX_,
                                               op0=ALU.max, op1=ALU.min), waits=[("ve", t3)])
            t5 = p.s(lambda e: e.activation(out=af_t[0:16, LNB:LNB + T], in_=af_t[0:16, BET:BET + T],
                                            func=AF.Ln), waits=[("ve", t4)])
            t6 = p.v(lambda e: e.tensor_tensor_scan(out=af_t[0:16, DEN:DEN + T],
                                                    data0=af_t[0:16, BET:BET + T],
                                                    data1=ones16c[:].broadcast_to((16, T)),
                                                    initial=0.0, op0=ALU.mult, op1=ALU.add),
                     waits=[("ve", t4)])
            t7 = p.v(lambda e: e.tensor_scalar_add(out=af_t[0:16, DEN:DEN + T],
                                                   in0=af_t[0:16, DEN:DEN + T], scalar1=1e-6),
                     waits=[("ve", t6)])
            t8 = p.v(lambda e: e.reciprocal(out=af_t[0:16, DEN:DEN + T], in_=af_t[0:16, DEN:DEN + T]),
                     waits=[("ve", t7)])
            tc = None
            for c in range(NCH):
                tc = p.v(lambda e, c=c: e.tensor_tensor_scan(
                    out=af_t[0:16, CCS + 128 * c:CCS + 128 * (c + 1)],
                    data0=af_t[0:16, LNB + 128 * c:LNB + 128 * (c + 1)],
                    data1=af_t[0:16, LNB + 128 * c:LNB + 128 * (c + 1)],
                    initial=0.0, op0=ALU.add, op1=ALU.bypass),
                    waits=[("sc", t5)] if c == 0 else ())
            t9 = p.s(lambda e: e.activation(out=af_t[0:16, GRW:GRW + T], in_=af_t[0:16, CCS:CCS + T],
                                            func=AF.Exp), waits=[("ve", tc)])
            t10 = p.v(lambda e: e.tensor_mul(out=af_t[0:16, FQr:FQr + T], in0=af_t[0:16, GRW:GRW + T],
                                             in1=af_t[0:16, DEN:DEN + T]),
                      waits=[("sc", t9), ("ve", t8)])
            t11 = p.s(lambda e: e.activation(out=af_t[0:16, EIr:EIr + T], in_=af_t[0:16, CCS:CCS + T],
                                             func=AF.Exp, scale=-1.0), waits=[("ve", tc)])
            clast = bass.AP(af_t[:, 0:1].tensor, CCS + 127, [[A4.pstride, 16], [128, 16], [0, 128]])
            t12 = p.v(lambda e: e.tensor_sub(
                out=af_t[0:16, EPr:EPr + T].rearrange("p (a b) -> p a b", a=16),
                in0=clast,
                in1=af_t[0:16, CCS:CCS + T].rearrange("p (a b) -> p a b", a=16)),
                waits=[("ve", tc)])
            t13 = p.s(lambda e: e.activation(out=af_t[0:16, EPr:EPr + T], in_=af_t[0:16, EPr:EPr + T],
                                             func=AF.Exp), waits=[("ve", t12)])
            t14 = p.s(lambda e: e.activation(out=af_t[0:1, RM2:RM2 + T], in_=PS[0:1, 2048:4096],
                                             func=AF.Sqrt, bias=1e-6, scale=1.0 / D),
                      waits=[("te", mm_all)])
            t15 = p.v(lambda e: e.reciprocal(out=af_t[0:1, RM2:RM2 + T], in_=af_t[0:1, RM2:RM2 + T]),
                      waits=[("sc", t14)])
            p.barrier()
            FQT = A4.alloc(256); EIT = A4.alloc(256); EPT = A4.alloc(256); RM2T = A4.alloc(16)
            GBIG = A4.alloc(128)
            for (src, dst, bnk) in ((FQr, FQT, 0), (EIr, EIT, 1), (EPr, EPT, 2)):
                for c in range(NCH):
                    p.t(lambda e, src=src, c=c, bnk=bnk: e.matmul(
                        out=bank(bnk, 0, 128, 16 * c, 16),
                        lhsT=af_t[0:16, src + 128 * c:src + 128 * (c + 1)],
                        rhs=id16[:], start=True, stop=True))
            for c in range(NCH):
                p.t(lambda e, c=c: e.matmul(
                    out=bank(3, 0, 128, c, 1),
                    lhsT=af_t[0:1, RM2 + 128 * c:RM2 + 128 * (c + 1)],
                    rhs=id16[0:1, 0:1], start=True, stop=True))
            tmm = p.now("te")
            e1 = p.v(lambda e: e.tensor_copy(out=af_t[:, FQT:FQT + 256], in_=PS[:, 0:256]),
                     waits=[("te", tmm)])
            e2 = p.v(lambda e: e.tensor_copy(out=af_t[:, EIT:EIT + 256], in_=PS[:, 512:768]))
            e3 = p.v(lambda e: e.tensor_copy(out=af_t[:, EPT:EPT + 256], in_=PS[:, 1024:1280]))
            e4 = p.v(lambda e: e.tensor_copy(out=af_t[:, RM2T:RM2T + 16], in_=PS[:, 1536:1552]))
            for c in range(NCH):
                p.v(lambda e, c=c: e.tensor_scalar_mul(out=af_t[:, EIT + 16 * c:EIT + 16 * (c + 1)],
                                                       in0=af_t[:, EIT + 16 * c:EIT + 16 * (c + 1)],
                                                       scalar1=af_t[:, RM2T + c:RM2T + c + 1]),
                    waits=[("ve", e4)] if c == 0 else ())
                p.v(lambda e, c=c: e.tensor_scalar_mul(out=af_t[:, EPT + 16 * c:EPT + 16 * (c + 1)],
                                                       in0=af_t[:, EPT + 16 * c:EPT + 16 * (c + 1)],
                                                       scalar1=af_t[:, RM2T + c:RM2T + c + 1]))
            GST = A4.alloc(128)
            rst = []
            for hp in range(8):
                src_ap = bass.AP(af_t[:, 0:1].tensor, GRW + 127 + A4.pstride * 2 * hp,
                                 [[A4.pstride, 2], [128, 16]])
                rst.append(p.dma("sync", af_t[0:2, GST + 16 * hp:GST + 16 * (hp + 1)], src_ap,
                                 waits=[("sc", p.now("sc")), ("ve", p.now("ve"))] if hp == 0 else (),
                                 slow=True))
            tg = p.t(lambda e: e.matmul(out=bank(4, 0, 128, 0, 128), lhsT=sel2f[:],
                                        rhs=af_t[0:2, GST:GST + 128], start=True, stop=True),
                     waits=[("q_sync", rst[-1])])
            p.v(lambda e: e.tensor_copy(out=af_t[:, GBIG:GBIG + 128], in_=PS[:, 2048:2176]),
                waits=[("te", tg)])
            p.barrier()

            QTM = A2.alloc(4 * D)
            KRTM = A2.alloc(4 * D)
            KSTM = A2.alloc(4 * D)
            VPTM = A2.alloc(4 * D)
            QFM = A2.alloc(NFC * 512)
            KFM = A2.alloc(NFC * 512)
            OFM = A2.alloc(NFC * 512)
            TTB = A2.alloc(16 * 128)
            MPADe = A2.alloc(512)
            MPADo = A2.alloc(512)
            MF = DEN
            MTMP = DEN + 512
            XIN = FQr
            XIN2 = FQr + D
            YY = GRW
            YT = CCS
            lnw_bc = named("a_lnw_sb", [128, D], BF16)
            lnb_bc = named("a_lnb_sb", [128, D], BF16)
            bo_row = named("a_bo_sb", [1, D], BF16)
            p.dma("gpsimd", lnw_bc[:], I["a_lnw_bc"])
            p.dma("gpsimd", lnb_bc[:], I["a_lnb_bc"])
            p.dma("gpsimd", bo_row[:], I["a_bo_row"])
            mz1 = p.v(lambda e: e.memset(ab_t[:, MPADe:MPADe + 512], 0.0))
            mz2 = p.v(lambda e: e.memset(ab_t[:, MPADo:MPADo + 512], 0.0))
            mz3 = p.v(lambda e: e.memset(af_t[:, MF:MF + 512], 0.0))
            p.barrier()

            wmm_hist = {}
            for qr in range(4):
                tq = make_xn0(qr)
                for (wi, kind) in ((0, "q"), (1, "k"), (2, "v")):
                    wslot = WBUF + (NFC * D) * (wi % 2)
                    wld = p.dma("gpsimd",
                                bass.AP(ab_t[:, 0:1].tensor, wslot,
                                        [[A2.pstride, 128], [D, NFC], [1, D]]),
                                wscr[D * wi:D * (wi + 1), :].rearrange("(f p) d -> p f d", p=128),
                                waits=[("te", wmm_hist.get(wi % 2, 0))])
                    for tt in range(4):
                        c = 4 * qr + tt
                        mmt = None
                        for oseg in range(2):
                            for f in range(NFC):
                                mmt = p.t(lambda e, oseg=oseg, f=f, tt=tt, wslot=wslot: e.matmul(
                                    out=bank(oseg, 0, 128, 0, 512),
                                    lhsT=xn0blk(tt, f),
                                    rhs=ab_t[:, wslot + D * f + 512 * oseg:wslot + D * f + 512 * (oseg + 1)],
                                    start=(f == 0), stop=(f == NFC - 1)),
                                    waits=([("q_sync", tq), ("q_gpsimd", wld),
                                            ("ve", p.now("ve")), ("sc", p.now("sc"))]
                                           if (f == 0 and oseg == 0) else ()))
                        if kind == "q":
                            p.v(lambda e, tt=tt, c=c: e.tensor_tensor(
                                out=ab_t[:, QTM + D * tt:QTM + D * (tt + 1)].rearrange(
                                    "p (a b) -> p a b", a=16),
                                in0=PS[:, 0:1024].rearrange("p (a b) -> p a b", a=16),
                                in1=bass.AP(af_t[:, 0:1].tensor, FQT + 16 * c,
                                            [[A4.pstride, 128], [1, 16], [0, 64]]),
                                op=ALU.mult), waits=[("te", mmt)])
                        elif kind == "k":
                            krc = p.s(lambda e, tt=tt: e.copy(
                                out=ab_t[:, KRTM + D * tt:KRTM + D * (tt + 1)],
                                in_=PS[:, 0:1024]), waits=[("te", mmt)])
                            p.v(lambda e, tt=tt, c=c: e.tensor_tensor(
                                out=ab_t[:, KSTM + D * tt:KSTM + D * (tt + 1)].rearrange(
                                    "p (a b) -> p a b", a=16),
                                in0=PS[:, 0:1024].rearrange("p (a b) -> p a b", a=16),
                                in1=bass.AP(af_t[:, 0:1].tensor, EPT + 16 * c,
                                            [[A4.pstride, 128], [1, 16], [0, 64]]),
                                op=ALU.mult), waits=[("te", mmt), ("sc", krc)])
                        else:
                            p.v(lambda e, tt=tt, c=c: e.tensor_scalar_mul(
                                out=ab_t[:, VPTM + D * tt:VPTM + D * (tt + 1)],
                                in0=PS[:, 0:1024],
                                scalar1=af_t[:, RM2T + c:RM2T + c + 1]), waits=[("te", mmt)])
                    wmm_hist[wi % 2] = p.now("te")
                sc_now = p.now("sc")
                ve_now = p.now("ve")
                trq = []
                for (SRC, DST) in ((QTM, QFM), (KRTM, KFM)):
                    in_ap = bass.AP(ab_t[:, 0:1].tensor, SRC, [[A2.pstride, 128], [128, 32], [1, 128]])
                    out_ap = bass.AP(ab_t[:, 0:1].tensor, DST, [[A2.pstride, 128], [128, 32], [1, 128]])
                    trq.append(p.dmat(out_ap, in_ap,
                                      waits=([("ve", ve_now), ("sc", sc_now)]
                                             if SRC == QTM else ())))
                tr_done = trq[-1]

                for tt in range(4):
                    c = 4 * qr + tt
                    st_toks = {}
                    for hp in range(8):
                        for par in range(2):
                            h = 2 * hp + par
                            sb_ = 6 + (h % 2)
                            mm = p.t(lambda e, hp=hp, par=par, tt=tt, sb_=sb_: e.matmul(
                                out=bank(sb_, 0, 128, 0, 128),
                                lhsT=ab_t[64 * par:64 * par + 64,
                                          KFM + D * tt + 128 * hp:KFM + D * tt + 128 * (hp + 1)],
                                rhs=ab_t[64 * par:64 * par + 64,
                                         QFM + D * tt + 128 * hp:QFM + D * tt + 128 * (hp + 1)],
                                start=True, stop=True),
                                waits=([("q_sync", tr_done), ("ve", p.now("ve"))] if h == 0 else
                                       [("ve", st_toks.get(h - 2, None))]))
                            st_toks[h] = p.v(lambda e, h=h, c=c, sb_=sb_: e.scalar_tensor_tensor(
                                out=ab_t[:, TTB + 128 * h:TTB + 128 * (h + 1)],
                                in0=bank(sb_, 0, 128, 0, 128),
                                scalar=af_t[:, EIT + 16 * c + h:EIT + 16 * c + h + 1],
                                in1=triSI[:], op0=ALU.mult, op1=ALU.mult),
                                waits=[("te", mm)])
                    otok = None
                    for hp in range(8):
                        for par in range(2):
                            h = 2 * hp + par
                            tp = (0, 64 * par)
                            MP = MPADe if par == 0 else MPADo
                            if c > 0:
                                p.t(lambda e, hp=hp, par=par, tt=tt, tp=tp, MP=MP: e.matmul(
                                    out=PS[64 * par:64 * par + 64, 128 * hp:128 * hp + 128],
                                    lhsT=ab_t[:, MP + 64 * hp:MP + 64 * (hp + 1)],
                                    rhs=ab_t[:, QFM + D * tt + 128 * hp:QFM + D * tt + 128 * (hp + 1)],
                                    start=True, stop=False, tile_position=tp),
                                    waits=[("ve", st_toks[h])] if True else ())
                            otok = p.t(lambda e, hp=hp, par=par, tt=tt, h=h, tp=tp, c=c: e.matmul(
                                out=PS[64 * par:64 * par + 64, 128 * hp:128 * hp + 128],
                                lhsT=ab_t[:, VPTM + D * tt + 64 * h:VPTM + D * tt + 64 * (h + 1)],
                                rhs=ab_t[:, TTB + 128 * h:TTB + 128 * (h + 1)],
                                start=(c == 0), stop=True, tile_position=tp),
                                waits=[("ve", st_toks[h])])
                            p.t(lambda e, hp=hp, par=par, tt=tt, h=h, tp=tp: e.matmul(
                                out=PS[64 * par:64 * par + 64, 1024 + 64 * hp:1024 + 64 * (hp + 1)],
                                lhsT=ab_t[:, KSTM + D * tt + 64 * h:KSTM + D * tt + 64 * (h + 1)],
                                rhs=ab_t[:, VPTM + D * tt + 64 * h:VPTM + D * tt + 64 * (h + 1)],
                                start=True, stop=True, tile_position=tp))
                    kv_tok = p.now("te")
                    oev = p.s(lambda e, tt=tt: e.copy(
                        out=bass.AP(ab_t[:, 0:1].tensor, OFM + 128 * tt,
                                    [[A2.pstride, 128], [512, 8], [1, 128]]),
                        in_=PS[:, 0:1024].rearrange("p (a b) -> p a b", a=8)),
                        waits=[("te", kv_tok)])
                    u1 = p.v(lambda e, c=c: e.tensor_tensor(
                        out=af_t[:, MTMP:MTMP + 512], in0=af_t[:, MF:MF + 512],
                        in1=bass.AP(af_t[:, 0:1].tensor, GBIG + c,
                                    [[A4.pstride, 128], [16, 8], [0, 64]]),
                        op=ALU.mult), waits=[("te", kv_tok), ("ve", mz3)])
                    u2 = p.v(lambda e: e.tensor_add(out=af_t[:, MF:MF + 512],
                                                    in0=af_t[:, MTMP:MTMP + 512],
                                                    in1=PS[:, 1024:1536]), waits=[("ve", u1)])
                    u3 = p.v(lambda e: e.tensor_copy(out=ab_t[0:64, MPADe:MPADe + 512],
                                                     in_=af_t[0:64, MF:MF + 512]),
                             waits=[("ve", u2), ("ve", mz1)])
                    u4 = p.v(lambda e: e.tensor_copy(out=ab_t[64:128, MPADo:MPADo + 512],
                                                     in_=af_t[64:128, MF:MF + 512]),
                             waits=[("ve", u2), ("ve", mz2)])
                    p.t(lambda e: e.nop(), waits=[("ve", u4), ("sc", oev)])

                wold = p.dma("gpsimd",
                             bass.AP(ab_t[:, 0:1].tensor, WBUF + NFC * D,
                                     [[A2.pstride, 128], [D, NFC], [1, D]]),
                             I["a_WoT"].rearrange("(f p) d -> p f d", p=128),
                             waits=[("te", wmm_hist.get(1, 0))])
                for tt in range(4):
                    c = 4 * qr + tt
                    mmt = None
                    for oseg in range(2):
                        for f in range(NFC):
                            mmt = p.t(lambda e, oseg=oseg, f=f, tt=tt: e.matmul(
                                out=bank(2 + oseg, 0, 128, 0, 512),
                                lhsT=ab_t[:, OFM + 512 * f + 128 * tt:OFM + 512 * f + 128 * (tt + 1)],
                                rhs=ab_t[:, WBUF + NFC * D + D * f + 512 * oseg:WBUF + NFC * D + D * f + 512 * (oseg + 1)],
                                start=(f == 0), stop=False),
                                waits=(([("q_gpsimd", wold)] if tt == 0 else [])
                                       + ([("ve", p.now("ve")), ("sc", p.now("sc"))]
                                          if (f == 0 and oseg == 0) else [])))
                        mmt = p.t(lambda e, oseg=oseg, tt=tt: e.matmul(
                            out=bank(2 + oseg, 0, 128, 0, 512),
                            lhsT=ones_row[0:1, 0:128],
                            rhs=bo_row[0:1, 512 * oseg:512 * (oseg + 1)],
                            start=False, stop=True))
                    ld = p.dma("scalar", af_t[:, XIN:XIN + D], x_src[128 * c:128 * (c + 1), :],
                               waits=[("ve", p.now("ve"))])
                    w1_ = p.v(lambda e, c=c: e.tensor_scalar(
                        out=af_t[:, XIN2:XIN2 + D], in0=af_t[:, XIN:XIN + D],
                        scalar1=af_t[:, NMU + c:NMU + c + 1], scalar2=af_t[:, RSG + c:RSG + c + 1],
                        op0=ALU.add, op1=ALU.mult), waits=[("q_scalar", ld)])
                    w2_ = p.v(lambda e: e.tensor_add(out=af_t[:, YY:YY + D],
                                                     in0=af_t[:, XIN2:XIN2 + D], in1=PS[:, 1024:2048]),
                              waits=[("ve", w1_), ("te", mmt)])
                    bnt = tm_stats(af_t[:, YY:YY + D], af_t[:, MV + 2 * c:MV + 2 * c + 2],
                                   [("ve", w2_)])
                    a1 = p.s(lambda e, c=c: e.activation(out=af_t[:, TSa:TSa + 1],
                                                         in_=af_t[:, MV + 2 * c + 1:MV + 2 * c + 2],
                                                         func=AF.Sqrt, bias=1e-5),
                             waits=[("ve", bnt)])
                    a2 = p.v(lambda e: e.reciprocal(out=af_t[:, TSb:TSb + 1], in_=af_t[:, TSa:TSa + 1]),
                             waits=[("sc", a1)])
                    a3 = p.v(lambda e, c=c: e.tensor_scalar(
                        out=af_t[:, YT:YT + D], in0=af_t[:, YY:YY + D],
                        scalar1=af_t[:, MV + 2 * c:MV + 2 * c + 1], scalar2=af_t[:, TSb:TSb + 1],
                        op0=ALU.subtract, op1=ALU.mult),
                        waits=[("ve", a2), ("q_scalar", p.now("q_scalar"))])
                    a4 = p.v(lambda e: e.tensor_mul(out=af_t[:, YT:YT + D], in0=af_t[:, YT:YT + D],
                                                    in1=lnw_bc[:]), waits=[("ve", a3)])
                    a5 = p.v(lambda e: e.tensor_add(out=af_t[:, YT:YT + D], in0=af_t[:, YT:YT + D],
                                                    in1=lnb_bc[:]), waits=[("ve", a4)])
                    a6 = p.v(lambda e: e.tensor_add(out=af_t[:, YT:YT + D], in0=af_t[:, YT:YT + D],
                                                    in1=af_t[:, XIN:XIN + D]),
                             waits=[("ve", a5)])
                    p.dma("scalar", x_dst[128 * c:128 * (c + 1), :], af_t[:, YT:YT + D],
                          waits=[("ve", a6)])
                wmm_hist[1] = p.now("te")
            p.barrier()
            A4.pop(); A2.pop()

        BETA_MIN_, BETA_MAX_ = 0.9, 0.9995

        def emit_ffn(x_src, x_dst, nm_sfx, stats_for_next=False):
            A4.push(); A2.push()
            XF = A4.alloc(16 * D)
            WTMP0 = A4.alloc(4 * D)
            RNC = A4.alloc(16)
            NM_ = A4.alloc(8)
            B1A = A4.alloc(22)
            B1G = A4.alloc(22)

            def xf(c):
                return af_t[:, XF + D * c:XF + D * (c + 1)]

            lnt = None
            app = {}
            for ehalf in (0, 1):
                for qr in (2 * ehalf, 2 * ehalf + 1):
                    ld = p.dma("scalar",
                               bass.AP(af_t[:, 0:1].tensor, XF + 4 * D * qr,
                                       [[A4.pstride, 128], [D, 4], [1, D]]),
                               x_src[512 * qr:512 * (qr + 1), :].rearrange("(j p) d -> p j d", p=128))
                    for j in range(4):
                        c = 4 * qr + j
                        lnt = tm_stats(xf(c), af_t[:, MV + 2 * c:MV + 2 * c + 2],
                                       [("q_scalar", ld)] if j == 0 else [])
                rm_e = stats_rowmath(lnt, False, c0=8 * ehalf, c1=8 * ehalf + 8)
                ap_last = None
                for c in range(8 * ehalf, 8 * ehalf + 8):
                    ap_last = p.v(lambda e, c=c: e.tensor_scalar(
                        out=xf(c), in0=xf(c),
                        scalar1=af_t[:, NMU + c:NMU + c + 1], scalar2=af_t[:, RSG + c:RSG + c + 1],
                        op0=ALU.add, op1=ALU.mult),
                        waits=[("ve", rm_e)] if c == 8 * ehalf else ())
                app[ehalf] = ap_last

            def build_shift_full(XBA, XF0, HSH, c0=0, c1=NCH, extra0=()):
                n = c1 - c0
                w0 = list(extra0) + [("q_sync", p.now("q_sync")),
                                     ("q_scalar", p.now("q_scalar"))]
                dA = p.dma("gpsimd",
                           bass.AP(ab_t[:, 0:1].tensor, XBA + 512 * c0,
                                   [[A2.pstride, 128], [512, n], [1, 512]]),
                           bass.AP(af_t[:, 0:1].tensor, XF0 + D * c0,
                                   [[A4.pstride, 128], [D, n], [1, 512]]), waits=w0)
                mz = None
                if c0 == 0:
                    mz = p.v(lambda e: e.memset(ab_t[0:1, HSH:HSH + 512], 0.0),
                             waits=list(extra0) + [("q_gpsimd", p.now("q_gpsimd")),
                                    ("q_scalar", p.now("q_scalar")),
                                    ("q_sync", p.now("q_sync"))])
                p.dma("gpsimd",
                      bass.AP(ab_t[:, 0:1].tensor, HSH + 512 * c0 + A2.pstride,
                              [[A2.pstride, 64], [512, n], [1, 512]]),
                      bass.AP(af_t[:, 0:1].tensor, XF0 + D * c0 + 512,
                              [[A4.pstride, 64], [D, n], [1, 512]]))
                dB = p.dma("gpsimd",
                           bass.AP(ab_t[:, 0:1].tensor, HSH + 512 * c0 + 65 * A2.pstride,
                                   [[A2.pstride, 63], [512, n], [1, 512]]),
                           bass.AP(af_t[:, 0:1].tensor, XF0 + D * c0 + 512 + 64 * A4.pstride,
                                   [[A4.pstride, 63], [D, n], [1, 512]]))
                cb0 = max(c0, 1)
                dC = p.dma("gpsimd",
                           bass.AP(ab_t[:, 0:1].tensor, HSH + 512 * cb0,
                                   [[A2.pstride, 1], [512, c1 - cb0], [1, 512]]),
                           bass.AP(af_t[:, 0:1].tensor, XF0 + 512 + D * (cb0 - 1) + A4.pstride * 127,
                                   [[A4.pstride, 1], [D, c1 - cb0], [1, 512]]))
                return dA, dC, mz

            def shift_builder(stage_off, hsh_off, consume, extra0=()):
                cts = {}
                dms = {}
                for c in range(NCH):
                    sl = stage_off + D * (c % 2)
                    w = list(extra0) + [("ve", p.now("ve")), ("q_sync", p.now("q_sync")),
                                        ("q_scalar", p.now("q_scalar")),
                                        ("te", p.now("te")), ("sc", p.now("sc"))] if c == 0 else \
                        [("q_sync", p.now("q_sync")), ("q_scalar", p.now("q_scalar"))]
                    cts[c] = p.v(lambda e, c=c, sl=sl: e.tensor_copy(out=ab_t[:, sl:sl + D],
                                                                    in_=xf(c)), waits=w)
                    if c == 0:
                        mz = p.v(lambda e: e.memset(ab_t[0:1, hsh_off:hsh_off + 512], 0.0))
                        d1 = p.dma("scalar", bass.AP(ab_t[:, 0:1].tensor, hsh_off + A2.pstride,
                                                     [[A2.pstride, 127], [1, 512]]),
                                   bass.AP(ab_t[:, 0:1].tensor, sl + 512,
                                           [[A2.pstride, 127], [1, 512]]),
                                   waits=[("ve", cts[0])])
                    else:
                        d1 = p.dma("scalar", bass.AP(ab_t[:, 0:1].tensor, hsh_off + A2.pstride,
                                                     [[A2.pstride, 127], [1, 512]]),
                                   bass.AP(ab_t[:, 0:1].tensor, sl + 512,
                                           [[A2.pstride, 127], [1, 512]]),
                                   waits=[("ve", cts[c])])
                        d1 = p.dma("scalar", ab_t[0:1, hsh_off:hsh_off + 512],
                                   bass.AP(ab_t[:, 0:1].tensor,
                                           stage_off + D * ((c - 1) % 2) + 512 + A2.pstride * 127,
                                           [[A2.pstride, 1], [1, 512]]))
                    dms[c] = consume(c, sl, hsh_off, [("q_scalar", d1), ("ve", cts[c])] +
                                     ([("ve", mz)] if c == 0 else []))
                return dms

            HHALF = NCH * 512
            for l in range(2):
                A2.push()
                QFM = A2.alloc(NCH * 512)
                KFM = A2.alloc(NCH * 512)
                VTM = A2.alloc(NCH * 512)
                QSB = A2.alloc(128)
                A2.push()
                HFM = A2.alloc(NCH * D)
                A2.push()
                XBA = A2.alloc(NCH * 512)
                HSH1 = A2.alloc(NCH * 512)
                if l == 0:
                    e0a, e0b = [("ve", app[0])], [("ve", app[1])]
                else:
                    e0a = e0b = [("ve", fadd_last)]
                sh_ct, sh_d2, sh_mz = build_shift_full(XBA, XF, HSH1, 0, 8, extra0=e0a)
                sh_ct1, sh_d21, _ = build_shift_full(XBA, XF, HSH1, 8, NCH, extra0=e0b)
                trA = p.dmat(bass.AP(ab_t[:, 0:1].tensor, HFM,
                                     [[A2.pstride, 128], [128, 32], [1, 128]]),
                             bass.AP(ab_t[:, 0:1].tensor, XBA,
                                     [[A2.pstride, 128], [128, 32], [1, 128]]),
                             waits=[("q_gpsimd", sh_ct), ("te", p.now("te"))])
                trB = p.dmat(bass.AP(ab_t[:, 0:1].tensor, HFM + HHALF,
                                     [[A2.pstride, 128], [128, 32], [1, 128]]),
                             bass.AP(ab_t[:, 0:1].tensor, HSH1,
                                     [[A2.pstride, 128], [128, 32], [1, 128]]),
                             waits=[("q_gpsimd", sh_d2), ("ve", sh_mz)])
                trA1 = p.dmat(bass.AP(ab_t[:, 0:1].tensor, HFM + 4096,
                                      [[A2.pstride, 128], [128, 32], [1, 128]]),
                              bass.AP(ab_t[:, 0:1].tensor, XBA + 4096,
                                      [[A2.pstride, 128], [128, 32], [1, 128]]),
                              waits=[("q_gpsimd", sh_ct1)])
                trB1 = p.dmat(bass.AP(ab_t[:, 0:1].tensor, HFM + HHALF + 4096,
                                      [[A2.pstride, 128], [128, 32], [1, 128]]),
                              bass.AP(ab_t[:, 0:1].tensor, HSH1 + 4096,
                                      [[A2.pstride, 128], [128, 32], [1, 128]]),
                              waits=[("q_gpsimd", sh_d21)])
                sh_tok0 = trB
                sh_tok = trB1
                A2.pop()
                WSL = A2.alloc(NFC * 512)
                TBS = {k: A2.alloc(NCH * 64) for k in ("qC", "qS", "kC", "kS")}
                SQS = A2.alloc(512); QH = A2.alloc(512); T1 = A2.alloc(512); T2 = A2.alloc(512)
                QST = A2.alloc(2 * 512)

                tbl_hist = {}
                for (ti, (dst, nm, bc)) in enumerate((("qC", "qC_bf", f"f_qs{l}_bc"),
                                                      ("qS", "qS_bf", f"f_qs{l}_sw"),
                                                      ("kC", "kC_bf", f"f_ks{l}_bc"),
                                                      ("kS", "kS_bf", f"f_ks{l}_sw"))):
                    in_ap = bass.AP(I[nm].tensor, 0, [[64, 128], [128 * 64, NCH], [1, 64]])
                    out_ap = bass.AP(ab_t[:, 0:1].tensor, TBS[dst],
                                     [[A2.pstride, 128], [64, NCH], [1, 64]])
                    lt = p.dma("scalar", out_ap, in_ap,
                               waits=[("ve", p.now("ve")), ("q_sync", trA1)])
                    qsl = QSB + 64 * (ti % 2)
                    lq = p.dma("gpsimd", ab_t[:, qsl:qsl + 64], I[bc],
                               waits=[("ve", tbl_hist.get(ti - 2,
                                             fadd_last if l == 1 else 0))])
                    p.v(lambda e, dst=dst, qsl=qsl: e.tensor_tensor(
                        out=bass.AP(ab_t[:, 0:1].tensor, TBS[dst],
                                    [[A2.pstride, 128], [64, NCH], [1, 64]]),
                        in0=bass.AP(ab_t[:, 0:1].tensor, TBS[dst],
                                    [[A2.pstride, 128], [64, NCH], [1, 64]]),
                        in1=bass.AP(ab_t[:, 0:1].tensor, qsl,
                                    [[A2.pstride, 128], [0, NCH], [1, 64]]),
                        op=ALU.mult), waits=[("q_scalar", lt), ("q_gpsimd", lq)])
                    tbl_hist[ti] = p.now("ve")

                prev_kind_last = trB

                def hfm_blk(c, f):
                    o = HFM + (HHALF if f >= 4 else 0) + 512 * c + 128 * (f % 4)
                    return ab_t[:, o:o + 128]

                for (oi, kind) in ((0, "q"), (1, "k"), (2, "v")):
                    wld = p.dma("gpsimd", ab_t[:, WSL:WSL + NFC * 512].rearrange(
                                    "p (f d) -> p f d", f=NFC),
                                I[f"f_wqkv{l}"][:, 512 * oi:512 * (oi + 1)].rearrange(
                                    "(f p) d -> p f d", p=128),
                                waits=[("te", p.now("te")), ("q_sync", trA)])
                    cp_toks = {}
                    tr_toks = {}
                    kind_sync0 = prev_kind_last
                    for c in range(NCH):
                        bnk = c % 2
                        mmt = None
                        for f in range(NFC):
                            mmt = p.t(lambda e, c=c, f=f, bnk=bnk: e.matmul(
                                out=bank(bnk, 0, 128, 0, 512),
                                lhsT=hfm_blk(c, f),
                                rhs=ab_t[:, WSL + 512 * f:WSL + 512 * (f + 1)],
                                start=(f == 0), stop=(f == NFC - 1)),
                                waits=([("q_sync", sh_tok0), ("q_gpsimd", wld),
                                        ("sc", p.now("sc")), ("ve", p.now("ve"))]
                                       if (c == 0 and f == 0) else
                                       (([("q_sync", sh_tok)] if c == 8 else []) +
                                        [("sc", cp_toks.get(c - 2, 0)),
                                         ("ve", cp_toks.get(c - 2, 0) and 0)] if f == 0 else ())))
                        if kind == "v":
                            cp_toks[c] = p.s(lambda e, c=c: e.copy(
                                out=ab_t[:, VTM + 512 * c:VTM + 512 * (c + 1)],
                                in_=bank(c % 2, 0, 128, 0, 512)), waits=[("te", mmt)])
                            continue
                        sl2 = QST + 512 * (c % 2)
                        cp_toks[c] = p.s(lambda e, c=c, sl2=sl2: e.copy(
                            out=ab_t[:, sl2:sl2 + 512],
                            in_=bank(c % 2, 0, 128, 0, 512)),
                            waits=[("te", mmt), ("q_sync", tr_toks.get(c - 2, kind_sync0))])
                        sq = p.s(lambda e, sl2=sl2: e.activation(
                            out=ab_t[:, SQS:SQS + 512], in_=ab_t[:, sl2:sl2 + 512],
                            func=AF.Square), waits=[("sc", cp_toks[c]), ("ve", p.now("ve"))])
                        r1 = p.v(lambda e: e.reduce_sum(
                            out=af_t[:, RNC:RNC + 8],
                            in_=ab_t[:, SQS:SQS + 512].rearrange("p (h d) -> p h d", h=8),
                            axis=AX.X), waits=[("sc", sq)])
                        r2 = p.s(lambda e: e.activation(out=af_t[:, RNC + 8:RNC + 16],
                                                        in_=af_t[:, RNC:RNC + 8], func=AF.Sqrt),
                                 waits=[("ve", r1)])
                        r3 = p.v(lambda e: e.tensor_scalar_max(out=af_t[:, RNC + 8:RNC + 16],
                                                               in0=af_t[:, RNC + 8:RNC + 16],
                                                               scalar1=1e-12), waits=[("sc", r2)])
                        r4 = p.v(lambda e: e.reciprocal(out=af_t[:, RNC:RNC + 8],
                                                        in_=af_t[:, RNC + 8:RNC + 16]),
                                 waits=[("ve", r3)])
                        Ct, St = ("qC", "qS") if kind == "q" else ("kC", "kS")
                        h1 = p.v(lambda e, sl2=sl2: e.tensor_tensor(
                            out=ab_t[:, QH:QH + 512].rearrange("p (h d) -> p h d", h=8),
                            in0=ab_t[:, sl2:sl2 + 512].rearrange("p (h d) -> p h d", h=8),
                            in1=bass.AP(af_t[:, 0:1].tensor, RNC,
                                        [[A4.pstride, 128], [1, 8], [0, 64]]),
                            op=ALU.mult), waits=[("ve", r4)])
                        t1_ = p.v(lambda e, c=c, Ct=Ct: e.tensor_tensor(
                            out=ab_t[:, T1:T1 + 512].rearrange("p (h d) -> p h d", h=8),
                            in0=ab_t[:, QH:QH + 512].rearrange("p (h d) -> p h d", h=8),
                            in1=bass.AP(ab_t[:, 0:1].tensor, TBS[Ct] + 64 * c,
                                        [[A2.pstride, 128], [0, 8], [1, 64]]),
                            op=ALU.mult), waits=[("ve", h1)])
                        swap_ap = bass.AP(ab_t[:, 0:1].tensor, QH + 32,
                                          [[A2.pstride, 128], [64, 8], [-32, 2], [1, 32]])
                        t2_ = p.v(lambda e, c=c, St=St, swap_ap=swap_ap: e.tensor_tensor(
                            out=ab_t[:, T2:T2 + 512].rearrange("p (h a b) -> p h a b", h=8, a=2),
                            in0=swap_ap,
                            in1=bass.AP(ab_t[:, 0:1].tensor, TBS[St] + 64 * c,
                                        [[A2.pstride, 128], [0, 8], [32, 2], [1, 32]]),
                            op=ALU.mult), waits=[("ve", t1_)])
                        t3_ = p.v(lambda e, sl2=sl2: e.tensor_add(
                            out=ab_t[:, sl2:sl2 + 512],
                            in0=ab_t[:, T1:T1 + 512], in1=ab_t[:, T2:T2 + 512]),
                            waits=[("ve", t2_)])
                        DSTF = QFM if kind == "q" else KFM
                        tr_toks[c] = p.dmat(
                            bass.AP(ab_t[:, 0:1].tensor, DSTF + 512 * c,
                                    [[A2.pstride, 128], [128, 4], [1, 128]]),
                            bass.AP(ab_t[:, 0:1].tensor, sl2,
                                    [[A2.pstride, 128], [128, 4], [1, 128]]),
                            waits=[("ve", t3_)])
                        if c == NCH - 1:
                            prev_kind_last = tr_toks[c]
                A2.pop()
                tr_all = p.now("q_sync")

                A2.push()
                OFM = A2.alloc(NCH * 512)
                PBF = A2.alloc(2 * 4 * 640)
                ABF = A2.alloc(8 * 512)
                DQB = A2.alloc(2 * 4 * 128)
                iters = [(h, w) for h in range(HEADS_F) for w in range(4)]
                NI = len(iters)
                p_toks = {}; dq_toks = {}; s_toks = {}; tr_tok = {}
                etok = {}; tok_trlast = {}; av_last = {}
                sc_attn0 = p.now("sc"); ve_attn0 = p.now("ve")

                def pb(j):
                    return PBF + 2560 * (j % 2)

                def db(j):
                    return DQB + 512 * (j % 2)

                def nmo(j):
                    return NM_ + 4 * (j % 2)

                def tr_qcs(w, kc):
                    if w > 0:
                        return [q_ for q_ in range(4) if 0 <= kc - q_ <= 4]
                    return [q_ for q_ in range(4) if q_ >= kc]

                def emit_S(j, qc):
                    (h, w) = iters[j]
                    hp, par = h // 2, h % 2
                    cq = 4 * w + qc
                    if w == 0:
                        width = 128 * (qc + 1)
                        kc0g = 0
                    else:
                        width = 640
                        kc0g = 4 * (w - 1) + qc
                    nb = width // 128
                    qb = ab_t[64 * par:64 * par + 64,
                              QFM + 512 * cq + 128 * hp:QFM + 512 * cq + 128 * (hp + 1)]
                    rhs1 = bass.AP(ab_t[:, 0:1].tensor,
                                   KFM + 512 * kc0g + 128 * hp + A2.pstride * 64 * par,
                                   [[A2.pstride, 64], [512, min(nb, 4)], [1, 128]])
                    if j == 0 and qc == 0:
                        w0 = [("q_sync", tr_all), ("sc", sc_attn0), ("ve", ve_attn0)]
                    else:
                        w0 = [("sc", p_toks[(j, qc - 1)] if qc > 0 else p_toks[(j - 1, 3)])]
                    m1 = p.t(lambda e, qb=qb, rhs1=rhs1, n=min(nb, 4): e.matmul(
                        out=bank(2, 0, 128, 0, 128 * n), lhsT=qb, rhs=rhs1,
                        start=True, stop=True), waits=w0)
                    if nb > 4:
                        rhs2 = bass.AP(ab_t[:, 0:1].tensor,
                                       KFM + 512 * (kc0g + 4) + 128 * hp + A2.pstride * 64 * par,
                                       [[A2.pstride, 64], [512, 1], [1, 128]])
                        m1 = p.t(lambda e, qb=qb, rhs2=rhs2: e.matmul(
                            out=bank(3, 0, 128, 0, 128), lhsT=qb, rhs=rhs2,
                            start=True, stop=True))
                    s_toks[(j, qc)] = m1

                def emit_soft(j, qc):
                    (h, w) = iters[j]
                    width = 640 if w > 0 else 128 * (qc + 1)
                    if w == 0:
                        mk = p.v(lambda e, qc=qc: e.tensor_add(
                            out=bank(2, 0, 128, 128 * qc, 128),
                            in0=bank(2, 0, 128, 128 * qc, 128), in1=triR[:]),
                            waits=[("te", s_toks[(j, qc)])])
                    else:
                        p.v(lambda e: e.tensor_add(
                            out=bank(2, 0, 128, 0, 128),
                            in0=bank(2, 0, 128, 0, 128), in1=triL[:]),
                            waits=[("te", s_toks[(j, qc)])])
                        mk = p.v(lambda e: e.tensor_add(
                            out=bank(3, 0, 128, 0, 128),
                            in0=bank(3, 0, 128, 0, 128), in1=triR[:]))
                    no = nmo(j)
                    mx = p.v(lambda e, width=width, no=no: e.reduce_max(
                        out=af_t[:, no:no + 1], in_=PS[:, 1024:1024 + width],
                        axis=AX.X, negate=True), waits=[("ve", mk)])
                    ex = p.s(lambda e, j=j, qc=qc, width=width, no=no: e.activation(
                        out=ab_t[:, pb(j) + 640 * qc:pb(j) + 640 * qc + width],
                        in_=PS[:, 1024:1024 + width], func=AF.Exp,
                        bias=af_t[:, no:no + 1],
                        accum_out=af_t[:, no + 1:no + 2]),
                        waits=[("ve", mx), ("te", tok_trlast.get(j - 2, 0))])
                    rc = p.v(lambda e, no=no: e.reciprocal(
                        out=af_t[:, no + 2:no + 3], in_=af_t[:, no + 1:no + 2]),
                        waits=[("sc", ex)])
                    dqt = p.v(lambda e, j=j, qc=qc, no=no: e.tensor_scalar_mul(
                        out=ab_t[:, db(j) + 128 * qc:db(j) + 128 * (qc + 1)],
                        in0=id128[:], scalar1=af_t[:, no + 2:no + 3]),
                        waits=[("ve", rc), ("te", tok_trlast.get(j - 2, 0))])
                    p_toks[(j, qc)] = ex
                    dq_toks[(j, qc)] = dqt

                def emit_tr(i, kc):
                    (h, w) = iters[i]
                    qcs = tr_qcs(w, kc)
                    qmax = max(qcs)
                    wl = [("sc", p_toks[(i, qmax)]), ("ve", dq_toks[(i, qmax)])]
                    if kc >= 2:
                        wl.append(etok[(i, kc - 2)])
                    elif i > 0:
                        pk = etok.get((i - 1, 6 + kc)) or etok.get((i - 1, 2 + kc))
                        if pk is not None:
                            wl.append(pk)
                    tt = None
                    for (qi, qc) in enumerate(qcs):
                        sb = (kc - qc) if w > 0 else kc
                        tt = p.t(lambda e, i=i, qc=qc, sb=sb, kc=kc: e.matmul(
                            out=bank(kc % 2, 0, 128, 128 * qc, 128),
                            lhsT=ab_t[:, pb(i) + 640 * qc + 128 * sb:pb(i) + 640 * qc + 128 * (sb + 1)],
                            rhs=ab_t[:, db(i) + 128 * qc:db(i) + 128 * (qc + 1)],
                            start=True, stop=True), waits=wl if qi == 0 else ())
                    tr_tok[(i, kc)] = tt

                def emit_evac(i, kc):
                    (h, w) = iters[i]
                    qcs = tr_qcs(w, kc)
                    span0, span1 = 128 * min(qcs), 128 * (max(qcs) + 1)
                    if kc % 2 == 0:
                        ev = p.s(lambda e, kc=kc, span0=span0, span1=span1: e.copy(
                            out=ab_t[:, ABF + 512 * kc + span0:ABF + 512 * kc + span1],
                            in_=bank(kc % 2, 0, 128, span0, span1 - span0)),
                            waits=[("te", tr_tok[(i, kc)])])
                        etok[(i, kc)] = ("sc", ev)
                    else:
                        ev = p.v(lambda e, kc=kc, span0=span0, span1=span1: e.tensor_copy(
                            out=ab_t[:, ABF + 512 * kc + span0:ABF + 512 * kc + span1],
                            in_=bank(kc % 2, 0, 128, span0, span1 - span0)),
                            waits=[("te", tr_tok[(i, kc)])])
                        etok[(i, kc)] = ("ve", ev)

                def emit_AV(i, qc):
                    (h, w) = iters[i]
                    if w > 0:
                        kcs = list(range(qc, qc + 5))
                    else:
                        kcs = list(range(0, qc + 1))
                    es = max([etok[(i, kc)][1] for kc in kcs if kc % 2 == 0], default=0)
                    ev = max([etok[(i, kc)][1] for kc in kcs if kc % 2 == 1], default=0)
                    last = None
                    for (ki, kc) in enumerate(kcs):
                        vglob = (4 * (w - 1) + kc) if w > 0 else kc
                        last = p.t(lambda e, qc=qc, kc=kc, vglob=vglob, h=h,
                                   st_=(ki == 0), sp_=(ki == len(kcs) - 1): e.matmul(
                            out=bank(4 + qc, 0, 64, 0, 128),
                            lhsT=ab_t[:, VTM + 512 * vglob + 64 * h:VTM + 512 * vglob + 64 * (h + 1)],
                            rhs=ab_t[:, ABF + 512 * kc + 128 * qc:ABF + 512 * kc + 128 * (qc + 1)],
                            start=st_, stop=sp_),
                            waits=[("sc", es), ("ve", ev)] if ki == 0 else ())
                    return last

                for qc in range(4):
                    emit_S(0, qc)
                    emit_soft(0, qc)
                for i in range(NI):
                    (h, w) = iters[i]
                    hp, par = h // 2, h % 2
                    nkc = 8 if w > 0 else 4
                    have_next = i + 1 < NI
                    for blk in range(4):
                        k0 = 2 * blk
                        if k0 < nkc:
                            emit_tr(i, k0)
                            emit_tr(i, k0 + 1)
                            emit_evac(i, k0)
                            emit_evac(i, k0 + 1)
                        if have_next:
                            if blk >= 1:
                                emit_soft(i + 1, blk - 1)
                            if blk < 4:
                                emit_S(i + 1, blk)
                    if have_next:
                        emit_soft(i + 1, 3)
                    tok_trlast[i] = tr_tok[(i, nkc - 1)]
                    avl = None
                    for qc in range(4):
                        avl = emit_AV(i, qc)
                    av_last[i] = avl
                    oev = p.s(lambda e, h=h, w=w, hp=hp, par=par: e.copy(
                        out=bass.AP(ab_t[:, 0:1].tensor,
                                    OFM + 512 * 4 * w + 128 * hp + A2.pstride * 64 * par,
                                    [[A2.pstride, 64], [512, 4], [1, 128]]),
                        in_=bass.AP(PS[:, 0:1].tensor, 512 * 4,
                                    [[4096, 64], [512, 4], [1, 128]])),
                        waits=[("te", av_last[i])])

                WSL2 = A2.alloc(4 * D)
                wld = p.dma("gpsimd", ab_t[:, WSL2:WSL2 + 4 * D].rearrange("p (f d) -> p f d", f=4),
                            I[f"f_wo{l}"].rearrange("(f p) d -> p f d", p=128),
                            waits=[("te", p.now("te"))])
                radd = {}
                for c in range(NCH):
                    mmt = None
                    b0 = 2 * (c % 2)
                    for oseg in range(2):
                        for f in range(4):
                            mmt = p.t(lambda e, c=c, oseg=oseg, f=f, b0=b0: e.matmul(
                                out=bank(b0 + oseg, 0, 128, 0, 512),
                                lhsT=ab_t[:, OFM + 512 * c + 128 * f:OFM + 512 * c + 128 * (f + 1)],
                                rhs=ab_t[:, WSL2 + D * f + 512 * oseg:WSL2 + D * f + 512 * (oseg + 1)],
                                start=(f == 0), stop=(f == 3)),
                                waits=([("q_gpsimd", wld), ("ve", p.now("ve")),
                                        ("sc", p.now("sc"))]
                                       if (c == 0 and oseg == 0 and f == 0) else
                                       ([("ve", radd.get(c - 2, 0))]
                                        if (oseg == 0 and f == 0) else ())))
                    radd[c] = p.v(lambda e, c=c, b0=b0: e.tensor_add(
                        out=xf(c), in0=xf(c), in1=PS[:, 512 * b0:512 * b0 + 1024]),
                        waits=[("te", mmt)])
                A2.pop()
                A2.pop()

                A2.push()
                HFM2 = A2.alloc(NCH * D)
                LNGB = A2.alloc(D)
                LNBB = A2.alloc(D)
                A2.push()
                XBA2 = A2.alloc(NCH * 512)
                HSH2 = A2.alloc(NCH * 512)
                XNS = A2.alloc(4 * 512)
                l1_ = p.dma("gpsimd", ab_t[:, LNGB:LNGB + D],
                            bass.AP(I[f"f_lng_row{l}"].tensor, 0, [[0, 128], [1, D]]),
                            waits=[("ve", radd[3])])
                l2_ = p.dma("gpsimd", ab_t[:, LNBB:LNBB + D],
                            bass.AP(I[f"f_lnb_row{l}"].tensor, 0, [[0, 128], [1, D]]))
                sh_ct2, sh_d22, sh_mz2 = build_shift_full(
                    XBA2, XF, HSH2, 0, 8, extra0=[("ve", radd[11])])
                sh_ct2b, sh_d22b, _ = build_shift_full(
                    XBA2, XF, HSH2, 8, NCH, extra0=[("ve", radd[15])])
                lnt2 = None
                xn_done = None
                xn_th = {}
                for chalf in (0, 1):
                    for c in range(8 * chalf, 8 * chalf + 8):
                        p.v(lambda e, c=c: e.bn_stats(out=bn6[:, 0:6],
                                                      in_=ab_t[:, XBA2 + 512 * c:XBA2 + 512 * (c + 1)]),
                            waits=([("ve", radd[11]), ("q_gpsimd", sh_ct2)] if c == 0
                                   else ([("ve", lnt2), ("q_gpsimd", sh_ct2b)] if c == 8
                                         else [("ve", lnt2)])))
                        t2b = p.v(lambda e, c=c: e.bn_stats(
                            out=bn6[:, 6:12], in_=ab_t[:, HSH2 + 512 * c:HSH2 + 512 * (c + 1)]),
                            waits=([("q_gpsimd", sh_d22), ("ve", sh_mz2)] if c == 0
                                   else ([("q_gpsimd", sh_d22b)] if c == 8 else ())))
                        lnt2 = p.v(lambda e, c=c: e.bn_aggr(out=af_t[:, MV + 2 * c:MV + 2 * c + 2],
                                                            in_=bn6), waits=[("ve", t2b)])
                    rm2 = stats_rowmath(lnt2, False, nmu_off=NMU2, rsg_off=RSG2,
                                        c0=8 * chalf, c1=8 * chalf + 8)
                    for qr4 in (2 * chalf, 2 * chalf + 1):
                        for (half, SRCB, dsto) in ((0, XBA2, 0), (1, HSH2, HHALF)):
                            last = None
                            for j in range(4):
                                c = 4 * qr4 + j
                                t_ts = p.v(lambda e, c=c, j=j, SRCB=SRCB: e.tensor_scalar(
                                    out=ab_t[:, XNS + 512 * j:XNS + 512 * (j + 1)],
                                    in0=ab_t[:, SRCB + 512 * c:SRCB + 512 * (c + 1)],
                                    scalar1=af_t[:, NMU2 + c:NMU2 + c + 1],
                                    scalar2=af_t[:, RSG2 + c:RSG2 + c + 1],
                                    op0=ALU.add, op1=ALU.mult),
                                    waits=([("q_sync", xn_done), ("q_gpsimd", l2_),
                                            ("ve", rm2)] if j == 0 else []))
                                t_tm = p.v(lambda e, j=j, half=half: e.tensor_mul(
                                    out=ab_t[:, XNS + 512 * j:XNS + 512 * (j + 1)],
                                    in0=ab_t[:, XNS + 512 * j:XNS + 512 * (j + 1)],
                                    in1=ab_t[:, LNGB + 512 * half:LNGB + 512 * (half + 1)]),
                                    waits=[("ve", t_ts)])
                                last = p.v(lambda e, j=j, half=half: e.tensor_add(
                                    out=ab_t[:, XNS + 512 * j:XNS + 512 * (j + 1)],
                                    in0=ab_t[:, XNS + 512 * j:XNS + 512 * (j + 1)],
                                    in1=ab_t[:, LNBB + 512 * half:LNBB + 512 * (half + 1)]),
                                    waits=[("ve", t_tm)])
                            xn_done = p.dmat(
                                bass.AP(ab_t[:, 0:1].tensor, HFM2 + dsto + 2048 * qr4,
                                        [[A2.pstride, 128], [128, 16], [1, 128]]),
                                bass.AP(ab_t[:, 0:1].tensor, XNS,
                                        [[A2.pstride, 128], [128, 16], [1, 128]]),
                                waits=[("ve", last)])
                            if (qr4, half) in ((1, 1), (3, 1)):
                                xn_th[qr4 // 2] = xn_done
                A2.pop()

                A2.push()
                HB = A2.alloc(22 * D)
                HBL = A2.alloc(D)
                GSC = A2.alloc(4 * 512)
                WSL3 = A2.alloc(2 * NFC * 256)
                b1a = af_t[:, B1A:B1A + 22]
                b1g = af_t[:, B1G:B1G + 22]
                lb1 = p.dma("scalar", b1a, I[f"f_b1a{l}"], waits=[("sc", p.now("sc")),
                                                                  ("ve", p.now("ve"))])
                lb2 = p.dma("scalar", b1g, I[f"f_b1g{l}"])
                mzo = p.v(lambda e: e.memset(ab_t[32:64, HBL:HBL + D], 1.0))
                for th in range(2):
                    wga = {}
                    hv_toks = {}
                    glu_hist = {}
                    for j in range(22):
                        rows = 128 if j < 21 else 42
                        mz_w = None
                        if j == 21:
                            mz_w = p.v(lambda e, j=j: e.memset(
                                ab_t[:, WSL3 + 4096 * (j % 2):WSL3 + 4096 * (j % 2) + 4096], 0.0),
                                waits=[("te", wga.get(j - 2, p.now("te")))])
                        wa = p.dma("gpsimd",
                                   bass.AP(ab_t[:, 0:1].tensor, WSL3 + 4096 * (j % 2),
                                           [[A2.pstride, 128], [256, NFC], [1, rows]]),
                                   bass.AP(I[f"f_w1{l}"].tensor, 128 * j,
                                           [[2 * HID, 128], [128 * 2 * HID, NFC], [1, rows]]),
                                   waits=[("te", wga.get(j - 2, p.now("te"))), ("ve", mz_w)])
                        wg = p.dma("gpsimd",
                                   bass.AP(ab_t[:, 0:1].tensor, WSL3 + 4096 * (j % 2) + 128,
                                           [[A2.pstride, 128], [256, NFC], [1, rows]]),
                                   bass.AP(I[f"f_w1{l}"].tensor, HID + 128 * j,
                                           [[2 * HID, 128], [128 * 2 * HID, NFC], [1, rows]]))
                        mm_l = None
                        for seg in range(2):
                            for (which, b0) in (("a", 0), ("g", 2)):
                                for f in range(NFC):
                                    rhs_ap = bass.AP(
                                        ab_t[:, 0:1].tensor,
                                        HFM2 + (HHALF if f >= 4 else 0)
                                        + 512 * (8 * th + 4 * seg) + 128 * (f % 4),
                                        [[A2.pstride, 128], [512, 4], [1, 128]])
                                    wofs = WSL3 + 4096 * (j % 2) + 256 * f + (0 if which == "a" else 128)
                                    mm_l = p.t(lambda e, rhs_ap=rhs_ap, wofs=wofs,
                                               b0=b0, seg=seg, f=f: e.matmul(
                                        out=bank(b0 + seg, 0, 128, 0, 512),
                                        lhsT=ab_t[:, wofs:wofs + 128],
                                        rhs=rhs_ap, start=(f == 0), stop=(f == NFC - 1)),
                                        waits=([("q_gpsimd", wg), ("q_sync", xn_th[th]),
                                                ("sc", glu_hist.get((j - 1, seg), 0)),
                                                ("ve", hv_toks.get((j - 1, seg), 0))]
                                               if (f == 0 and which == "a") else ()))
                        wga[j] = mm_l
                        for seg in range(2):
                            gslot = GSC + 512 * (2 * (j % 2) + seg)
                            gl = p.s(lambda e, j=j, seg=seg, rows=rows, gslot=gslot: e.activation(
                                out=ab_t[0:rows, gslot:gslot + 512], in_=bank(2 + seg, 0, rows, 0, 512),
                                func=AF.Gelu, bias=af_t[0:rows, B1G + j:B1G + j + 1]),
                                waits=[("te", wga[j]), ("q_scalar", lb2),
                                       ("ve", hv_toks.get((j - 2, seg), 0))])
                            dst = (HB + D * j) if j < 21 else HBL
                            hv_toks[(j, seg)] = p.v(
                                lambda e, seg=seg, rows=rows, dst=dst, j=j, gslot=gslot: e.scalar_tensor_tensor(
                                    out=ab_t[0:rows, dst + 512 * seg:dst + 512 * (seg + 1)],
                                    in0=bank(0 + seg, 0, rows, 0, 512),
                                    scalar=af_t[0:rows, B1A + j:B1A + j + 1],
                                    in1=ab_t[0:rows, gslot:gslot + 512],
                                    op0=ALU.add, op1=ALU.mult),
                                waits=[("sc", gl), ("q_scalar", lb1), ("ve", mzo)] +
                                      ([("q_sync", xn_done)] if j in (16, 17) else []))
                            glu_hist[(j, seg)] = gl
                    hb_done = p.now("ve")
                    for grp in range(2):
                        for oseg in range(2):
                            mmw = {}
                            for j in range(22):
                                rows = 128 if j < 21 else 43
                                w2l = p.dma("gpsimd",
                                            ab_t[0:rows, WSL3 + 4096 * (j % 2):WSL3 + 4096 * (j % 2) + 512],
                                            I[f"f_w2a{l}"][128 * j:128 * j + rows,
                                                           512 * oseg:512 * (oseg + 1)],
                                            waits=[("te", mmw.get(j - 2, p.now("te"))),
                                                   ("ve", hv_toks.get((j, 1), hb_done))])
                                for tt in range(4):
                                    if j < 21:
                                        lhsT = ab_t[:, HB + D * j + 512 * grp + 128 * tt:
                                                    HB + D * j + 512 * grp + 128 * (tt + 1)]
                                    else:
                                        lhsT = ab_t[0:43, HBL + 512 * grp + 128 * tt:
                                                    HBL + 512 * grp + 128 * (tt + 1)]
                                    mmw[j] = p.t(lambda e, j=j, tt=tt, rows=rows, lhsT=lhsT: e.matmul(
                                        out=bank(4 + tt, 0, 128, 0, 512),
                                        lhsT=lhsT,
                                        rhs=ab_t[0:rows, WSL3 + 4096 * (j % 2):WSL3 + 4096 * (j % 2) + 512],
                                        start=(j == 0), stop=(j == 21)),
                                        waits=(([("q_gpsimd", w2l)] +
                                                ([("ve", p.now("ve"))] if j == 0 else []))
                                               if tt == 0 else ()))
                            for tt in range(4):
                                cg = 8 * th + 4 * grp + tt
                                fadd_last = p.v(lambda e, cg=cg, tt=tt, oseg=oseg: e.tensor_add(
                                    out=af_t[:, XF + D * cg + 512 * oseg:XF + D * cg + 512 * (oseg + 1)],
                                    in0=af_t[:, XF + D * cg + 512 * oseg:XF + D * cg + 512 * (oseg + 1)],
                                    in1=bank(4 + tt, 0, 128, 0, 512)),
                                    waits=[("te", mmw[21])])
                A2.pop()
                A2.pop()
            for qr in range(4):
                ld = p.dma("scalar",
                           bass.AP(af_t[:, 0:1].tensor, WTMP0,
                                   [[A4.pstride, 128], [D, 4], [1, D]]),
                           x_src[512 * qr:512 * (qr + 1), :].rearrange("(j p) d -> p j d", p=128))
                ads = []
                for j in range(4):
                    c = 4 * qr + j
                    ads.append(p.v(lambda e, c=c, j=j: e.tensor_add(
                        out=xf(c), in0=af_t[:, WTMP0 + D * j:WTMP0 + D * (j + 1)], in1=xf(c)),
                        waits=([("q_scalar", ld)] +
                               ([("ve", fadd_last)] if qr == 0 else []))
                        if j == 0 else ()))
                    if stats_for_next:
                        tm_stats(xf(c), af_t[:, MV + 2 * c:MV + 2 * c + 2],
                                 [("ve", ads[-1])])
                p.dma("scalar",
                      x_dst[512 * qr:512 * (qr + 1), :].rearrange("(j p) d -> p j d", p=128),
                      bass.AP(af_t[:, 0:1].tensor, XF + 4 * D * qr,
                              [[A4.pstride, 128], [D, 4], [1, D]]),
                      waits=[("ve", ads[-1])])
            p.barrier()
            A4.pop(); A2.pop()

        def emit_attn_b(x_src, x_dst):
            A4.push(); A2.push()
            RNC = A4.alloc(40)
            RCPT = A4.alloc(256)
            YSC = A4.alloc(D)
            XQ = A4.alloc(16 * D)
            ldq = {}
            for qr in range(4):
                ldq[qr] = p.dma("scalar",
                                bass.AP(af_t[:, 0:1].tensor, XQ + 4 * D * qr,
                                        [[A4.pstride, 128], [D, 4], [1, D]]),
                                x_src[512 * qr:512 * (qr + 1), :].rearrange(
                                    "(j p) d -> p j d", p=128))
            rm_b = stats_rowmath(0, False)
            A2.push()
            XNTM = A2.alloc(2 * D)
            XNFM = A2.alloc(NCH * D)
            WB = A2.alloc(NFC * D)
            BRW = A2.alloc(D)
            EVS = A2.alloc(2 * D)
            tr_xn = {}
            for c in range(NCH):
                sl = XNTM + D * (c % 2)
                tv = p.v(lambda e, c=c, sl=sl: e.tensor_scalar(
                    out=ab_t[:, sl:sl + D], in0=af_t[:, XQ + D * c:XQ + D * (c + 1)],
                    scalar1=af_t[:, NMU + c:NMU + c + 1], scalar2=af_t[:, RSG + c:RSG + c + 1],
                    op0=ALU.add, op1=ALU.mult),
                    waits=[("q_sync", tr_xn.get(c - 2, 0)),
                           ("q_scalar", ldq[c // 4])] +
                          ([("ve", rm_b)] if c == 0 else []))
                tr_xn[c] = p.dmat(bass.AP(ab_t[:, 0:1].tensor, XNFM + D * c,
                                          [[A2.pstride, 128], [128, 8], [1, 128]]),
                                  bass.AP(ab_t[:, 0:1].tensor, sl,
                                          [[A2.pstride, 128], [128, 8], [1, 128]]),
                                  waits=[("ve", tv)])
            for (wi, (wn, bn_, relu)) in enumerate((("b_WqT", "b_bq_row", True),
                                                    ("b_WkT", "b_bk_row", True),
                                                    ("b_WvT", "b_bv_row", False))):
                wld = p.dma("gpsimd", ab_t[:, WB:WB + NFC * D].rearrange("p (f d) -> p f d", f=NFC),
                            I[wn].rearrange("(f p) d -> p f d", p=128),
                            waits=[("te", p.now("te"))])
                bld = p.dma("gpsimd", ab_t[0:1, BRW:BRW + D], I[bn_])
                evt = {}
                svt = {}
                for c in range(NCH):
                    bnk = 2 * (c % 2)
                    mmt = None
                    for oseg in range(2):
                        for f in range(NFC):
                            mmt = p.t(lambda e, c=c, f=f, oseg=oseg, bnk=bnk: e.matmul(
                                out=bank(bnk + oseg, 0, 128, 0, 512),
                                lhsT=ab_t[:, XNFM + D * c + 128 * f:XNFM + D * c + 128 * (f + 1)],
                                rhs=ab_t[:, WB + D * f + 512 * oseg:WB + D * f + 512 * (oseg + 1)],
                                start=(f == 0), stop=False),
                                waits=([("q_gpsimd", bld), ("q_sync", tr_xn[NCH - 1]),
                                        ("ve", p.now("ve")), ("sc", p.now("sc"))]
                                       if (c == 0 and oseg == 0 and f == 0) else
                                       ([("ve", evt.get(c - 2, 0))]
                                        if (oseg == 0 and f == 0) else ())))
                            if f == NFC - 1:
                                mmt = p.t(lambda e, c=c, oseg=oseg, bnk=bnk: e.matmul(
                                    out=bank(bnk + oseg, 0, 128, 0, 512),
                                    lhsT=ones_row[0:1, 0:128],
                                    rhs=ab_t[0:1, BRW + 512 * oseg:BRW + 512 * (oseg + 1)],
                                    start=False, stop=True))
                    sl = EVS + D * (c % 2)
                    if relu:
                        evt[c] = p.v(lambda e, sl=sl, bnk=bnk: e.tensor_scalar_max(
                            out=ab_t[:, sl:sl + D], in0=PS[:, 512 * bnk:512 * bnk + 1024],
                            scalar1=0.0), waits=[("te", mmt), ("q_gpsimd", svt.get(c - 2, 0))])
                    else:
                        evt[c] = p.v(lambda e, sl=sl, bnk=bnk: e.tensor_copy(
                            out=ab_t[:, sl:sl + D], in_=PS[:, 512 * bnk:512 * bnk + 1024]),
                            waits=[("te", mmt), ("q_gpsimd", svt.get(c - 2, 0))])
                    svt[c] = p.dma("gpsimd", bscr[T * wi + 128 * c:T * wi + 128 * (c + 1), :],
                                   ab_t[:, sl:sl + D], waits=[("ve", evt[c])])
            p.barrier()
            A2.pop()
            A2.push()
            KST = A2.alloc(4 * 1024)
            VST = A2.alloc(4 * 544)
            KVB = A2.alloc(16 * 65)
            KVB2 = A2.alloc(16 * 65)
            KSR = A2.alloc(D)
            KBC = A2.alloc(D)
            mz1 = p.v(lambda e: e.memset(
                bass.AP(ab_t[:, 0:1].tensor, VST + 64, [[A2.pstride, 128], [544, 4], [65, 8], [1, 1]]), 1.0))
            mzk1 = p.v(lambda e: e.memset(ab_t[:, KVB:KVB + 16 * 65], 0.0))
            mzk2 = p.v(lambda e: e.memset(ab_t[:, KVB2:KVB2 + 16 * 65], 0.0))
            kv_ev = {}
            mms = {}
            te0b = p.now("te")
            for ps_ in range(2):
                for c in range(NCH):
                    ci = NCH * ps_ + c
                    sk = KST + 1024 * (ci % 4)
                    sv = VST + 544 * (ci % 4)
                    lk = p.dma("scalar", ab_t[:, sk:sk + 512],
                               bscr[T + 128 * c:T + 128 * (c + 1), 512 * ps_:512 * (ps_ + 1)],
                               waits=[("te", mms.get(ci - 4, te0b))])
                    lv = p.dma("sync",
                               bass.AP(ab_t[:, 0:1].tensor, sv, [[A2.pstride, 128], [65, 8], [1, 64]]),
                               bscr[2 * T + 128 * c:2 * T + 128 * (c + 1),
                                    512 * ps_:512 * (ps_ + 1)].rearrange("p (h d) -> p h d", h=8),
                               waits=[("ve", mz1), ("te", mms.get(ci - 4, te0b))])
                    for hh in range(8):
                        mms[ci] = p.t(lambda e, c=c, hh=hh, sk=sk, sv=sv: e.matmul(
                            out=bank(hh, 0, 64, 0, 65),
                            lhsT=ab_t[:, sk + 64 * hh:sk + 64 * (hh + 1)],
                            rhs=ab_t[:, sv + 65 * hh:sv + 65 * (hh + 1)],
                            start=(c == 0), stop=(c == NCH - 1)),
                            waits=([("q_scalar", lk), ("q_sync", lv)] if hh == 0 else ()))
                for hh in range(8):
                    h = 8 * ps_ + hh
                    kv_ev[h] = p.s(lambda e, h=h, hh=hh: e.copy(
                        out=ab_t[0:64, KVB + 65 * h:KVB + 65 * (h + 1)],
                        in_=bank(hh, 0, 64, 0, 65)),
                        waits=[("te", mms[NCH * ps_ + NCH - 1]), ("ve", mzk1)])
            QS2 = A2.alloc(2 * D)
            QF2 = A2.alloc(2 * D)
            TT2 = A2.alloc(D)
            GBC = A2.alloc(D)
            BBC = A2.alloc(D)
            OSC = A2.alloc(D)
            lg1 = p.dma("gpsimd", ab_t[:, GBC:GBC + D], I["b_g_bc"])
            lg2 = p.dma("gpsimd", ab_t[:, BBC:BBC + D], I["b_b_bc"])
            lqs = {}
            trq = {}
            for c in (0, 1):
                sq = QS2 + D * (c % 2)
                qf = QF2 + D * (c % 2)
                lqs[c] = p.dma("scalar", ab_t[:, sq:sq + D], bscr[128 * c:128 * (c + 1), :],
                               waits=[("te", te0b)])
                trq[c] = p.dmat(bass.AP(ab_t[:, 0:1].tensor, qf,
                                        [[A2.pstride, 128], [128, 8], [1, 128]]),
                                bass.AP(ab_t[:, 0:1].tensor, sq,
                                        [[A2.pstride, 128], [128, 8], [1, 128]]),
                                waits=[("q_scalar", lqs[c])])
            ktr = None
            for h in range(16):
                ktr = p.t(lambda e, h=h: e.matmul(
                    out=PS[0:1, 64 * h:64 * (h + 1)],
                    lhsT=ab_t[0:64, KVB + 65 * h + 64:KVB + 65 * h + 65],
                    rhs=id128[0:64, 0:64], start=True, stop=True),
                    waits=[("sc", p.now("sc"))] if h == 0 else ())
            kse = p.v(lambda e: e.tensor_copy(out=ab_t[0:1, KSR:KSR + D], in_=PS[0:1, 0:1024]),
                      waits=[("te", ktr)])
            kcp = p.dma("sync", bass.AP(ab_t[:, 0:1].tensor, KVB2 + A2.pstride * 64,
                                        [[A2.pstride, 64], [1, 16 * 65]]),
                        ab_t[0:64, KVB:KVB + 16 * 65],
                        waits=[("sc", kv_ev[15]), ("ve", mzk2)])
            bc1 = p.t(lambda e: e.matmul(out=bank(4, 0, 128, 0, 512),
                                         lhsT=ones_row[0:1, 0:128], rhs=ab_t[0:1, KSR:KSR + 512],
                                         start=True, stop=True), waits=[("ve", kse)])
            bc2 = p.t(lambda e: e.matmul(out=bank(5, 0, 128, 0, 512),
                                         lhsT=ones_row[0:1, 0:128],
                                         rhs=ab_t[0:1, KSR + 512:KSR + 1024],
                                         start=True, stop=True))
            kbe = p.v(lambda e: e.tensor_copy(out=ab_t[:, KBC:KBC + D], in_=PS[:, 2048:3072]),
                      waits=[("te", bc2)])
            p.barrier()
            st_prev = {}
            stb = {}
            for c in range(NCH):
                sq = QS2 + D * (c % 2)
                qf = QF2 + D * (c % 2)
                if c < 2:
                    lq = lqs[c]
                else:
                    lq = p.dma("scalar", ab_t[:, sq:sq + D], bscr[128 * c:128 * (c + 1), :],
                               waits=[("ve", st_prev.get(c - 2, 0)), ("q_sync", trq.get(c - 2, 0))])
                    trq[c] = p.dmat(bass.AP(ab_t[:, 0:1].tensor, qf,
                                            [[A2.pstride, 128], [128, 8], [1, 128]]),
                                    bass.AP(ab_t[:, 0:1].tensor, sq,
                                            [[A2.pstride, 128], [128, 8], [1, 128]]),
                                    waits=[("q_scalar", lq)])
                d1 = p.v(lambda e, sq=sq: e.tensor_tensor(
                    out=ab_t[:, TT2:TT2 + D], in0=ab_t[:, sq:sq + D], in1=ab_t[:, KBC:KBC + D],
                    op=ALU.mult), waits=[("q_scalar", lq), ("ve", kbe),
                                         ("q_gpsimd", lg2), ("ve", st_prev.get(c - 1, 0))])
                d2 = p.v(lambda e: e.reduce_sum(
                    out=af_t[:, RNC:RNC + 16],
                    in_=ab_t[:, TT2:TT2 + D].rearrange("p (h d) -> p h d", h=16),
                    axis=AX.X), waits=[("ve", d1)])
                d3 = p.v(lambda e: e.tensor_scalar_max(out=af_t[:, RNC:RNC + 16],
                                                       in0=af_t[:, RNC:RNC + 16], scalar1=1e-6),
                         waits=[("ve", d2)])
                d4 = p.v(lambda e, c=c: e.reciprocal(out=af_t[:, RCPT + 16 * c:RCPT + 16 * (c + 1)],
                                                     in_=af_t[:, RNC:RNC + 16]), waits=[("ve", d3)])
                b01 = 4 + 2 * (c % 2)
                mmc = None
                for h in range(16):
                    hp, par = h // 2, h % 2
                    kvsrc = (ab_t[:, KVB + 65 * h:KVB + 65 * h + 64] if par == 0 else
                             ab_t[:, KVB2 + 65 * h:KVB2 + 65 * h + 64])
                    mmc = p.t(lambda e, h=h, hp=hp, par=par, qf=qf, b01=b01, kvsrc=kvsrc: e.matmul(
                        out=bank(b01 + (h >= 8), 0, 128, 64 * (h % 8), 64),
                        lhsT=ab_t[:, qf + 128 * hp:qf + 128 * (hp + 1)],
                        rhs=kvsrc,
                        start=(h % 8 == 0), stop=(h % 8 == 7)),
                        waits=([("q_sync", trq[c]), ("ve", st_prev.get(c - 2, 0)),
                                ("sc", p.now("sc")), ("q_sync", kcp)] if h == 0 else ()))
                o1 = p.v(lambda e, c=c, b01=b01: e.tensor_tensor(
                    out=ab_t[:, OSC:OSC + D].rearrange("p (h d) -> p h d", h=16),
                    in0=PS[:, 512 * b01:512 * b01 + 1024].rearrange("p (h d) -> p h d", h=16),
                    in1=bass.AP(af_t[:, 0:1].tensor, RCPT + 16 * c,
                                [[A4.pstride, 128], [1, 16], [0, 64]]),
                    op=ALU.mult), waits=[("te", mmc), ("ve", d4)])
                x1 = p.v(lambda e, c=c: e.tensor_scalar(
                    out=af_t[:, YSC:YSC + D], in0=af_t[:, XQ + D * c:XQ + D * (c + 1)],
                    scalar1=af_t[:, NMU + c:NMU + c + 1], scalar2=af_t[:, RSG + c:RSG + c + 1],
                    op0=ALU.add, op1=ALU.mult),
                    waits=[("ve", o1)] +
                          ([("q_scalar", stb[c - 1])] if c == 1 else []))
                y1 = p.v(lambda e: e.tensor_add(out=af_t[:, YSC:YSC + D],
                                                in0=af_t[:, YSC:YSC + D],
                                                in1=ab_t[:, OSC:OSC + D]), waits=[("ve", x1)])
                bnt = tm_stats(af_t[:, YSC:YSC + D], af_t[:, MV + 2 * c:MV + 2 * c + 2],
                               [("ve", y1)])
                a1 = p.s(lambda e, c=c: e.activation(out=af_t[:, TSa:TSa + 1],
                                                     in_=af_t[:, MV + 2 * c + 1:MV + 2 * c + 2],
                                                     func=AF.Sqrt, bias=1e-5), waits=[("ve", bnt)])
                a2 = p.v(lambda e: e.reciprocal(out=af_t[:, TSb:TSb + 1], in_=af_t[:, TSa:TSa + 1]),
                         waits=[("sc", a1)])
                a3 = p.v(lambda e, c=c: e.tensor_scalar(
                    out=af_t[:, YSC:YSC + D], in0=af_t[:, YSC:YSC + D],
                    scalar1=af_t[:, MV + 2 * c:MV + 2 * c + 1], scalar2=af_t[:, TSb:TSb + 1],
                    op0=ALU.subtract, op1=ALU.mult), waits=[("ve", a2)])
                a4 = p.v(lambda e: e.tensor_mul(out=af_t[:, YSC:YSC + D],
                                                in0=af_t[:, YSC:YSC + D], in1=ab_t[:, GBC:GBC + D]),
                         waits=[("ve", a3)])
                a5 = p.v(lambda e: e.tensor_add(out=af_t[:, YSC:YSC + D],
                                                in0=af_t[:, YSC:YSC + D], in1=ab_t[:, BBC:BBC + D]),
                         waits=[("ve", a4)])
                a6 = p.v(lambda e, c=c: e.tensor_add(out=af_t[:, YSC:YSC + D],
                                                in0=af_t[:, YSC:YSC + D],
                                                in1=af_t[:, XQ + D * c:XQ + D * (c + 1)]),
                         waits=[("ve", a5)])
                st_prev[c] = a6
                stb[c] = p.dma("scalar", x_dst[128 * c:128 * (c + 1), :], af_t[:, YSC:YSC + D],
                               waits=[("ve", a6)])
            p.barrier()
            A2.pop(); A4.pop(); A2.pop()

        emit_attn_a(I["x"], out)
        if STAGE >= 2:
            emit_ffn(out, out, "a", stats_for_next=(STAGE >= 3))
        if STAGE >= 3:
            emit_attn_b(out, out)
        if STAGE >= 4:
            emit_ffn(out, out, "b")

        for q in ("q_scalar", "q_sync", "q_gpsimd"):
            p.sem(q)
        p.g(lambda e: e.nop(), waits=[("q_scalar", p.now("q_scalar")),
                                      ("q_sync", p.now("q_sync")),
                                      ("q_gpsimd", p.now("q_gpsimd"))])
        with nc.Block() as block:
            p.finalize(block)
    return nc


_NC_CACHE = {}


def _get_nc():
    if "nc" not in _NC_CACHE:
        _NC_CACHE["nc"] = build_program()
    return _NC_CACHE["nc"]


def _cc(a):
    return np.ascontiguousarray(np.asarray(a, np.float32))


def kernel(**inp):
    nc = _get_nc()
    sh = {}
    sh["a_WqT"] = _cc(np.asarray(inp["L0_attn_a_Wq"]).T)
    sh["a_WkT"] = _cc(np.asarray(inp["L0_attn_a_Wk"]).T)
    sh["a_WvT"] = _cc(np.asarray(inp["L0_attn_a_Wv"]).T)
    sh["a_WoT"] = _cc(np.asarray(inp["L0_attn_a_Wo"]).T)
    sh["a_bwT"] = _cc(np.asarray(inp["L0_attn_a_beta_w"]).T)
    sh["a_rin"] = _cc(np.asarray(inp["L0_attn_a_rin_w"]).reshape(8, 128).T)
    sh["a_rk"] = _cc(np.asarray(inp["L0_attn_a_rk_w"]).reshape(8, 128).T)
    sh["a_rv"] = _cc(np.asarray(inp["L0_attn_a_rv_w"]).reshape(8, 128).T)
    sh["a_bo_row"] = _cc(np.asarray(inp["L0_attn_a_bo"])[None, :])
    sh["a_lnw_bc"] = _cc(np.broadcast_to(np.asarray(inp["L0_attn_a_ln_w"]), (128, D)))
    sh["a_lnb_bc"] = _cc(np.broadcast_to(np.asarray(inp["L0_attn_a_ln_b"]), (128, D)))
    sh["a_bb_col"] = _cc(np.asarray(inp["L0_attn_a_beta_b"])[:, None])
    sh["a_bbl_col"] = _cc(np.asarray(inp["L0_attn_a_base_beta_logit"])[:, None])
    sw = np.r_[np.arange(32, 64), np.arange(0, 32)]
    for l in range(2):
        sh[f"f_wqkv{l}"] = _cc(np.asarray(inp["L0_ffn_wqkv"])[l])
        sh[f"f_wo{l}"] = _cc(np.asarray(inp["L0_ffn_wo"])[l])
        sh[f"f_w1{l}"] = _cc(np.asarray(inp["L0_ffn_w1"])[l])
        w2 = np.asarray(inp["L0_ffn_w2"])[l]
        b2 = np.asarray(inp["L0_ffn_b2"])[l]
        sh[f"f_w2a{l}"] = _cc(np.concatenate([w2, b2[None, :]], axis=0))
        b1 = np.asarray(inp["L0_ffn_b1"])[l]
        b1a = np.zeros((22 * 128,), np.float32); b1a[:HID] = b1[:HID]
        b1g = np.zeros((22 * 128,), np.float32); b1g[:HID] = b1[HID:]
        sh[f"f_b1a{l}"] = _cc(b1a.reshape(22, 128).T)
        sh[f"f_b1g{l}"] = _cc(b1g.reshape(22, 128).T)
        sh[f"f_lng_row{l}"] = _cc(np.asarray(inp["L0_ffn_ln_g"])[l][None, :])
        sh[f"f_lnb_row{l}"] = _cc(np.asarray(inp["L0_ffn_ln_b"])[l][None, :])
        qs = np.asarray(inp["L0_ffn_q_scale"])[l]; ks = np.asarray(inp["L0_ffn_k_scale"])[l]
        sh[f"f_qs{l}_bc"] = _cc(np.broadcast_to(qs, (128, 64)))
        sh[f"f_ks{l}_bc"] = _cc(np.broadcast_to(ks, (128, 64)))
        sh[f"f_qs{l}_sw"] = _cc(np.broadcast_to(qs[sw], (128, 64)))
        sh[f"f_ks{l}_sw"] = _cc(np.broadcast_to(ks[sw], (128, 64)))
    sh["b_WqT"] = _cc(np.asarray(inp["L0_attn_b_Wq"]).T)
    sh["b_WkT"] = _cc(np.asarray(inp["L0_attn_b_Wk"]).T)
    sh["b_WvT"] = _cc(np.asarray(inp["L0_attn_b_Wv"]).T)
    sh["b_bq_row"] = _cc(np.asarray(inp["L0_attn_b_bq"])[None, :])
    sh["b_bk_row"] = _cc(np.asarray(inp["L0_attn_b_bk"])[None, :])
    sh["b_bv_row"] = _cc(np.asarray(inp["L0_attn_b_bv"])[None, :])
    sh["b_g_bc"] = _cc(np.broadcast_to(np.asarray(inp["L0_attn_b_gamma"]), (128, D)))
    sh["b_b_bc"] = _cc(np.broadcast_to(np.asarray(inp["L0_attn_b_beta"]), (128, D)))
    for k, v in CONSTS.items():
        sh[k] = v
    x = np.asarray(inp["L0_attn_a_x"])
    in_maps = [dict(sh, x=_cc(x[b])) for b in range(8)]
    res = run_bass_kernel_spmd(nc, in_maps, core_ids=list(range(8)))
    return np.stack([r["out"] for r in res.results], axis=0)


# version 13 (final)
# speedup vs baseline: 1.0184x; 1.0127x over previous
import numpy as np
import ml_dtypes
import concourse.bass as bass
import concourse.mybir as mybir
from concourse.bass_utils import run_bass_kernel_spmd
from contextlib import ExitStack

F32 = mybir.dt.float32
BF16 = mybir.dt.bfloat16
AF = mybir.ActivationFunctionType
ALU = mybir.AluOpType
AX = mybir.AxisListType

T = 2048
D = 1024
NCH = 16
NFC = 8
NEG = -1.0e30
NH_A, DH_A = 16, 64
HEADS_F, DH_F, INNER, WINDOW, HID = 8, 64, 512, 512, 2730
QK_SCALE = 8.0

STAGE = 4


def _bf(x):
    return np.ascontiguousarray(np.asarray(x, np.float32)).astype(ml_dtypes.bfloat16)


def host_constants():
    c = {}
    ii = np.arange(128)
    c["id128_bf"] = _bf(np.eye(128))
    c["triSI_bf"] = _bf((ii[:, None] <= ii[None, :]).astype(np.float32))
    c["triL_f32"] = np.where(ii[None, :] < ii[:, None], np.float32(NEG), np.float32(0)).astype(np.float32)
    c["triR_f32"] = np.where(ii[None, :] > ii[:, None], np.float32(NEG), np.float32(0)).astype(np.float32)
    _s2 = np.stack([np.r_[np.ones(64), np.zeros(64)], np.r_[np.zeros(64), np.ones(64)]])
    c["sel2_bf"] = _bf(_s2)
    c["sel2_f32"] = _s2.astype(np.float32)
    c["ones_row_bf"] = _bf(np.ones((1, 128)))
    c["id16_f32"] = np.eye(16, dtype=np.float32)
    half = np.arange(0, DH_F, 2, dtype=np.float64)
    inv_freq = 1.0 / 10000.0 ** (half / DH_F)
    t = np.arange(T, dtype=np.float64)
    freqs = np.concatenate([t[:, None] * inv_freq[None, :]] * 2, axis=-1)
    base = (half + 0.4 * DH_F) / (1.4 * DH_F)
    power = (t - T // 2) / float(WINDOW // 2)
    scale = np.concatenate([base[None, :] ** power[:, None]] * 2, axis=-1)
    cos, sin = np.cos(freqs), np.sin(freqs)
    sgn = np.concatenate([-np.ones(32), np.ones(32)])[None, :]
    c["qC_bf"] = _bf(cos * scale * QK_SCALE)
    c["qS_bf"] = _bf(sin * scale * sgn * QK_SCALE)
    c["kC_bf"] = _bf(cos / scale)
    c["kS_bf"] = _bf(sin / scale * sgn)
    return c


CONSTS = host_constants()


class Prog:
    ENGS = ("tensor", "vector", "scalar", "sync", "gpsimd")

    def __init__(self, nc, ctx):
        self.nc = nc
        self.ctx = ctx
        self.buf = {e: [] for e in self.ENGS}
        self.semh = {}
        self.cnt = {}
        self.desc = {}

    def sem(self, name):
        if name not in self.semh:
            self.semh[name] = self.ctx.enter_context(self.nc.semaphore(name))
            self.cnt[name] = 0
        return self.semh[name]

    def now(self, name):
        self.sem(name)
        return self.cnt[name]

    def _emit(self, eng, fn, inc, by, waits):
        for w in waits:
            if w is None:
                continue
            (s, v) = w
            if v is None or v == 0:
                continue
            h = self.sem(s)
            self.buf[eng].append(lambda e, h=h, v=v: e.wait_ge(h, v))
        if inc is not None:
            h = self.sem(inc)
            self.cnt[inc] += by
            tgt = self.cnt[inc]
            self.buf[eng].append(lambda e, fn=fn, h=h, by=by: fn(e).then_inc(h, by))
            return tgt
        self.buf[eng].append(fn)
        return None

    def t(self, fn, waits=()):
        return self._emit("tensor", fn, "te", 1, waits)

    def v(self, fn, waits=()):
        return self._emit("vector", fn, "ve", 1, waits)

    def s(self, fn, waits=()):
        return self._emit("scalar", fn, "sc", 1, waits)

    def g(self, fn, waits=()):
        return self._emit("gpsimd", fn, "gp", 1, waits)

    def dma(self, q, out, in_, waits=(), slow=False):
        w = list(waits) + [("q_" + q, self.now("q_" + q))]
        return self._emit(q, lambda e, o=out, i=in_: e.dma_start(
            out=o, in_=i, allow_slow_non_contiguous=slow), "q_" + q, 16, w)

    def dmat(self, out, in_, waits=()):
        w = list(waits) + [("q_sync", self.now("q_sync"))]
        return self._emit("sync", lambda e, o=out, i=in_: e.dma_start_transpose(out=o, in_=i),
                          "q_sync", 16, w)

    def barrier(self, engs=None):
        engs = engs or self.ENGS
        for (e, q) in (("sync", "q_sync"), ("scalar", "q_scalar"), ("gpsimd", "q_gpsimd")):
            if e in engs:
                h = self.sem(q)
                v = self.cnt[q]
                if v:
                    self.buf[e].append(lambda x, h=h, v=v: x.wait_ge(h, v))
        toks = {}
        for e in engs:
            toks[e] = self._emit(e, lambda x: x.drain(), "bar_" + e, 1, ())
        for e in engs:
            for o in engs:
                if o != e:
                    h = self.sem("bar_" + o)
                    self.buf[e].append(lambda x, h=h, v=toks[o]: x.wait_ge(h, v))

    def finalize(self, block):
        for e in self.ENGS:
            lst = self.buf[e]
            getattr(block, e)(lambda eng, lst=lst: [f(eng) for f in lst])


class Arena:
    def __init__(self, tensor, cols):
        self.t = tensor
        self.cols = cols
        self.off = 0
        self.hw = 0
        self.marks = []

    def alloc(self, n, align=16):
        o = (self.off + align - 1) // align * align
        assert o + n <= self.cols, f"arena overflow {o}+{n}>{self.cols}"
        self.off = o + n
        self.hw = max(self.hw, self.off)
        return o

    def push(self):
        self.marks.append(self.off)

    def pop(self):
        self.off = self.marks.pop()

    @property
    def pstride(self):
        return self.t[:, 0:1].ap[0][0]

    def ap(self, off, dims, p=128, base_p=0):
        dd = [[self.pstride, p]] + [list(d) for d in dims]
        a = bass.AP(self.t[:, 0:1].tensor, off, dd)
        if base_p:
            a = bass.AP(a.tensor, off + base_p * 0, dd)
        return a

    def r(self, off, n, p0=0, p1=128):
        return self.t[p0:p1, off:off + n]


def build_program():
    nc = bass.Bass("TRN2", target_bir_lowering=False, debug=False)
    I = {}

    def din(name, shape, dtype=F32):
        I[name] = nc.dram_tensor(name, list(shape), dtype, kind="ExternalInput").ap()

    din("x", (T, D))
    out = nc.dram_tensor("out", [T, D], F32, kind="ExternalOutput").ap()
    wscr = nc.dram_tensor("wscr", [3 * D, D], BF16, kind="Internal").ap()
    bscr = nc.dram_tensor("bscr", [3 * T, D], BF16, kind="Internal").ap()

    din("a_WqT", (D, D)); din("a_WkT", (D, D)); din("a_WvT", (D, D)); din("a_WoT", (D, D))
    din("a_bwT", (D, NH_A))
    din("a_rin", (128, 8)); din("a_rk", (128, 8)); din("a_rv", (128, 8))
    din("a_bo_row", (1, D)); din("a_lnw_bc", (128, D)); din("a_lnb_bc", (128, D))
    din("a_bb_col", (NH_A, 1)); din("a_bbl_col", (NH_A, 1))
    for l in range(2):
        din(f"f_wqkv{l}", (D, 3 * INNER))
        din(f"f_wo{l}", (INNER, D))
        din(f"f_w1{l}", (D, 2 * HID))
        din(f"f_w2a{l}", (HID + 1, D))
        din(f"f_b1a{l}", (128, 22)); din(f"f_b1g{l}", (128, 22))
        din(f"f_lng_row{l}", (1, D)); din(f"f_lnb_row{l}", (1, D))
        din(f"f_qs{l}_bc", (128, 64)); din(f"f_ks{l}_bc", (128, 64))
        din(f"f_qs{l}_sw", (128, 64)); din(f"f_ks{l}_sw", (128, 64))
    din("b_WqT", (D, D)); din("b_WkT", (D, D)); din("b_WvT", (D, D))
    din("b_bq_row", (1, D)); din("b_bk_row", (1, D)); din("b_bv_row", (1, D))
    din("b_g_bc", (128, D)); din("b_b_bc", (128, D))
    for k, v in CONSTS.items():
        din(k, v.shape, BF16 if v.dtype == ml_dtypes.bfloat16 else F32)

    ctx = ExitStack()
    with ctx:
        af_t = ctx.enter_context(nc.sbuf_tensor("af", [128, 21804], F32))
        ab_t = ctx.enter_context(nc.sbuf_tensor("ab", [128, 57480], BF16))
        PS = ctx.enter_context(nc.psum_tensor("PS", [128, 4096], F32))
        A4 = Arena(af_t, 21804)
        A2 = Arena(ab_t, 57480)
        p = Prog(nc, ctx)

        def named(name, shape, dt=F32):
            return ctx.enter_context(nc.sbuf_tensor(name, shape, dt))

        id128 = named("id128", [128, 128], BF16)
        triSI = named("triSI", [128, 128], BF16)
        triL = named("triL", [128, 128], F32)
        triR = named("triR", [128, 128], F32)
        sel2 = named("sel2", [2, 128], BF16)
        sel2f = named("sel2f", [2, 128], F32)
        ones_row = named("ones_rowX", [1, 128], BF16)
        id16 = named("id16", [16, 16], F32)
        ones16c = named("ones16c", [16, 1], F32)
        cvals = named("cvals", [128, 8], F32)

        for (sb, nm) in ((id128, "id128_bf"), (triSI, "triSI_bf"), (triL, "triL_f32"),
                         (triR, "triR_f32"), (sel2, "sel2_bf"), (sel2f, "sel2_f32"),
                         (ones_row, "ones_row_bf"),
                         (id16, "id16_f32")):
            p.dma("scalar", sb[:], I[nm])
        p.v(lambda e: e.memset(ones16c[:], 1.0))
        for (k, val) in enumerate((1e-5, 1e-6, 1.0 / D, -1.0, 1.0, 0.0, 2.0, 0.5)):
            p.v(lambda e, k=k, val=val: e.memset(cvals[:, k:k + 1], val))
            nc.const_aps.register(F32, val, cvals[:, k:k + 1])
        p.barrier()

        def bank(b, p0, p1, c0, n):
            return PS[p0:p1, 512 * b + c0:512 * b + c0 + n]

        BN6 = A4.alloc(12)
        bn6 = af_t[:, BN6:BN6 + 12]
        MV = A4.alloc(32)
        NMU = A4.alloc(16)
        RSG = A4.alloc(16)
        SC0 = A4.alloc(16)
        TSa = A4.alloc(16)
        TSb = A4.alloc(16)
        NMU2 = A4.alloc(16)
        RSG2 = A4.alloc(16)

        def mv_view(which):
            return bass.AP(af_t[:, 0:1].tensor, MV + which, [[A4.pstride, 128], [2, 16]])

        def tm_stats(src_ap, dst2_ap, waits):
            p.v(lambda e: e.bn_stats(out=bn6[:, 0:6], in_=src_ap[:, 0:512]),
                waits=list(waits) + [("ve", p.now("ve"))])
            t2 = p.v(lambda e: e.bn_stats(out=bn6[:, 6:12], in_=src_ap[:, 512:1024]))
            return p.v(lambda e: e.bn_aggr(out=dst2_ap, in_=bn6), waits=[("ve", t2)])

        def stats_rowmath(last_tok, with_rms0, nmu_off=NMU, rsg_off=RSG, c0=0, c1=16):
            n = c1 - c0
            mvv = lambda which: bass.AP(af_t[:, 0:1].tensor, MV + 2 * c0 + which,
                                        [[A4.pstride, 128], [2, n]])
            p.v(lambda e: e.tensor_scalar_mul(out=af_t[:, nmu_off + c0:nmu_off + c1],
                                              in0=mvv(0),
                                              scalar1=-1.0), waits=[("ve", last_tok)])
            a1 = p.s(lambda e: e.activation(out=af_t[:, TSa + c0:TSa + c1], in_=mvv(1),
                                            func=AF.Sqrt,
                                            bias=1e-5), waits=[("ve", last_tok)])
            r1 = p.v(lambda e: e.reciprocal(out=af_t[:, rsg_off + c0:rsg_off + c1],
                                            in_=af_t[:, TSa + c0:TSa + c1]), waits=[("sc", a1)])
            if not with_rms0:
                return r1
            q1 = p.v(lambda e: e.tensor_mul(out=af_t[:, TSa:TSa + 16], in0=af_t[:, rsg_off:rsg_off + 16],
                                            in1=af_t[:, rsg_off:rsg_off + 16]), waits=[("ve", r1)])
            q2 = p.v(lambda e: e.tensor_scalar(out=af_t[:, TSa:TSa + 16], in0=af_t[:, TSa:TSa + 16],
                                               scalar1=-1e-5, scalar2=1.0 + 1e-6,
                                               op0=ALU.mult, op1=ALU.add), waits=[("ve", q1)])
            q3 = p.s(lambda e: e.activation(out=af_t[:, TSb:TSb + 16], in_=af_t[:, TSa:TSa + 16],
                                            func=AF.Sqrt), waits=[("ve", q2)])
            q4 = p.v(lambda e: e.reciprocal(out=af_t[:, TSa:TSa + 16], in_=af_t[:, TSb:TSb + 16]),
                     waits=[("sc", q3)])
            return p.v(lambda e: e.tensor_mul(out=af_t[:, SC0:SC0 + 16],
                                              in0=af_t[:, rsg_off:rsg_off + 16],
                                              in1=af_t[:, TSa:TSa + 16]), waits=[("ve", q4)])

        def emit_attn_a(x_src, x_dst):
            A4.push(); A2.push()
            rin = named("a_rin_sb", [128, 8]); rkv = named("a_rk_sb", [128, 8])
            rvv = named("a_rv_sb", [128, 8])
            l1 = p.dma("scalar", rin[:], I["a_rin"])
            l2 = p.dma("scalar", rkv[:], I["a_rk"])
            l3 = p.dma("scalar", rvv[:], I["a_rv"])
            s1 = p.v(lambda e: e.tensor_mul(out=rkv[:], in0=rkv[:], in1=rin[:]),
                     waits=[("q_scalar", l3)])
            s2 = p.v(lambda e: e.tensor_mul(out=rvv[:], in0=rvv[:], in1=rin[:]), waits=[("ve", s1)])
            WBUF = A2.alloc(2 * NFC * D)
            WTMP = A4.alloc(2 * D)
            prev = None
            st_tok = None
            for (wi, (nm, scl)) in enumerate((("a_WqT", rin), ("a_WkT", rkv), ("a_WvT", rvv))):
                for f in range(NFC):
                    slot = WTMP + D * (f % 2)
                    lw = p.dma("scalar", af_t[:, slot:slot + D], I[nm][128 * f:128 * (f + 1), :],
                               waits=[("ve", prev)])
                    prev = p.v(lambda e, f=f, scl=scl, slot=slot: e.tensor_scalar_mul(
                        out=ab_t[:, WBUF + D * f:WBUF + D * (f + 1)],
                        in0=af_t[:, slot:slot + D], scalar1=scl[:, f:f + 1]),
                        waits=[("q_scalar", lw), ("ve", s2),
                               ("q_gpsimd", st_tok)])
                    st_tok = p.dma("gpsimd", wscr[D * wi + 128 * f:D * wi + 128 * (f + 1), :],
                                   ab_t[:, WBUF + D * f:WBUF + D * (f + 1)],
                                   waits=[("ve", prev)])
            XCH = A4.alloc(4 * D)
            lnt = None
            for qr in range(4):
                ld = p.dma("scalar",
                           bass.AP(af_t[:, 0:1].tensor, XCH, [[A4.pstride, 128], [D, 4], [1, D]]),
                           x_src[512 * qr:512 * (qr + 1), :].rearrange("(j p) d -> p j d", p=128),
                           waits=[("ve", lnt)])
                for j in range(4):
                    c = 4 * qr + j
                    lnt = tm_stats(af_t[:, XCH + D * j:XCH + D * (j + 1)],
                                   af_t[:, MV + 2 * c:MV + 2 * c + 2],
                                   [("q_scalar", ld)] if j == 0 else [])
            sc_tok = stats_rowmath(lnt, True)
            BWT = A2.alloc(NFC * 16)
            for f in range(NFC):
                slot = WTMP + D * (f % 2)
                lb = p.dma("scalar", af_t[:, slot:slot + 16], I["a_bwT"][128 * f:128 * (f + 1), :],
                           waits=[("ve", prev)])
                prev = p.v(lambda e, f=f, slot=slot: e.tensor_scalar_mul(
                    out=ab_t[:, BWT + 16 * f:BWT + 16 * (f + 1)],
                    in0=af_t[:, slot:slot + 16], scalar1=rin[:, f:f + 1]),
                    waits=[("q_scalar", lb)])
            RIN2 = A2.alloc(8)
            for f in range(NFC):
                prev = p.v(lambda e, f=f: e.tensor_mul(out=ab_t[:, RIN2 + f:RIN2 + f + 1],
                                                       in0=rin[:, f:f + 1], in1=rin[:, f:f + 1]),
                           waits=[("q_scalar", l1), ("ve", prev)])
            p.barrier()

            XN0TM = A2.alloc(4 * D)
            XN0FM = A2.alloc(NFC * 512)

            def make_xn0(qr):
                ld = p.dma("scalar",
                           bass.AP(af_t[:, 0:1].tensor, XCH, [[A4.pstride, 128], [D, 4], [1, D]]),
                           x_src[512 * qr:512 * (qr + 1), :].rearrange("(j p) d -> p j d", p=128),
                           waits=[("ve", p.now("ve"))])
                tvs = []
                for j in range(4):
                    c = 4 * qr + j
                    tvs.append(p.v(lambda e, c=c, j=j: e.tensor_scalar(
                        out=ab_t[:, XN0TM + D * j:XN0TM + D * (j + 1)],
                        in0=af_t[:, XCH + D * j:XCH + D * (j + 1)],
                        scalar1=af_t[:, NMU + c:NMU + c + 1], scalar2=af_t[:, SC0 + c:SC0 + c + 1],
                        op0=ALU.add, op1=ALU.mult), waits=[("q_scalar", ld)] if j == 0 else ()))
                in_ap = bass.AP(ab_t[:, 0:1].tensor, XN0TM, [[A2.pstride, 128], [128, 32], [1, 128]])
                out_ap = bass.AP(ab_t[:, 0:1].tensor, XN0FM, [[A2.pstride, 128], [128, 32], [1, 128]])
                return p.dmat(out_ap, in_ap, waits=[("ve", tvs[-1]), ("te", p.now("te"))])

            def xn0fm(f):
                return bass.AP(ab_t[:, 0:1].tensor, XN0FM + 128 * f,
                               [[A2.pstride, 128], [D, 4], [1, 128]])

            def xn0blk(tt, f):
                return ab_t[:, XN0FM + D * tt + 128 * f:XN0FM + D * tt + 128 * (f + 1)]

            XSQ = A2.alloc(512)
            for qr in range(4):
                tq = make_xn0(qr)
                sqv = None
                for f in range(NFC):
                    p.t(lambda e, qr=qr, f=f: e.matmul(
                        out=bank(qr, 0, 16, 0, 512),
                        lhsT=ab_t[:, BWT + 16 * f:BWT + 16 * (f + 1)],
                        rhs=xn0fm(f), start=(f == 0), stop=(f == NFC - 1)),
                        waits=[("q_sync", tq)] if f == 0 else ())
                    sqv = p.s(lambda e, f=f: e.activation(out=ab_t[:, XSQ:XSQ + 512], in_=xn0fm(f),
                                                          func=AF.Square),
                              waits=[("q_sync", tq), ("te", p.now("te"))])
                    p.t(lambda e, qr=qr, f=f: e.matmul(
                        out=bank(4 + qr, 0, 1, 0, 512),
                        lhsT=ab_t[:, RIN2 + f:RIN2 + f + 1],
                        rhs=ab_t[:, XSQ:XSQ + 512], start=(f == 0), stop=(f == NFC - 1)),
                        waits=[("sc", sqv)])
            mm_all = p.now("te")

            BET = A4.alloc(T); LNB = A4.alloc(T); DEN = A4.alloc(T); CCS = A4.alloc(T)
            GRW = A4.alloc(T); FQr = A4.alloc(T)
            EIr = BET
            EPr = LNB
            RM2 = A4.alloc(T)
            bbc = named("a_bb_sb", [16, 1]); bbl = named("a_bbl_sb", [16, 1])
            sbase = named("a_sbase", [16, 1])
            l4 = p.dma("scalar", bbc[:], I["a_bb_col"])
            l5 = p.dma("scalar", bbl[:], I["a_bbl_col"])
            t1 = p.s(lambda e: e.activation(out=af_t[0:16, BET:BET + T], in_=PS[0:16, 0:2048],
                                            func=AF.Sigmoid, bias=bbc[:]),
                     waits=[("te", mm_all), ("q_scalar", l4)])
            t2 = p.s(lambda e: e.activation(out=sbase[:], in_=bbl[:], func=AF.Sigmoid),
                     waits=[("q_scalar", l5)])
            t3 = p.v(lambda e: e.tensor_scalar_mul(out=af_t[0:16, BET:BET + T],
                                                   in0=af_t[0:16, BET:BET + T], scalar1=sbase[:]),
                     waits=[("sc", t1), ("sc", t2)])
            t4 = p.v(lambda e: e.tensor_scalar(out=af_t[0:16, BET:BET + T],
                                               in0=af_t[0:16, BET:BET + T],
                                               scalar1=BETA_MIN_, scalar2=BETA_MAX_,
                                               op0=ALU.max, op1=ALU.min), waits=[("ve", t3)])
            t5 = p.s(lambda e: e.activation(out=af_t[0:16, LNB:LNB + T], in_=af_t[0:16, BET:BET + T],
                                            func=AF.Ln), waits=[("ve", t4)])
            t6 = p.v(lambda e: e.tensor_tensor_scan(out=af_t[0:16, DEN:DEN + T],
                                                    data0=af_t[0:16, BET:BET + T],
                                                    data1=ones16c[:].broadcast_to((16, T)),
                                                    initial=0.0, op0=ALU.mult, op1=ALU.add),
                     waits=[("ve", t4)])
            t7 = p.v(lambda e: e.tensor_scalar_add(out=af_t[0:16, DEN:DEN + T],
                                                   in0=af_t[0:16, DEN:DEN + T], scalar1=1e-6),
                     waits=[("ve", t6)])
            t8 = p.v(lambda e: e.reciprocal(out=af_t[0:16, DEN:DEN + T], in_=af_t[0:16, DEN:DEN + T]),
                     waits=[("ve", t7)])
            tc = None
            for c in range(NCH):
                tc = p.v(lambda e, c=c: e.tensor_tensor_scan(
                    out=af_t[0:16, CCS + 128 * c:CCS + 128 * (c + 1)],
                    data0=af_t[0:16, LNB + 128 * c:LNB + 128 * (c + 1)],
                    data1=af_t[0:16, LNB + 128 * c:LNB + 128 * (c + 1)],
                    initial=0.0, op0=ALU.add, op1=ALU.bypass),
                    waits=[("sc", t5)] if c == 0 else ())
            t9 = p.s(lambda e: e.activation(out=af_t[0:16, GRW:GRW + T], in_=af_t[0:16, CCS:CCS + T],
                                            func=AF.Exp), waits=[("ve", tc)])
            t10 = p.v(lambda e: e.tensor_mul(out=af_t[0:16, FQr:FQr + T], in0=af_t[0:16, GRW:GRW + T],
                                             in1=af_t[0:16, DEN:DEN + T]),
                      waits=[("sc", t9), ("ve", t8)])
            t11 = p.s(lambda e: e.activation(out=af_t[0:16, EIr:EIr + T], in_=af_t[0:16, CCS:CCS + T],
                                             func=AF.Exp, scale=-1.0), waits=[("ve", tc)])
            clast = bass.AP(af_t[:, 0:1].tensor, CCS + 127, [[A4.pstride, 16], [128, 16], [0, 128]])
            t12 = p.v(lambda e: e.tensor_sub(
                out=af_t[0:16, EPr:EPr + T].rearrange("p (a b) -> p a b", a=16),
                in0=clast,
                in1=af_t[0:16, CCS:CCS + T].rearrange("p (a b) -> p a b", a=16)),
                waits=[("ve", tc)])
            t13 = p.s(lambda e: e.activation(out=af_t[0:16, EPr:EPr + T], in_=af_t[0:16, EPr:EPr + T],
                                             func=AF.Exp), waits=[("ve", t12)])
            t14 = p.s(lambda e: e.activation(out=af_t[0:1, RM2:RM2 + T], in_=PS[0:1, 2048:4096],
                                             func=AF.Sqrt, bias=1e-6, scale=1.0 / D),
                      waits=[("te", mm_all)])
            t15 = p.v(lambda e: e.reciprocal(out=af_t[0:1, RM2:RM2 + T], in_=af_t[0:1, RM2:RM2 + T]),
                      waits=[("sc", t14)])
            p.barrier()
            FQT = A4.alloc(256); EIT = A4.alloc(256); EPT = A4.alloc(256); RM2T = A4.alloc(16)
            GBIG = A4.alloc(128)
            for (src, dst, bnk) in ((FQr, FQT, 0), (EIr, EIT, 1), (EPr, EPT, 2)):
                for c in range(NCH):
                    p.t(lambda e, src=src, c=c, bnk=bnk: e.matmul(
                        out=bank(bnk, 0, 128, 16 * c, 16),
                        lhsT=af_t[0:16, src + 128 * c:src + 128 * (c + 1)],
                        rhs=id16[:], start=True, stop=True))
            for c in range(NCH):
                p.t(lambda e, c=c: e.matmul(
                    out=bank(3, 0, 128, c, 1),
                    lhsT=af_t[0:1, RM2 + 128 * c:RM2 + 128 * (c + 1)],
                    rhs=id16[0:1, 0:1], start=True, stop=True))
            tmm = p.now("te")
            e1 = p.v(lambda e: e.tensor_copy(out=af_t[:, FQT:FQT + 256], in_=PS[:, 0:256]),
                     waits=[("te", tmm)])
            e2 = p.v(lambda e: e.tensor_copy(out=af_t[:, EIT:EIT + 256], in_=PS[:, 512:768]))
            e3 = p.v(lambda e: e.tensor_copy(out=af_t[:, EPT:EPT + 256], in_=PS[:, 1024:1280]))
            e4 = p.v(lambda e: e.tensor_copy(out=af_t[:, RM2T:RM2T + 16], in_=PS[:, 1536:1552]))
            for c in range(NCH):
                p.v(lambda e, c=c: e.tensor_scalar_mul(out=af_t[:, EIT + 16 * c:EIT + 16 * (c + 1)],
                                                       in0=af_t[:, EIT + 16 * c:EIT + 16 * (c + 1)],
                                                       scalar1=af_t[:, RM2T + c:RM2T + c + 1]),
                    waits=[("ve", e4)] if c == 0 else ())
                p.v(lambda e, c=c: e.tensor_scalar_mul(out=af_t[:, EPT + 16 * c:EPT + 16 * (c + 1)],
                                                       in0=af_t[:, EPT + 16 * c:EPT + 16 * (c + 1)],
                                                       scalar1=af_t[:, RM2T + c:RM2T + c + 1]))
            GST = A4.alloc(128)
            rst = []
            for hp in range(8):
                src_ap = bass.AP(af_t[:, 0:1].tensor, GRW + 127 + A4.pstride * 2 * hp,
                                 [[A4.pstride, 2], [128, 16]])
                rst.append(p.dma("sync", af_t[0:2, GST + 16 * hp:GST + 16 * (hp + 1)], src_ap,
                                 waits=[("sc", p.now("sc")), ("ve", p.now("ve"))] if hp == 0 else (),
                                 slow=True))
            tg = p.t(lambda e: e.matmul(out=bank(4, 0, 128, 0, 128), lhsT=sel2f[:],
                                        rhs=af_t[0:2, GST:GST + 128], start=True, stop=True),
                     waits=[("q_sync", rst[-1])])
            p.v(lambda e: e.tensor_copy(out=af_t[:, GBIG:GBIG + 128], in_=PS[:, 2048:2176]),
                waits=[("te", tg)])
            p.barrier()

            QTM = A2.alloc(4 * D)
            KRTM = A2.alloc(4 * D)
            KSTM = A2.alloc(4 * D)
            VPTM = A2.alloc(4 * D)
            QFM = A2.alloc(NFC * 512)
            KFM = A2.alloc(NFC * 512)
            OFM = A2.alloc(NFC * 512)
            TTB = A2.alloc(16 * 128)
            MPADe = A2.alloc(512)
            MPADo = A2.alloc(512)
            MF = DEN
            MTMP = DEN + 512
            XIN = FQr
            XIN2 = FQr + D
            YY = GRW
            YT = CCS
            lnw_bc = named("a_lnw_sb", [128, D], BF16)
            lnb_bc = named("a_lnb_sb", [128, D], BF16)
            bo_row = named("a_bo_sb", [1, D], BF16)
            p.dma("gpsimd", lnw_bc[:], I["a_lnw_bc"])
            p.dma("gpsimd", lnb_bc[:], I["a_lnb_bc"])
            p.dma("gpsimd", bo_row[:], I["a_bo_row"])
            mz1 = p.v(lambda e: e.memset(ab_t[:, MPADe:MPADe + 512], 0.0))
            mz2 = p.v(lambda e: e.memset(ab_t[:, MPADo:MPADo + 512], 0.0))
            mz3 = p.v(lambda e: e.memset(af_t[:, MF:MF + 512], 0.0))
            p.barrier()

            wmm_hist = {}
            for qr in range(4):
                tq = make_xn0(qr)
                for (wi, kind) in ((0, "q"), (1, "k"), (2, "v")):
                    wslot = WBUF + (NFC * D) * (wi % 2)
                    wld = p.dma("gpsimd",
                                bass.AP(ab_t[:, 0:1].tensor, wslot,
                                        [[A2.pstride, 128], [D, NFC], [1, D]]),
                                wscr[D * wi:D * (wi + 1), :].rearrange("(f p) d -> p f d", p=128),
                                waits=[("te", wmm_hist.get(wi % 2, 0))])
                    for tt in range(4):
                        c = 4 * qr + tt
                        mmt = None
                        for oseg in range(2):
                            for f in range(NFC):
                                mmt = p.t(lambda e, oseg=oseg, f=f, tt=tt, wslot=wslot: e.matmul(
                                    out=bank(oseg, 0, 128, 0, 512),
                                    lhsT=xn0blk(tt, f),
                                    rhs=ab_t[:, wslot + D * f + 512 * oseg:wslot + D * f + 512 * (oseg + 1)],
                                    start=(f == 0), stop=(f == NFC - 1)),
                                    waits=([("q_sync", tq), ("q_gpsimd", wld),
                                            ("ve", p.now("ve")), ("sc", p.now("sc"))]
                                           if (f == 0 and oseg == 0) else ()))
                        if kind == "q":
                            p.v(lambda e, tt=tt, c=c: e.tensor_tensor(
                                out=ab_t[:, QTM + D * tt:QTM + D * (tt + 1)].rearrange(
                                    "p (a b) -> p a b", a=16),
                                in0=PS[:, 0:1024].rearrange("p (a b) -> p a b", a=16),
                                in1=bass.AP(af_t[:, 0:1].tensor, FQT + 16 * c,
                                            [[A4.pstride, 128], [1, 16], [0, 64]]),
                                op=ALU.mult), waits=[("te", mmt)])
                        elif kind == "k":
                            krc = p.s(lambda e, tt=tt: e.copy(
                                out=ab_t[:, KRTM + D * tt:KRTM + D * (tt + 1)],
                                in_=PS[:, 0:1024]), waits=[("te", mmt)])
                            p.v(lambda e, tt=tt, c=c: e.tensor_tensor(
                                out=ab_t[:, KSTM + D * tt:KSTM + D * (tt + 1)].rearrange(
                                    "p (a b) -> p a b", a=16),
                                in0=PS[:, 0:1024].rearrange("p (a b) -> p a b", a=16),
                                in1=bass.AP(af_t[:, 0:1].tensor, EPT + 16 * c,
                                            [[A4.pstride, 128], [1, 16], [0, 64]]),
                                op=ALU.mult), waits=[("te", mmt), ("sc", krc)])
                        else:
                            p.v(lambda e, tt=tt, c=c: e.tensor_scalar_mul(
                                out=ab_t[:, VPTM + D * tt:VPTM + D * (tt + 1)],
                                in0=PS[:, 0:1024],
                                scalar1=af_t[:, RM2T + c:RM2T + c + 1]), waits=[("te", mmt)])
                    wmm_hist[wi % 2] = p.now("te")
                sc_now = p.now("sc")
                ve_now = p.now("ve")
                trq = []
                for (SRC, DST) in ((QTM, QFM), (KRTM, KFM)):
                    in_ap = bass.AP(ab_t[:, 0:1].tensor, SRC, [[A2.pstride, 128], [128, 32], [1, 128]])
                    out_ap = bass.AP(ab_t[:, 0:1].tensor, DST, [[A2.pstride, 128], [128, 32], [1, 128]])
                    trq.append(p.dmat(out_ap, in_ap,
                                      waits=([("ve", ve_now), ("sc", sc_now)]
                                             if SRC == QTM else ())))
                tr_done = trq[-1]

                for tt in range(4):
                    c = 4 * qr + tt
                    st_toks = {}
                    for hp in range(8):
                        for par in range(2):
                            h = 2 * hp + par
                            sb_ = 6 + (h % 2)
                            mm = p.t(lambda e, hp=hp, par=par, tt=tt, sb_=sb_: e.matmul(
                                out=bank(sb_, 0, 128, 0, 128),
                                lhsT=ab_t[64 * par:64 * par + 64,
                                          KFM + D * tt + 128 * hp:KFM + D * tt + 128 * (hp + 1)],
                                rhs=ab_t[64 * par:64 * par + 64,
                                         QFM + D * tt + 128 * hp:QFM + D * tt + 128 * (hp + 1)],
                                start=True, stop=True),
                                waits=([("q_sync", tr_done), ("ve", p.now("ve"))] if h == 0 else
                                       [("ve", st_toks.get(h - 2, None))]))
                            st_toks[h] = p.v(lambda e, h=h, c=c, sb_=sb_: e.scalar_tensor_tensor(
                                out=ab_t[:, TTB + 128 * h:TTB + 128 * (h + 1)],
                                in0=bank(sb_, 0, 128, 0, 128),
                                scalar=af_t[:, EIT + 16 * c + h:EIT + 16 * c + h + 1],
                                in1=triSI[:], op0=ALU.mult, op1=ALU.mult),
                                waits=[("te", mm)])
                    otok = None
                    for hp in range(8):
                        for par in range(2):
                            h = 2 * hp + par
                            tp = (0, 64 * par)
                            MP = MPADe if par == 0 else MPADo
                            if c > 0:
                                p.t(lambda e, hp=hp, par=par, tt=tt, tp=tp, MP=MP: e.matmul(
                                    out=PS[64 * par:64 * par + 64, 128 * hp:128 * hp + 128],
                                    lhsT=ab_t[:, MP + 64 * hp:MP + 64 * (hp + 1)],
                                    rhs=ab_t[:, QFM + D * tt + 128 * hp:QFM + D * tt + 128 * (hp + 1)],
                                    start=True, stop=False, tile_position=tp),
                                    waits=[("ve", st_toks[h])] if True else ())
                            otok = p.t(lambda e, hp=hp, par=par, tt=tt, h=h, tp=tp, c=c: e.matmul(
                                out=PS[64 * par:64 * par + 64, 128 * hp:128 * hp + 128],
                                lhsT=ab_t[:, VPTM + D * tt + 64 * h:VPTM + D * tt + 64 * (h + 1)],
                                rhs=ab_t[:, TTB + 128 * h:TTB + 128 * (h + 1)],
                                start=(c == 0), stop=True, tile_position=tp),
                                waits=[("ve", st_toks[h])])
                            p.t(lambda e, hp=hp, par=par, tt=tt, h=h, tp=tp: e.matmul(
                                out=PS[64 * par:64 * par + 64, 1024 + 64 * hp:1024 + 64 * (hp + 1)],
                                lhsT=ab_t[:, KSTM + D * tt + 64 * h:KSTM + D * tt + 64 * (h + 1)],
                                rhs=ab_t[:, VPTM + D * tt + 64 * h:VPTM + D * tt + 64 * (h + 1)],
                                start=True, stop=True, tile_position=tp))
                    kv_tok = p.now("te")
                    oev = p.s(lambda e, tt=tt: e.copy(
                        out=bass.AP(ab_t[:, 0:1].tensor, OFM + 128 * tt,
                                    [[A2.pstride, 128], [512, 8], [1, 128]]),
                        in_=PS[:, 0:1024].rearrange("p (a b) -> p a b", a=8)),
                        waits=[("te", kv_tok)])
                    u1 = p.v(lambda e, c=c: e.tensor_tensor(
                        out=af_t[:, MTMP:MTMP + 512], in0=af_t[:, MF:MF + 512],
                        in1=bass.AP(af_t[:, 0:1].tensor, GBIG + c,
                                    [[A4.pstride, 128], [16, 8], [0, 64]]),
                        op=ALU.mult), waits=[("te", kv_tok), ("ve", mz3)])
                    u2 = p.v(lambda e: e.tensor_add(out=af_t[:, MF:MF + 512],
                                                    in0=af_t[:, MTMP:MTMP + 512],
                                                    in1=PS[:, 1024:1536]), waits=[("ve", u1)])
                    u3 = p.v(lambda e: e.tensor_copy(out=ab_t[0:64, MPADe:MPADe + 512],
                                                     in_=af_t[0:64, MF:MF + 512]),
                             waits=[("ve", u2), ("ve", mz1)])
                    u4 = p.v(lambda e: e.tensor_copy(out=ab_t[64:128, MPADo:MPADo + 512],
                                                     in_=af_t[64:128, MF:MF + 512]),
                             waits=[("ve", u2), ("ve", mz2)])
                    p.t(lambda e: e.nop(), waits=[("ve", u4), ("sc", oev)])

                wold = p.dma("gpsimd",
                             bass.AP(ab_t[:, 0:1].tensor, WBUF + NFC * D,
                                     [[A2.pstride, 128], [D, NFC], [1, D]]),
                             I["a_WoT"].rearrange("(f p) d -> p f d", p=128),
                             waits=[("te", wmm_hist.get(1, 0))])
                for tt in range(4):
                    c = 4 * qr + tt
                    mmt = None
                    for oseg in range(2):
                        for f in range(NFC):
                            mmt = p.t(lambda e, oseg=oseg, f=f, tt=tt: e.matmul(
                                out=bank(2 + oseg, 0, 128, 0, 512),
                                lhsT=ab_t[:, OFM + 512 * f + 128 * tt:OFM + 512 * f + 128 * (tt + 1)],
                                rhs=ab_t[:, WBUF + NFC * D + D * f + 512 * oseg:WBUF + NFC * D + D * f + 512 * (oseg + 1)],
                                start=(f == 0), stop=False),
                                waits=(([("q_gpsimd", wold)] if tt == 0 else [])
                                       + ([("ve", p.now("ve")), ("sc", p.now("sc"))]
                                          if (f == 0 and oseg == 0) else [])))
                        mmt = p.t(lambda e, oseg=oseg, tt=tt: e.matmul(
                            out=bank(2 + oseg, 0, 128, 0, 512),
                            lhsT=ones_row[0:1, 0:128],
                            rhs=bo_row[0:1, 512 * oseg:512 * (oseg + 1)],
                            start=False, stop=True))
                    ld = p.dma("scalar", af_t[:, XIN:XIN + D], x_src[128 * c:128 * (c + 1), :],
                               waits=[("ve", p.now("ve"))])
                    w1_ = p.v(lambda e, c=c: e.tensor_scalar(
                        out=af_t[:, XIN2:XIN2 + D], in0=af_t[:, XIN:XIN + D],
                        scalar1=af_t[:, NMU + c:NMU + c + 1], scalar2=af_t[:, RSG + c:RSG + c + 1],
                        op0=ALU.add, op1=ALU.mult), waits=[("q_scalar", ld)])
                    w2_ = p.v(lambda e: e.tensor_add(out=af_t[:, YY:YY + D],
                                                     in0=af_t[:, XIN2:XIN2 + D], in1=PS[:, 1024:2048]),
                              waits=[("ve", w1_), ("te", mmt)])
                    bnt = tm_stats(af_t[:, YY:YY + D], af_t[:, MV + 2 * c:MV + 2 * c + 2],
                                   [("ve", w2_)])
                    a1 = p.s(lambda e, c=c: e.activation(out=af_t[:, TSa:TSa + 1],
                                                         in_=af_t[:, MV + 2 * c + 1:MV + 2 * c + 2],
                                                         func=AF.Sqrt, bias=1e-5),
                             waits=[("ve", bnt)])
                    a2 = p.v(lambda e: e.reciprocal(out=af_t[:, TSb:TSb + 1], in_=af_t[:, TSa:TSa + 1]),
                             waits=[("sc", a1)])
                    a3 = p.v(lambda e, c=c: e.tensor_scalar(
                        out=af_t[:, YT:YT + D], in0=af_t[:, YY:YY + D],
                        scalar1=af_t[:, MV + 2 * c:MV + 2 * c + 1], scalar2=af_t[:, TSb:TSb + 1],
                        op0=ALU.subtract, op1=ALU.mult),
                        waits=[("ve", a2), ("q_scalar", p.now("q_scalar"))])
                    a4 = p.v(lambda e: e.tensor_mul(out=af_t[:, YT:YT + D], in0=af_t[:, YT:YT + D],
                                                    in1=lnw_bc[:]), waits=[("ve", a3)])
                    a5 = p.v(lambda e: e.tensor_add(out=af_t[:, YT:YT + D], in0=af_t[:, YT:YT + D],
                                                    in1=lnb_bc[:]), waits=[("ve", a4)])
                    a6 = p.v(lambda e: e.tensor_add(out=af_t[:, YT:YT + D], in0=af_t[:, YT:YT + D],
                                                    in1=af_t[:, XIN:XIN + D]),
                             waits=[("ve", a5)])
                    p.dma("scalar", x_dst[128 * c:128 * (c + 1), :], af_t[:, YT:YT + D],
                          waits=[("ve", a6)])
                wmm_hist[1] = p.now("te")
            p.barrier()
            A4.pop(); A2.pop()

        BETA_MIN_, BETA_MAX_ = 0.9, 0.9995

        def emit_ffn(x_src, x_dst, nm_sfx):
            A4.push(); A2.push()
            XF = A4.alloc(16 * D)
            WTMP0 = A4.alloc(4 * D)
            RNC = A4.alloc(16)
            NM_ = A4.alloc(8)
            B1A = A4.alloc(22)
            B1G = A4.alloc(22)

            def xf(c):
                return af_t[:, XF + D * c:XF + D * (c + 1)]

            lnt = None
            app = {}
            for ehalf in (0, 1):
                for qr in (2 * ehalf, 2 * ehalf + 1):
                    ld = p.dma("scalar",
                               bass.AP(af_t[:, 0:1].tensor, XF + 4 * D * qr,
                                       [[A4.pstride, 128], [D, 4], [1, D]]),
                               x_src[512 * qr:512 * (qr + 1), :].rearrange("(j p) d -> p j d", p=128))
                    for j in range(4):
                        c = 4 * qr + j
                        lnt = tm_stats(xf(c), af_t[:, MV + 2 * c:MV + 2 * c + 2],
                                       [("q_scalar", ld)] if j == 0 else [])
                rm_e = stats_rowmath(lnt, False, c0=8 * ehalf, c1=8 * ehalf + 8)
                ap_last = None
                for c in range(8 * ehalf, 8 * ehalf + 8):
                    ap_last = p.v(lambda e, c=c: e.tensor_scalar(
                        out=xf(c), in0=xf(c),
                        scalar1=af_t[:, NMU + c:NMU + c + 1], scalar2=af_t[:, RSG + c:RSG + c + 1],
                        op0=ALU.add, op1=ALU.mult),
                        waits=[("ve", rm_e)] if c == 8 * ehalf else ())
                app[ehalf] = ap_last

            def build_shift_full(XBA, XF0, HSH, c0=0, c1=NCH, extra0=()):
                n = c1 - c0
                w0 = list(extra0) + [("q_sync", p.now("q_sync")),
                                     ("q_scalar", p.now("q_scalar"))]
                dA = p.dma("gpsimd",
                           bass.AP(ab_t[:, 0:1].tensor, XBA + 512 * c0,
                                   [[A2.pstride, 128], [512, n], [1, 512]]),
                           bass.AP(af_t[:, 0:1].tensor, XF0 + D * c0,
                                   [[A4.pstride, 128], [D, n], [1, 512]]), waits=w0)
                mz = None
                if c0 == 0:
                    mz = p.v(lambda e: e.memset(ab_t[0:1, HSH:HSH + 512], 0.0),
                             waits=list(extra0) + [("q_gpsimd", p.now("q_gpsimd")),
                                    ("q_scalar", p.now("q_scalar")),
                                    ("q_sync", p.now("q_sync"))])
                p.dma("gpsimd",
                      bass.AP(ab_t[:, 0:1].tensor, HSH + 512 * c0 + A2.pstride,
                              [[A2.pstride, 64], [512, n], [1, 512]]),
                      bass.AP(af_t[:, 0:1].tensor, XF0 + D * c0 + 512,
                              [[A4.pstride, 64], [D, n], [1, 512]]))
                dB = p.dma("gpsimd",
                           bass.AP(ab_t[:, 0:1].tensor, HSH + 512 * c0 + 65 * A2.pstride,
                                   [[A2.pstride, 63], [512, n], [1, 512]]),
                           bass.AP(af_t[:, 0:1].tensor, XF0 + D * c0 + 512 + 64 * A4.pstride,
                                   [[A4.pstride, 63], [D, n], [1, 512]]))
                cb0 = max(c0, 1)
                dC = p.dma("gpsimd",
                           bass.AP(ab_t[:, 0:1].tensor, HSH + 512 * cb0,
                                   [[A2.pstride, 1], [512, c1 - cb0], [1, 512]]),
                           bass.AP(af_t[:, 0:1].tensor, XF0 + 512 + D * (cb0 - 1) + A4.pstride * 127,
                                   [[A4.pstride, 1], [D, c1 - cb0], [1, 512]]))
                return dA, dC, mz

            def shift_builder(stage_off, hsh_off, consume, extra0=()):
                cts = {}
                dms = {}
                for c in range(NCH):
                    sl = stage_off + D * (c % 2)
                    w = list(extra0) + [("ve", p.now("ve")), ("q_sync", p.now("q_sync")),
                                        ("q_scalar", p.now("q_scalar")),
                                        ("te", p.now("te")), ("sc", p.now("sc"))] if c == 0 else \
                        [("q_sync", p.now("q_sync")), ("q_scalar", p.now("q_scalar"))]
                    cts[c] = p.v(lambda e, c=c, sl=sl: e.tensor_copy(out=ab_t[:, sl:sl + D],
                                                                    in_=xf(c)), waits=w)
                    if c == 0:
                        mz = p.v(lambda e: e.memset(ab_t[0:1, hsh_off:hsh_off + 512], 0.0))
                        d1 = p.dma("scalar", bass.AP(ab_t[:, 0:1].tensor, hsh_off + A2.pstride,
                                                     [[A2.pstride, 127], [1, 512]]),
                                   bass.AP(ab_t[:, 0:1].tensor, sl + 512,
                                           [[A2.pstride, 127], [1, 512]]),
                                   waits=[("ve", cts[0])])
                    else:
                        d1 = p.dma("scalar", bass.AP(ab_t[:, 0:1].tensor, hsh_off + A2.pstride,
                                                     [[A2.pstride, 127], [1, 512]]),
                                   bass.AP(ab_t[:, 0:1].tensor, sl + 512,
                                           [[A2.pstride, 127], [1, 512]]),
                                   waits=[("ve", cts[c])])
                        d1 = p.dma("scalar", ab_t[0:1, hsh_off:hsh_off + 512],
                                   bass.AP(ab_t[:, 0:1].tensor,
                                           stage_off + D * ((c - 1) % 2) + 512 + A2.pstride * 127,
                                           [[A2.pstride, 1], [1, 512]]))
                    dms[c] = consume(c, sl, hsh_off, [("q_scalar", d1), ("ve", cts[c])] +
                                     ([("ve", mz)] if c == 0 else []))
                return dms

            HHALF = NCH * 512
            for l in range(2):
                A2.push()
                QFM = A2.alloc(NCH * 512)
                KFM = A2.alloc(NCH * 512)
                VTM = A2.alloc(NCH * 512)
                QSB = A2.alloc(128)
                A2.push()
                HFM = A2.alloc(NCH * D)
                A2.push()
                XBA = A2.alloc(NCH * 512)
                HSH1 = A2.alloc(NCH * 512)
                if l == 0:
                    e0a, e0b = [("ve", app[0])], [("ve", app[1])]
                else:
                    e0a = e0b = [("ve", fadd_last)]
                sh_ct, sh_d2, sh_mz = build_shift_full(XBA, XF, HSH1, 0, 8, extra0=e0a)
                sh_ct1, sh_d21, _ = build_shift_full(XBA, XF, HSH1, 8, NCH, extra0=e0b)
                trA = p.dmat(bass.AP(ab_t[:, 0:1].tensor, HFM,
                                     [[A2.pstride, 128], [128, 32], [1, 128]]),
                             bass.AP(ab_t[:, 0:1].tensor, XBA,
                                     [[A2.pstride, 128], [128, 32], [1, 128]]),
                             waits=[("q_gpsimd", sh_ct), ("te", p.now("te"))])
                trB = p.dmat(bass.AP(ab_t[:, 0:1].tensor, HFM + HHALF,
                                     [[A2.pstride, 128], [128, 32], [1, 128]]),
                             bass.AP(ab_t[:, 0:1].tensor, HSH1,
                                     [[A2.pstride, 128], [128, 32], [1, 128]]),
                             waits=[("q_gpsimd", sh_d2), ("ve", sh_mz)])
                trA1 = p.dmat(bass.AP(ab_t[:, 0:1].tensor, HFM + 4096,
                                      [[A2.pstride, 128], [128, 32], [1, 128]]),
                              bass.AP(ab_t[:, 0:1].tensor, XBA + 4096,
                                      [[A2.pstride, 128], [128, 32], [1, 128]]),
                              waits=[("q_gpsimd", sh_ct1)])
                trB1 = p.dmat(bass.AP(ab_t[:, 0:1].tensor, HFM + HHALF + 4096,
                                      [[A2.pstride, 128], [128, 32], [1, 128]]),
                              bass.AP(ab_t[:, 0:1].tensor, HSH1 + 4096,
                                      [[A2.pstride, 128], [128, 32], [1, 128]]),
                              waits=[("q_gpsimd", sh_d21)])
                sh_tok0 = trB
                sh_tok = trB1
                A2.pop()
                WSL = A2.alloc(NFC * 512)
                TBS = {k: A2.alloc(NCH * 64) for k in ("qC", "qS", "kC", "kS")}
                SQS = A2.alloc(512); QH = A2.alloc(512); T1 = A2.alloc(512); T2 = A2.alloc(512)
                QST = A2.alloc(2 * 512)

                tbl_hist = {}
                for (ti, (dst, nm, bc)) in enumerate((("qC", "qC_bf", f"f_qs{l}_bc"),
                                                      ("qS", "qS_bf", f"f_qs{l}_sw"),
                                                      ("kC", "kC_bf", f"f_ks{l}_bc"),
                                                      ("kS", "kS_bf", f"f_ks{l}_sw"))):
                    in_ap = bass.AP(I[nm].tensor, 0, [[64, 128], [128 * 64, NCH], [1, 64]])
                    out_ap = bass.AP(ab_t[:, 0:1].tensor, TBS[dst],
                                     [[A2.pstride, 128], [64, NCH], [1, 64]])
                    lt = p.dma("scalar", out_ap, in_ap,
                               waits=[("ve", p.now("ve")), ("q_sync", trA1)])
                    qsl = QSB + 64 * (ti % 2)
                    lq = p.dma("gpsimd", ab_t[:, qsl:qsl + 64], I[bc],
                               waits=[("ve", tbl_hist.get(ti - 2,
                                             fadd_last if l == 1 else 0))])
                    p.v(lambda e, dst=dst, qsl=qsl: e.tensor_tensor(
                        out=bass.AP(ab_t[:, 0:1].tensor, TBS[dst],
                                    [[A2.pstride, 128], [64, NCH], [1, 64]]),
                        in0=bass.AP(ab_t[:, 0:1].tensor, TBS[dst],
                                    [[A2.pstride, 128], [64, NCH], [1, 64]]),
                        in1=bass.AP(ab_t[:, 0:1].tensor, qsl,
                                    [[A2.pstride, 128], [0, NCH], [1, 64]]),
                        op=ALU.mult), waits=[("q_scalar", lt), ("q_gpsimd", lq)])
                    tbl_hist[ti] = p.now("ve")

                prev_kind_last = trB

                def hfm_blk(c, f):
                    o = HFM + (HHALF if f >= 4 else 0) + 512 * c + 128 * (f % 4)
                    return ab_t[:, o:o + 128]

                for (oi, kind) in ((0, "q"), (1, "k"), (2, "v")):
                    wld = p.dma("gpsimd", ab_t[:, WSL:WSL + NFC * 512].rearrange(
                                    "p (f d) -> p f d", f=NFC),
                                I[f"f_wqkv{l}"][:, 512 * oi:512 * (oi + 1)].rearrange(
                                    "(f p) d -> p f d", p=128),
                                waits=[("te", p.now("te")), ("q_sync", trA)])
                    cp_toks = {}
                    tr_toks = {}
                    kind_sync0 = prev_kind_last
                    for c in range(NCH):
                        bnk = c % 2
                        mmt = None
                        for f in range(NFC):
                            mmt = p.t(lambda e, c=c, f=f, bnk=bnk: e.matmul(
                                out=bank(bnk, 0, 128, 0, 512),
                                lhsT=hfm_blk(c, f),
                                rhs=ab_t[:, WSL + 512 * f:WSL + 512 * (f + 1)],
                                start=(f == 0), stop=(f == NFC - 1)),
                                waits=([("q_sync", sh_tok0), ("q_gpsimd", wld),
                                        ("sc", p.now("sc")), ("ve", p.now("ve"))]
                                       if (c == 0 and f == 0) else
                                       (([("q_sync", sh_tok)] if c == 8 else []) +
                                        [("sc", cp_toks.get(c - 2, 0)),
                                         ("ve", cp_toks.get(c - 2, 0) and 0)] if f == 0 else ())))
                        if kind == "v":
                            cp_toks[c] = p.s(lambda e, c=c: e.copy(
                                out=ab_t[:, VTM + 512 * c:VTM + 512 * (c + 1)],
                                in_=bank(c % 2, 0, 128, 0, 512)), waits=[("te", mmt)])
                            continue
                        sl2 = QST + 512 * (c % 2)
                        cp_toks[c] = p.s(lambda e, c=c, sl2=sl2: e.copy(
                            out=ab_t[:, sl2:sl2 + 512],
                            in_=bank(c % 2, 0, 128, 0, 512)),
                            waits=[("te", mmt), ("q_sync", tr_toks.get(c - 2, kind_sync0))])
                        sq = p.s(lambda e, sl2=sl2: e.activation(
                            out=ab_t[:, SQS:SQS + 512], in_=ab_t[:, sl2:sl2 + 512],
                            func=AF.Square), waits=[("sc", cp_toks[c]), ("ve", p.now("ve"))])
                        r1 = p.v(lambda e: e.reduce_sum(
                            out=af_t[:, RNC:RNC + 8],
                            in_=ab_t[:, SQS:SQS + 512].rearrange("p (h d) -> p h d", h=8),
                            axis=AX.X), waits=[("sc", sq)])
                        r2 = p.s(lambda e: e.activation(out=af_t[:, RNC + 8:RNC + 16],
                                                        in_=af_t[:, RNC:RNC + 8], func=AF.Sqrt),
                                 waits=[("ve", r1)])
                        r3 = p.v(lambda e: e.tensor_scalar_max(out=af_t[:, RNC + 8:RNC + 16],
                                                               in0=af_t[:, RNC + 8:RNC + 16],
                                                               scalar1=1e-12), waits=[("sc", r2)])
                        r4 = p.v(lambda e: e.reciprocal(out=af_t[:, RNC:RNC + 8],
                                                        in_=af_t[:, RNC + 8:RNC + 16]),
                                 waits=[("ve", r3)])
                        Ct, St = ("qC", "qS") if kind == "q" else ("kC", "kS")
                        h1 = p.v(lambda e, sl2=sl2: e.tensor_tensor(
                            out=ab_t[:, QH:QH + 512].rearrange("p (h d) -> p h d", h=8),
                            in0=ab_t[:, sl2:sl2 + 512].rearrange("p (h d) -> p h d", h=8),
                            in1=bass.AP(af_t[:, 0:1].tensor, RNC,
                                        [[A4.pstride, 128], [1, 8], [0, 64]]),
                            op=ALU.mult), waits=[("ve", r4)])
                        t1_ = p.v(lambda e, c=c, Ct=Ct: e.tensor_tensor(
                            out=ab_t[:, T1:T1 + 512].rearrange("p (h d) -> p h d", h=8),
                            in0=ab_t[:, QH:QH + 512].rearrange("p (h d) -> p h d", h=8),
                            in1=bass.AP(ab_t[:, 0:1].tensor, TBS[Ct] + 64 * c,
                                        [[A2.pstride, 128], [0, 8], [1, 64]]),
                            op=ALU.mult), waits=[("ve", h1)])
                        swap_ap = bass.AP(ab_t[:, 0:1].tensor, QH + 32,
                                          [[A2.pstride, 128], [64, 8], [-32, 2], [1, 32]])
                        t2_ = p.v(lambda e, c=c, St=St, swap_ap=swap_ap: e.tensor_tensor(
                            out=ab_t[:, T2:T2 + 512].rearrange("p (h a b) -> p h a b", h=8, a=2),
                            in0=swap_ap,
                            in1=bass.AP(ab_t[:, 0:1].tensor, TBS[St] + 64 * c,
                                        [[A2.pstride, 128], [0, 8], [32, 2], [1, 32]]),
                            op=ALU.mult), waits=[("ve", t1_)])
                        t3_ = p.v(lambda e, sl2=sl2: e.tensor_add(
                            out=ab_t[:, sl2:sl2 + 512],
                            in0=ab_t[:, T1:T1 + 512], in1=ab_t[:, T2:T2 + 512]),
                            waits=[("ve", t2_)])
                        DSTF = QFM if kind == "q" else KFM
                        tr_toks[c] = p.dmat(
                            bass.AP(ab_t[:, 0:1].tensor, DSTF + 512 * c,
                                    [[A2.pstride, 128], [128, 4], [1, 128]]),
                            bass.AP(ab_t[:, 0:1].tensor, sl2,
                                    [[A2.pstride, 128], [128, 4], [1, 128]]),
                            waits=[("ve", t3_)])
                        if c == NCH - 1:
                            prev_kind_last = tr_toks[c]
                A2.pop()
                tr_all = p.now("q_sync")

                A2.push()
                OFM = A2.alloc(NCH * 512)
                PBF = A2.alloc(2 * 4 * 640)
                ABF = A2.alloc(8 * 512)
                DQB = A2.alloc(2 * 4 * 128)
                iters = [(h, w) for h in range(HEADS_F) for w in range(4)]
                NI = len(iters)
                p_toks = {}; dq_toks = {}; s_toks = {}; tr_tok = {}
                etok = {}; tok_trlast = {}; av_last = {}
                sc_attn0 = p.now("sc"); ve_attn0 = p.now("ve")

                def pb(j):
                    return PBF + 2560 * (j % 2)

                def db(j):
                    return DQB + 512 * (j % 2)

                def nmo(j):
                    return NM_ + 4 * (j % 2)

                def tr_qcs(w, kc):
                    if w > 0:
                        return [q_ for q_ in range(4) if 0 <= kc - q_ <= 4]
                    return [q_ for q_ in range(4) if q_ >= kc]

                def emit_S(j, qc):
                    (h, w) = iters[j]
                    hp, par = h // 2, h % 2
                    cq = 4 * w + qc
                    if w == 0:
                        width = 128 * (qc + 1)
                        kc0g = 0
                    else:
                        width = 640
                        kc0g = 4 * (w - 1) + qc
                    nb = width // 128
                    qb = ab_t[64 * par:64 * par + 64,
                              QFM + 512 * cq + 128 * hp:QFM + 512 * cq + 128 * (hp + 1)]
                    rhs1 = bass.AP(ab_t[:, 0:1].tensor,
                                   KFM + 512 * kc0g + 128 * hp + A2.pstride * 64 * par,
                                   [[A2.pstride, 64], [512, min(nb, 4)], [1, 128]])
                    if j == 0 and qc == 0:
                        w0 = [("q_sync", tr_all), ("sc", sc_attn0), ("ve", ve_attn0)]
                    else:
                        w0 = [("sc", p_toks[(j, qc - 1)] if qc > 0 else p_toks[(j - 1, 3)])]
                    m1 = p.t(lambda e, qb=qb, rhs1=rhs1, n=min(nb, 4): e.matmul(
                        out=bank(2, 0, 128, 0, 128 * n), lhsT=qb, rhs=rhs1,
                        start=True, stop=True), waits=w0)
                    if nb > 4:
                        rhs2 = bass.AP(ab_t[:, 0:1].tensor,
                                       KFM + 512 * (kc0g + 4) + 128 * hp + A2.pstride * 64 * par,
                                       [[A2.pstride, 64], [512, 1], [1, 128]])
                        m1 = p.t(lambda e, qb=qb, rhs2=rhs2: e.matmul(
                            out=bank(3, 0, 128, 0, 128), lhsT=qb, rhs=rhs2,
                            start=True, stop=True))
                    s_toks[(j, qc)] = m1

                def emit_soft(j, qc):
                    (h, w) = iters[j]
                    width = 640 if w > 0 else 128 * (qc + 1)
                    if w == 0:
                        mk = p.v(lambda e, qc=qc: e.tensor_add(
                            out=bank(2, 0, 128, 128 * qc, 128),
                            in0=bank(2, 0, 128, 128 * qc, 128), in1=triR[:]),
                            waits=[("te", s_toks[(j, qc)])])
                    else:
                        p.v(lambda e: e.tensor_add(
                            out=bank(2, 0, 128, 0, 128),
                            in0=bank(2, 0, 128, 0, 128), in1=triL[:]),
                            waits=[("te", s_toks[(j, qc)])])
                        mk = p.v(lambda e: e.tensor_add(
                            out=bank(3, 0, 128, 0, 128),
                            in0=bank(3, 0, 128, 0, 128), in1=triR[:]))
                    no = nmo(j)
                    mx = p.v(lambda e, width=width, no=no: e.reduce_max(
                        out=af_t[:, no:no + 1], in_=PS[:, 1024:1024 + width],
                        axis=AX.X, negate=True), waits=[("ve", mk)])
                    ex = p.s(lambda e, j=j, qc=qc, width=width, no=no: e.activation(
                        out=ab_t[:, pb(j) + 640 * qc:pb(j) + 640 * qc + width],
                        in_=PS[:, 1024:1024 + width], func=AF.Exp,
                        bias=af_t[:, no:no + 1],
                        accum_out=af_t[:, no + 1:no + 2]),
                        waits=[("ve", mx), ("te", tok_trlast.get(j - 2, 0))])
                    rc = p.v(lambda e, no=no: e.reciprocal(
                        out=af_t[:, no + 2:no + 3], in_=af_t[:, no + 1:no + 2]),
                        waits=[("sc", ex)])
                    dqt = p.v(lambda e, j=j, qc=qc, no=no: e.tensor_scalar_mul(
                        out=ab_t[:, db(j) + 128 * qc:db(j) + 128 * (qc + 1)],
                        in0=id128[:], scalar1=af_t[:, no + 2:no + 3]),
                        waits=[("ve", rc), ("te", tok_trlast.get(j - 2, 0))])
                    p_toks[(j, qc)] = ex
                    dq_toks[(j, qc)] = dqt

                def emit_tr(i, kc):
                    (h, w) = iters[i]
                    qcs = tr_qcs(w, kc)
                    qmax = max(qcs)
                    wl = [("sc", p_toks[(i, qmax)]), ("ve", dq_toks[(i, qmax)])]
                    if kc >= 2:
                        wl.append(etok[(i, kc - 2)])
                    elif i > 0:
                        pk = etok.get((i - 1, 6 + kc)) or etok.get((i - 1, 2 + kc))
                        if pk is not None:
                            wl.append(pk)
                    tt = None
                    for (qi, qc) in enumerate(qcs):
                        sb = (kc - qc) if w > 0 else kc
                        tt = p.t(lambda e, i=i, qc=qc, sb=sb, kc=kc: e.matmul(
                            out=bank(kc % 2, 0, 128, 128 * qc, 128),
                            lhsT=ab_t[:, pb(i) + 640 * qc + 128 * sb:pb(i) + 640 * qc + 128 * (sb + 1)],
                            rhs=ab_t[:, db(i) + 128 * qc:db(i) + 128 * (qc + 1)],
                            start=True, stop=True), waits=wl if qi == 0 else ())
                    tr_tok[(i, kc)] = tt

                def emit_evac(i, kc):
                    (h, w) = iters[i]
                    qcs = tr_qcs(w, kc)
                    span0, span1 = 128 * min(qcs), 128 * (max(qcs) + 1)
                    if kc % 2 == 0:
                        ev = p.s(lambda e, kc=kc, span0=span0, span1=span1: e.copy(
                            out=ab_t[:, ABF + 512 * kc + span0:ABF + 512 * kc + span1],
                            in_=bank(kc % 2, 0, 128, span0, span1 - span0)),
                            waits=[("te", tr_tok[(i, kc)])])
                        etok[(i, kc)] = ("sc", ev)
                    else:
                        ev = p.v(lambda e, kc=kc, span0=span0, span1=span1: e.tensor_copy(
                            out=ab_t[:, ABF + 512 * kc + span0:ABF + 512 * kc + span1],
                            in_=bank(kc % 2, 0, 128, span0, span1 - span0)),
                            waits=[("te", tr_tok[(i, kc)])])
                        etok[(i, kc)] = ("ve", ev)

                def emit_AV(i, qc):
                    (h, w) = iters[i]
                    if w > 0:
                        kcs = list(range(qc, qc + 5))
                    else:
                        kcs = list(range(0, qc + 1))
                    es = max([etok[(i, kc)][1] for kc in kcs if kc % 2 == 0], default=0)
                    ev = max([etok[(i, kc)][1] for kc in kcs if kc % 2 == 1], default=0)
                    last = None
                    for (ki, kc) in enumerate(kcs):
                        vglob = (4 * (w - 1) + kc) if w > 0 else kc
                        last = p.t(lambda e, qc=qc, kc=kc, vglob=vglob, h=h,
                                   st_=(ki == 0), sp_=(ki == len(kcs) - 1): e.matmul(
                            out=bank(4 + qc, 0, 64, 0, 128),
                            lhsT=ab_t[:, VTM + 512 * vglob + 64 * h:VTM + 512 * vglob + 64 * (h + 1)],
                            rhs=ab_t[:, ABF + 512 * kc + 128 * qc:ABF + 512 * kc + 128 * (qc + 1)],
                            start=st_, stop=sp_),
                            waits=[("sc", es), ("ve", ev)] if ki == 0 else ())
                    return last

                for qc in range(4):
                    emit_S(0, qc)
                    emit_soft(0, qc)
                for i in range(NI):
                    (h, w) = iters[i]
                    hp, par = h // 2, h % 2
                    nkc = 8 if w > 0 else 4
                    have_next = i + 1 < NI
                    for blk in range(4):
                        k0 = 2 * blk
                        if k0 < nkc:
                            emit_tr(i, k0)
                            emit_tr(i, k0 + 1)
                            emit_evac(i, k0)
                            emit_evac(i, k0 + 1)
                        if have_next:
                            if blk >= 1:
                                emit_soft(i + 1, blk - 1)
                            if blk < 4:
                                emit_S(i + 1, blk)
                    if have_next:
                        emit_soft(i + 1, 3)
                    tok_trlast[i] = tr_tok[(i, nkc - 1)]
                    avl = None
                    for qc in range(4):
                        avl = emit_AV(i, qc)
                    av_last[i] = avl
                    oev = p.s(lambda e, h=h, w=w, hp=hp, par=par: e.copy(
                        out=bass.AP(ab_t[:, 0:1].tensor,
                                    OFM + 512 * 4 * w + 128 * hp + A2.pstride * 64 * par,
                                    [[A2.pstride, 64], [512, 4], [1, 128]]),
                        in_=bass.AP(PS[:, 0:1].tensor, 512 * 4,
                                    [[4096, 64], [512, 4], [1, 128]])),
                        waits=[("te", av_last[i])])

                WSL2 = A2.alloc(4 * D)
                wld = p.dma("gpsimd", ab_t[:, WSL2:WSL2 + 4 * D].rearrange("p (f d) -> p f d", f=4),
                            I[f"f_wo{l}"].rearrange("(f p) d -> p f d", p=128),
                            waits=[("te", p.now("te"))])
                radd = {}
                for c in range(NCH):
                    mmt = None
                    b0 = 2 * (c % 2)
                    for oseg in range(2):
                        for f in range(4):
                            mmt = p.t(lambda e, c=c, oseg=oseg, f=f, b0=b0: e.matmul(
                                out=bank(b0 + oseg, 0, 128, 0, 512),
                                lhsT=ab_t[:, OFM + 512 * c + 128 * f:OFM + 512 * c + 128 * (f + 1)],
                                rhs=ab_t[:, WSL2 + D * f + 512 * oseg:WSL2 + D * f + 512 * (oseg + 1)],
                                start=(f == 0), stop=(f == 3)),
                                waits=([("q_gpsimd", wld), ("ve", p.now("ve")),
                                        ("sc", p.now("sc"))]
                                       if (c == 0 and oseg == 0 and f == 0) else
                                       ([("ve", radd.get(c - 2, 0))]
                                        if (oseg == 0 and f == 0) else ())))
                    radd[c] = p.v(lambda e, c=c, b0=b0: e.tensor_add(
                        out=xf(c), in0=xf(c), in1=PS[:, 512 * b0:512 * b0 + 1024]),
                        waits=[("te", mmt)])
                A2.pop()
                A2.pop()

                A2.push()
                HFM2 = A2.alloc(NCH * D)
                LNGB = A2.alloc(D)
                LNBB = A2.alloc(D)
                A2.push()
                XBA2 = A2.alloc(NCH * 512)
                HSH2 = A2.alloc(NCH * 512)
                XNS = A2.alloc(4 * 512)
                l1_ = p.dma("gpsimd", ab_t[:, LNGB:LNGB + D],
                            bass.AP(I[f"f_lng_row{l}"].tensor, 0, [[0, 128], [1, D]]),
                            waits=[("ve", radd[3])])
                l2_ = p.dma("gpsimd", ab_t[:, LNBB:LNBB + D],
                            bass.AP(I[f"f_lnb_row{l}"].tensor, 0, [[0, 128], [1, D]]))
                sh_ct2, sh_d22, sh_mz2 = build_shift_full(
                    XBA2, XF, HSH2, 0, 8, extra0=[("ve", radd[11])])
                sh_ct2b, sh_d22b, _ = build_shift_full(
                    XBA2, XF, HSH2, 8, NCH, extra0=[("ve", radd[15])])
                lnt2 = None
                xn_done = None
                xn_th = {}
                for chalf in (0, 1):
                    for c in range(8 * chalf, 8 * chalf + 8):
                        p.v(lambda e, c=c: e.bn_stats(out=bn6[:, 0:6],
                                                      in_=ab_t[:, XBA2 + 512 * c:XBA2 + 512 * (c + 1)]),
                            waits=([("ve", radd[11]), ("q_gpsimd", sh_ct2)] if c == 0
                                   else ([("ve", lnt2), ("q_gpsimd", sh_ct2b)] if c == 8
                                         else [("ve", lnt2)])))
                        t2b = p.v(lambda e, c=c: e.bn_stats(
                            out=bn6[:, 6:12], in_=ab_t[:, HSH2 + 512 * c:HSH2 + 512 * (c + 1)]),
                            waits=([("q_gpsimd", sh_d22), ("ve", sh_mz2)] if c == 0
                                   else ([("q_gpsimd", sh_d22b)] if c == 8 else ())))
                        lnt2 = p.v(lambda e, c=c: e.bn_aggr(out=af_t[:, MV + 2 * c:MV + 2 * c + 2],
                                                            in_=bn6), waits=[("ve", t2b)])
                    rm2 = stats_rowmath(lnt2, False, nmu_off=NMU2, rsg_off=RSG2,
                                        c0=8 * chalf, c1=8 * chalf + 8)
                    for qr4 in (2 * chalf, 2 * chalf + 1):
                        for (half, SRCB, dsto) in ((0, XBA2, 0), (1, HSH2, HHALF)):
                            last = None
                            for j in range(4):
                                c = 4 * qr4 + j
                                t_ts = p.v(lambda e, c=c, j=j, SRCB=SRCB: e.tensor_scalar(
                                    out=ab_t[:, XNS + 512 * j:XNS + 512 * (j + 1)],
                                    in0=ab_t[:, SRCB + 512 * c:SRCB + 512 * (c + 1)],
                                    scalar1=af_t[:, NMU2 + c:NMU2 + c + 1],
                                    scalar2=af_t[:, RSG2 + c:RSG2 + c + 1],
                                    op0=ALU.add, op1=ALU.mult),
                                    waits=([("q_sync", xn_done), ("q_gpsimd", l2_),
                                            ("ve", rm2)] if j == 0 else []))
                                t_tm = p.v(lambda e, j=j, half=half: e.tensor_mul(
                                    out=ab_t[:, XNS + 512 * j:XNS + 512 * (j + 1)],
                                    in0=ab_t[:, XNS + 512 * j:XNS + 512 * (j + 1)],
                                    in1=ab_t[:, LNGB + 512 * half:LNGB + 512 * (half + 1)]),
                                    waits=[("ve", t_ts)])
                                last = p.v(lambda e, j=j, half=half: e.tensor_add(
                                    out=ab_t[:, XNS + 512 * j:XNS + 512 * (j + 1)],
                                    in0=ab_t[:, XNS + 512 * j:XNS + 512 * (j + 1)],
                                    in1=ab_t[:, LNBB + 512 * half:LNBB + 512 * (half + 1)]),
                                    waits=[("ve", t_tm)])
                            xn_done = p.dmat(
                                bass.AP(ab_t[:, 0:1].tensor, HFM2 + dsto + 2048 * qr4,
                                        [[A2.pstride, 128], [128, 16], [1, 128]]),
                                bass.AP(ab_t[:, 0:1].tensor, XNS,
                                        [[A2.pstride, 128], [128, 16], [1, 128]]),
                                waits=[("ve", last)])
                            if (qr4, half) in ((1, 1), (3, 1)):
                                xn_th[qr4 // 2] = xn_done
                A2.pop()

                A2.push()
                HB = A2.alloc(22 * D)
                HBL = A2.alloc(D)
                GSC = A2.alloc(4 * 512)
                WSL3 = A2.alloc(3 * 4096)
                b1a = af_t[:, B1A:B1A + 22]
                b1g = af_t[:, B1G:B1G + 22]
                lb1 = p.dma("scalar", b1a, I[f"f_b1a{l}"], waits=[("sc", p.now("sc")),
                                                                  ("ve", p.now("ve"))])
                lb2 = p.dma("scalar", b1g, I[f"f_b1g{l}"])
                mzo = p.v(lambda e: e.memset(ab_t[32:64, HBL:HBL + D], 1.0))
                for th in range(2):
                    wga = {}
                    te0_w = p.now("te")
                    hv_toks = {}
                    glu_hist = {}
                    for j in range(22):
                        rows = 128 if j < 21 else 42
                        mz_w = None
                        if j == 21:
                            mz_w = p.v(lambda e, j=j: e.memset(
                                ab_t[:, WSL3 + 4096 * (j % 3):WSL3 + 4096 * (j % 3) + 4096], 0.0),
                                waits=[("te", wga.get(j - 3, te0_w))])
                        wa = p.dma("gpsimd",
                                   bass.AP(ab_t[:, 0:1].tensor, WSL3 + 4096 * (j % 3),
                                           [[A2.pstride, 128], [256, NFC], [1, rows]]),
                                   bass.AP(I[f"f_w1{l}"].tensor, 128 * j,
                                           [[2 * HID, 128], [128 * 2 * HID, NFC], [1, rows]]),
                                   waits=[("te", wga.get(j - 3, te0_w)), ("ve", mz_w)])
                        wg = p.dma("gpsimd",
                                   bass.AP(ab_t[:, 0:1].tensor, WSL3 + 4096 * (j % 3) + 128,
                                           [[A2.pstride, 128], [256, NFC], [1, rows]]),
                                   bass.AP(I[f"f_w1{l}"].tensor, HID + 128 * j,
                                           [[2 * HID, 128], [128 * 2 * HID, NFC], [1, rows]]))
                        mm_l = None
                        for seg in range(2):
                            for (which, b0) in (("a", 0), ("g", 2)):
                                for f in range(NFC):
                                    rhs_ap = bass.AP(
                                        ab_t[:, 0:1].tensor,
                                        HFM2 + (HHALF if f >= 4 else 0)
                                        + 512 * (8 * th + 4 * seg) + 128 * (f % 4),
                                        [[A2.pstride, 128], [512, 4], [1, 128]])
                                    wofs = WSL3 + 4096 * (j % 3) + 256 * f + (0 if which == "a" else 128)
                                    mm_l = p.t(lambda e, rhs_ap=rhs_ap, wofs=wofs,
                                               b0=b0, seg=seg, f=f: e.matmul(
                                        out=bank(b0 + seg, 0, 128, 0, 512),
                                        lhsT=ab_t[:, wofs:wofs + 128],
                                        rhs=rhs_ap, start=(f == 0), stop=(f == NFC - 1)),
                                        waits=([("q_gpsimd", wg), ("q_sync", xn_th[th]),
                                                ("sc", glu_hist.get((j - 1, seg), 0)),
                                                ("ve", hv_toks.get((j - 1, seg), 0))]
                                               if (f == 0 and which == "a") else ()))
                        wga[j] = mm_l
                        for seg in range(2):
                            gslot = GSC + 512 * (2 * (j % 2) + seg)
                            gl = p.s(lambda e, j=j, seg=seg, rows=rows, gslot=gslot: e.activation(
                                out=ab_t[0:rows, gslot:gslot + 512], in_=bank(2 + seg, 0, rows, 0, 512),
                                func=AF.Gelu, bias=af_t[0:rows, B1G + j:B1G + j + 1]),
                                waits=[("te", wga[j]), ("q_scalar", lb2),
                                       ("ve", hv_toks.get((j - 2, seg), 0))])
                            dst = (HB + D * j) if j < 21 else HBL
                            hv_toks[(j, seg)] = p.v(
                                lambda e, seg=seg, rows=rows, dst=dst, j=j, gslot=gslot: e.scalar_tensor_tensor(
                                    out=ab_t[0:rows, dst + 512 * seg:dst + 512 * (seg + 1)],
                                    in0=bank(0 + seg, 0, rows, 0, 512),
                                    scalar=af_t[0:rows, B1A + j:B1A + j + 1],
                                    in1=ab_t[0:rows, gslot:gslot + 512],
                                    op0=ALU.add, op1=ALU.mult),
                                waits=[("sc", gl), ("q_scalar", lb1), ("ve", mzo)] +
                                      ([("q_sync", xn_done)] if j in (16, 17) else []))
                            glu_hist[(j, seg)] = gl
                    hb_done = p.now("ve")
                    for grp in range(2):
                        for oseg in range(2):
                            mmw = {}
                            te0_w = p.now("te")
                            for j in range(22):
                                rows = 128 if j < 21 else 43
                                w2l = p.dma("gpsimd",
                                            ab_t[0:rows, WSL3 + 4096 * (j % 3):WSL3 + 4096 * (j % 3) + 512],
                                            I[f"f_w2a{l}"][128 * j:128 * j + rows,
                                                           512 * oseg:512 * (oseg + 1)],
                                            waits=[("te", mmw.get(j - 3, te0_w)),
                                                   ("ve", hv_toks.get((j, 1), hb_done))])
                                for tt in range(4):
                                    if j < 21:
                                        lhsT = ab_t[:, HB + D * j + 512 * grp + 128 * tt:
                                                    HB + D * j + 512 * grp + 128 * (tt + 1)]
                                    else:
                                        lhsT = ab_t[0:43, HBL + 512 * grp + 128 * tt:
                                                    HBL + 512 * grp + 128 * (tt + 1)]
                                    mmw[j] = p.t(lambda e, j=j, tt=tt, rows=rows, lhsT=lhsT: e.matmul(
                                        out=bank(4 + tt, 0, 128, 0, 512),
                                        lhsT=lhsT,
                                        rhs=ab_t[0:rows, WSL3 + 4096 * (j % 3):WSL3 + 4096 * (j % 3) + 512],
                                        start=(j == 0), stop=(j == 21)),
                                        waits=(([("q_gpsimd", w2l)] +
                                                ([("ve", p.now("ve"))] if j == 0 else []))
                                               if tt == 0 else ()))
                            for tt in range(4):
                                cg = 8 * th + 4 * grp + tt
                                fadd_last = p.v(lambda e, cg=cg, tt=tt, oseg=oseg: e.tensor_add(
                                    out=af_t[:, XF + D * cg + 512 * oseg:XF + D * cg + 512 * (oseg + 1)],
                                    in0=af_t[:, XF + D * cg + 512 * oseg:XF + D * cg + 512 * (oseg + 1)],
                                    in1=bank(4 + tt, 0, 128, 0, 512)),
                                    waits=[("te", mmw[21])])
                A2.pop()
                A2.pop()
            for qr in range(4):
                ld = p.dma("scalar",
                           bass.AP(af_t[:, 0:1].tensor, WTMP0,
                                   [[A4.pstride, 128], [D, 4], [1, D]]),
                           x_src[512 * qr:512 * (qr + 1), :].rearrange("(j p) d -> p j d", p=128))
                ads = []
                for j in range(4):
                    c = 4 * qr + j
                    ads.append(p.v(lambda e, c=c, j=j: e.tensor_add(
                        out=xf(c), in0=af_t[:, WTMP0 + D * j:WTMP0 + D * (j + 1)], in1=xf(c)),
                        waits=([("q_scalar", ld)] +
                               ([("ve", fadd_last)] if qr == 0 else []))
                        if j == 0 else ()))
                p.dma("scalar",
                      x_dst[512 * qr:512 * (qr + 1), :].rearrange("(j p) d -> p j d", p=128),
                      bass.AP(af_t[:, 0:1].tensor, XF + 4 * D * qr,
                              [[A4.pstride, 128], [D, 4], [1, D]]),
                      waits=[("ve", ads[-1])])
            p.barrier()
            A4.pop(); A2.pop()

        def emit_attn_b(x_src, x_dst):
            A4.push(); A2.push()
            RNC = A4.alloc(40)
            RCPT = A4.alloc(256)
            YSC = A4.alloc(D)
            XQ = A4.alloc(16 * D)
            lnt = None
            for qr in range(4):
                sl = XQ + 4 * D * qr
                ld = p.dma("scalar",
                           bass.AP(af_t[:, 0:1].tensor, sl, [[A4.pstride, 128], [D, 4], [1, D]]),
                           x_src[512 * qr:512 * (qr + 1), :].rearrange("(j p) d -> p j d", p=128))
                for j in range(4):
                    c = 4 * qr + j
                    lnt = tm_stats(af_t[:, sl + D * j:sl + D * (j + 1)],
                                   af_t[:, MV + 2 * c:MV + 2 * c + 2],
                                   [("q_scalar", ld)] if j == 0 else [])
            stats_rowmath(lnt, False)
            p.barrier()
            A2.push()
            XNTM = A2.alloc(2 * D)
            XNFM = A2.alloc(NCH * D)
            WB = A2.alloc(NFC * D)
            BRW = A2.alloc(D)
            EVS = A2.alloc(2 * D)
            tr_xn = {}
            for c in range(NCH):
                sl = XNTM + D * (c % 2)
                tv = p.v(lambda e, c=c, sl=sl: e.tensor_scalar(
                    out=ab_t[:, sl:sl + D], in0=af_t[:, XQ + D * c:XQ + D * (c + 1)],
                    scalar1=af_t[:, NMU + c:NMU + c + 1], scalar2=af_t[:, RSG + c:RSG + c + 1],
                    op0=ALU.add, op1=ALU.mult),
                    waits=[("q_sync", tr_xn.get(c - 2, 0))])
                tr_xn[c] = p.dmat(bass.AP(ab_t[:, 0:1].tensor, XNFM + D * c,
                                          [[A2.pstride, 128], [128, 8], [1, 128]]),
                                  bass.AP(ab_t[:, 0:1].tensor, sl,
                                          [[A2.pstride, 128], [128, 8], [1, 128]]),
                                  waits=[("ve", tv)])
            for (wi, (wn, bn_, relu)) in enumerate((("b_WqT", "b_bq_row", True),
                                                    ("b_WkT", "b_bk_row", True),
                                                    ("b_WvT", "b_bv_row", False))):
                wld = p.dma("gpsimd", ab_t[:, WB:WB + NFC * D].rearrange("p (f d) -> p f d", f=NFC),
                            I[wn].rearrange("(f p) d -> p f d", p=128),
                            waits=[("te", p.now("te"))])
                bld = p.dma("gpsimd", ab_t[0:1, BRW:BRW + D], I[bn_])
                evt = {}
                svt = {}
                for c in range(NCH):
                    bnk = 2 * (c % 2)
                    mmt = None
                    for oseg in range(2):
                        for f in range(NFC):
                            mmt = p.t(lambda e, c=c, f=f, oseg=oseg, bnk=bnk: e.matmul(
                                out=bank(bnk + oseg, 0, 128, 0, 512),
                                lhsT=ab_t[:, XNFM + D * c + 128 * f:XNFM + D * c + 128 * (f + 1)],
                                rhs=ab_t[:, WB + D * f + 512 * oseg:WB + D * f + 512 * (oseg + 1)],
                                start=(f == 0), stop=False),
                                waits=([("q_gpsimd", bld), ("q_sync", tr_xn[NCH - 1]),
                                        ("ve", p.now("ve")), ("sc", p.now("sc"))]
                                       if (c == 0 and oseg == 0 and f == 0) else
                                       ([("ve", evt.get(c - 2, 0))]
                                        if (oseg == 0 and f == 0) else ())))
                            if f == NFC - 1:
                                mmt = p.t(lambda e, c=c, oseg=oseg, bnk=bnk: e.matmul(
                                    out=bank(bnk + oseg, 0, 128, 0, 512),
                                    lhsT=ones_row[0:1, 0:128],
                                    rhs=ab_t[0:1, BRW + 512 * oseg:BRW + 512 * (oseg + 1)],
                                    start=False, stop=True))
                    sl = EVS + D * (c % 2)
                    if relu:
                        evt[c] = p.v(lambda e, sl=sl, bnk=bnk: e.tensor_scalar_max(
                            out=ab_t[:, sl:sl + D], in0=PS[:, 512 * bnk:512 * bnk + 1024],
                            scalar1=0.0), waits=[("te", mmt), ("q_gpsimd", svt.get(c - 2, 0))])
                    else:
                        evt[c] = p.v(lambda e, sl=sl, bnk=bnk: e.tensor_copy(
                            out=ab_t[:, sl:sl + D], in_=PS[:, 512 * bnk:512 * bnk + 1024]),
                            waits=[("te", mmt), ("q_gpsimd", svt.get(c - 2, 0))])
                    svt[c] = p.dma("gpsimd", bscr[T * wi + 128 * c:T * wi + 128 * (c + 1), :],
                                   ab_t[:, sl:sl + D], waits=[("ve", evt[c])])
            p.barrier()
            A2.pop()
            A2.push()
            KST = A2.alloc(4 * 1024)
            VST = A2.alloc(4 * 544)
            KVB = A2.alloc(16 * 65)
            KVB2 = A2.alloc(16 * 65)
            KSR = A2.alloc(D)
            KBC = A2.alloc(D)
            mz1 = p.v(lambda e: e.memset(
                bass.AP(ab_t[:, 0:1].tensor, VST + 64, [[A2.pstride, 128], [544, 4], [65, 8], [1, 1]]), 1.0))
            mzk1 = p.v(lambda e: e.memset(ab_t[:, KVB:KVB + 16 * 65], 0.0))
            mzk2 = p.v(lambda e: e.memset(ab_t[:, KVB2:KVB2 + 16 * 65], 0.0))
            kv_ev = {}
            mms = {}
            te0b = p.now("te")
            for ps_ in range(2):
                for c in range(NCH):
                    ci = NCH * ps_ + c
                    sk = KST + 1024 * (ci % 4)
                    sv = VST + 544 * (ci % 4)
                    lk = p.dma("scalar", ab_t[:, sk:sk + 512],
                               bscr[T + 128 * c:T + 128 * (c + 1), 512 * ps_:512 * (ps_ + 1)],
                               waits=[("te", mms.get(ci - 4, te0b))])
                    lv = p.dma("sync",
                               bass.AP(ab_t[:, 0:1].tensor, sv, [[A2.pstride, 128], [65, 8], [1, 64]]),
                               bscr[2 * T + 128 * c:2 * T + 128 * (c + 1),
                                    512 * ps_:512 * (ps_ + 1)].rearrange("p (h d) -> p h d", h=8),
                               waits=[("ve", mz1), ("te", mms.get(ci - 4, te0b))])
                    for hh in range(8):
                        mms[ci] = p.t(lambda e, c=c, hh=hh, sk=sk, sv=sv: e.matmul(
                            out=bank(hh, 0, 64, 0, 65),
                            lhsT=ab_t[:, sk + 64 * hh:sk + 64 * (hh + 1)],
                            rhs=ab_t[:, sv + 65 * hh:sv + 65 * (hh + 1)],
                            start=(c == 0), stop=(c == NCH - 1)),
                            waits=([("q_scalar", lk), ("q_sync", lv)] if hh == 0 else ()))
                for hh in range(8):
                    h = 8 * ps_ + hh
                    kv_ev[h] = p.s(lambda e, h=h, hh=hh: e.copy(
                        out=ab_t[0:64, KVB + 65 * h:KVB + 65 * (h + 1)],
                        in_=bank(hh, 0, 64, 0, 65)),
                        waits=[("te", mms[NCH * ps_ + NCH - 1]), ("ve", mzk1)])
            QS2 = A2.alloc(2 * D)
            QF2 = A2.alloc(2 * D)
            TT2 = A2.alloc(D)
            GBC = A2.alloc(D)
            BBC = A2.alloc(D)
            OSC = A2.alloc(D)
            lg1 = p.dma("gpsimd", ab_t[:, GBC:GBC + D], I["b_g_bc"])
            lg2 = p.dma("gpsimd", ab_t[:, BBC:BBC + D], I["b_b_bc"])
            lqs = {}
            trq = {}
            for c in (0, 1):
                sq = QS2 + D * (c % 2)
                qf = QF2 + D * (c % 2)
                lqs[c] = p.dma("scalar", ab_t[:, sq:sq + D], bscr[128 * c:128 * (c + 1), :],
                               waits=[("te", te0b)])
                trq[c] = p.dmat(bass.AP(ab_t[:, 0:1].tensor, qf,
                                        [[A2.pstride, 128], [128, 8], [1, 128]]),
                                bass.AP(ab_t[:, 0:1].tensor, sq,
                                        [[A2.pstride, 128], [128, 8], [1, 128]]),
                                waits=[("q_scalar", lqs[c])])
            ktr = None
            for h in range(16):
                ktr = p.t(lambda e, h=h: e.matmul(
                    out=PS[0:1, 64 * h:64 * (h + 1)],
                    lhsT=ab_t[0:64, KVB + 65 * h + 64:KVB + 65 * h + 65],
                    rhs=id128[0:64, 0:64], start=True, stop=True),
                    waits=[("sc", p.now("sc"))] if h == 0 else ())
            kse = p.v(lambda e: e.tensor_copy(out=ab_t[0:1, KSR:KSR + D], in_=PS[0:1, 0:1024]),
                      waits=[("te", ktr)])
            kcp = p.dma("sync", bass.AP(ab_t[:, 0:1].tensor, KVB2 + A2.pstride * 64,
                                        [[A2.pstride, 64], [1, 16 * 65]]),
                        ab_t[0:64, KVB:KVB + 16 * 65],
                        waits=[("sc", kv_ev[15]), ("ve", mzk2)])
            bc1 = p.t(lambda e: e.matmul(out=bank(4, 0, 128, 0, 512),
                                         lhsT=ones_row[0:1, 0:128], rhs=ab_t[0:1, KSR:KSR + 512],
                                         start=True, stop=True), waits=[("ve", kse)])
            bc2 = p.t(lambda e: e.matmul(out=bank(5, 0, 128, 0, 512),
                                         lhsT=ones_row[0:1, 0:128],
                                         rhs=ab_t[0:1, KSR + 512:KSR + 1024],
                                         start=True, stop=True))
            kbe = p.v(lambda e: e.tensor_copy(out=ab_t[:, KBC:KBC + D], in_=PS[:, 2048:3072]),
                      waits=[("te", bc2)])
            p.barrier()
            st_prev = {}
            stb = {}
            for c in range(NCH):
                sq = QS2 + D * (c % 2)
                qf = QF2 + D * (c % 2)
                if c < 2:
                    lq = lqs[c]
                else:
                    lq = p.dma("scalar", ab_t[:, sq:sq + D], bscr[128 * c:128 * (c + 1), :],
                               waits=[("ve", st_prev.get(c - 2, 0)), ("q_sync", trq.get(c - 2, 0))])
                    trq[c] = p.dmat(bass.AP(ab_t[:, 0:1].tensor, qf,
                                            [[A2.pstride, 128], [128, 8], [1, 128]]),
                                    bass.AP(ab_t[:, 0:1].tensor, sq,
                                            [[A2.pstride, 128], [128, 8], [1, 128]]),
                                    waits=[("q_scalar", lq)])
                d1 = p.v(lambda e, sq=sq: e.tensor_tensor(
                    out=ab_t[:, TT2:TT2 + D], in0=ab_t[:, sq:sq + D], in1=ab_t[:, KBC:KBC + D],
                    op=ALU.mult), waits=[("q_scalar", lq), ("ve", kbe),
                                         ("q_gpsimd", lg2), ("ve", st_prev.get(c - 1, 0))])
                d2 = p.v(lambda e: e.reduce_sum(
                    out=af_t[:, RNC:RNC + 16],
                    in_=ab_t[:, TT2:TT2 + D].rearrange("p (h d) -> p h d", h=16),
                    axis=AX.X), waits=[("ve", d1)])
                d3 = p.v(lambda e: e.tensor_scalar_max(out=af_t[:, RNC:RNC + 16],
                                                       in0=af_t[:, RNC:RNC + 16], scalar1=1e-6),
                         waits=[("ve", d2)])
                d4 = p.v(lambda e, c=c: e.reciprocal(out=af_t[:, RCPT + 16 * c:RCPT + 16 * (c + 1)],
                                                     in_=af_t[:, RNC:RNC + 16]), waits=[("ve", d3)])
                b01 = 4 + 2 * (c % 2)
                mmc = None
                for h in range(16):
                    hp, par = h // 2, h % 2
                    kvsrc = (ab_t[:, KVB + 65 * h:KVB + 65 * h + 64] if par == 0 else
                             ab_t[:, KVB2 + 65 * h:KVB2 + 65 * h + 64])
                    mmc = p.t(lambda e, h=h, hp=hp, par=par, qf=qf, b01=b01, kvsrc=kvsrc: e.matmul(
                        out=bank(b01 + (h >= 8), 0, 128, 64 * (h % 8), 64),
                        lhsT=ab_t[:, qf + 128 * hp:qf + 128 * (hp + 1)],
                        rhs=kvsrc,
                        start=(h % 8 == 0), stop=(h % 8 == 7)),
                        waits=([("q_sync", trq[c]), ("ve", st_prev.get(c - 2, 0)),
                                ("sc", p.now("sc")), ("q_sync", kcp)] if h == 0 else ()))
                o1 = p.v(lambda e, c=c, b01=b01: e.tensor_tensor(
                    out=ab_t[:, OSC:OSC + D].rearrange("p (h d) -> p h d", h=16),
                    in0=PS[:, 512 * b01:512 * b01 + 1024].rearrange("p (h d) -> p h d", h=16),
                    in1=bass.AP(af_t[:, 0:1].tensor, RCPT + 16 * c,
                                [[A4.pstride, 128], [1, 16], [0, 64]]),
                    op=ALU.mult), waits=[("te", mmc), ("ve", d4)])
                x1 = p.v(lambda e, c=c: e.tensor_scalar(
                    out=af_t[:, YSC:YSC + D], in0=af_t[:, XQ + D * c:XQ + D * (c + 1)],
                    scalar1=af_t[:, NMU + c:NMU + c + 1], scalar2=af_t[:, RSG + c:RSG + c + 1],
                    op0=ALU.add, op1=ALU.mult),
                    waits=[("ve", o1)] +
                          ([("q_scalar", stb[c - 1])] if c == 1 else []))
                y1 = p.v(lambda e: e.tensor_add(out=af_t[:, YSC:YSC + D],
                                                in0=af_t[:, YSC:YSC + D],
                                                in1=ab_t[:, OSC:OSC + D]), waits=[("ve", x1)])
                bnt = tm_stats(af_t[:, YSC:YSC + D], af_t[:, MV + 2 * c:MV + 2 * c + 2],
                               [("ve", y1)])
                a1 = p.s(lambda e, c=c: e.activation(out=af_t[:, TSa:TSa + 1],
                                                     in_=af_t[:, MV + 2 * c + 1:MV + 2 * c + 2],
                                                     func=AF.Sqrt, bias=1e-5), waits=[("ve", bnt)])
                a2 = p.v(lambda e: e.reciprocal(out=af_t[:, TSb:TSb + 1], in_=af_t[:, TSa:TSa + 1]),
                         waits=[("sc", a1)])
                a3 = p.v(lambda e, c=c: e.tensor_scalar(
                    out=af_t[:, YSC:YSC + D], in0=af_t[:, YSC:YSC + D],
                    scalar1=af_t[:, MV + 2 * c:MV + 2 * c + 1], scalar2=af_t[:, TSb:TSb + 1],
                    op0=ALU.subtract, op1=ALU.mult), waits=[("ve", a2)])
                a4 = p.v(lambda e: e.tensor_mul(out=af_t[:, YSC:YSC + D],
                                                in0=af_t[:, YSC:YSC + D], in1=ab_t[:, GBC:GBC + D]),
                         waits=[("ve", a3)])
                a5 = p.v(lambda e: e.tensor_add(out=af_t[:, YSC:YSC + D],
                                                in0=af_t[:, YSC:YSC + D], in1=ab_t[:, BBC:BBC + D]),
                         waits=[("ve", a4)])
                a6 = p.v(lambda e, c=c: e.tensor_add(out=af_t[:, YSC:YSC + D],
                                                in0=af_t[:, YSC:YSC + D],
                                                in1=af_t[:, XQ + D * c:XQ + D * (c + 1)]),
                         waits=[("ve", a5)])
                st_prev[c] = a6
                stb[c] = p.dma("scalar", x_dst[128 * c:128 * (c + 1), :], af_t[:, YSC:YSC + D],
                               waits=[("ve", a6)])
            p.barrier()
            A2.pop(); A4.pop(); A2.pop()

        emit_attn_a(I["x"], out)
        if STAGE >= 2:
            emit_ffn(out, out, "a")
        if STAGE >= 3:
            emit_attn_b(out, out)
        if STAGE >= 4:
            emit_ffn(out, out, "b")

        for q in ("q_scalar", "q_sync", "q_gpsimd"):
            p.sem(q)
        p.g(lambda e: e.nop(), waits=[("q_scalar", p.now("q_scalar")),
                                      ("q_sync", p.now("q_sync")),
                                      ("q_gpsimd", p.now("q_gpsimd"))])
        with nc.Block() as block:
            p.finalize(block)
    return nc


_NC_CACHE = {}


def _get_nc():
    if "nc" not in _NC_CACHE:
        _NC_CACHE["nc"] = build_program()
    return _NC_CACHE["nc"]


def _cc(a):
    return np.ascontiguousarray(np.asarray(a, np.float32))


def kernel(**inp):
    nc = _get_nc()
    sh = {}
    sh["a_WqT"] = _cc(np.asarray(inp["L0_attn_a_Wq"]).T)
    sh["a_WkT"] = _cc(np.asarray(inp["L0_attn_a_Wk"]).T)
    sh["a_WvT"] = _cc(np.asarray(inp["L0_attn_a_Wv"]).T)
    sh["a_WoT"] = _cc(np.asarray(inp["L0_attn_a_Wo"]).T)
    sh["a_bwT"] = _cc(np.asarray(inp["L0_attn_a_beta_w"]).T)
    sh["a_rin"] = _cc(np.asarray(inp["L0_attn_a_rin_w"]).reshape(8, 128).T)
    sh["a_rk"] = _cc(np.asarray(inp["L0_attn_a_rk_w"]).reshape(8, 128).T)
    sh["a_rv"] = _cc(np.asarray(inp["L0_attn_a_rv_w"]).reshape(8, 128).T)
    sh["a_bo_row"] = _cc(np.asarray(inp["L0_attn_a_bo"])[None, :])
    sh["a_lnw_bc"] = _cc(np.broadcast_to(np.asarray(inp["L0_attn_a_ln_w"]), (128, D)))
    sh["a_lnb_bc"] = _cc(np.broadcast_to(np.asarray(inp["L0_attn_a_ln_b"]), (128, D)))
    sh["a_bb_col"] = _cc(np.asarray(inp["L0_attn_a_beta_b"])[:, None])
    sh["a_bbl_col"] = _cc(np.asarray(inp["L0_attn_a_base_beta_logit"])[:, None])
    sw = np.r_[np.arange(32, 64), np.arange(0, 32)]
    for l in range(2):
        sh[f"f_wqkv{l}"] = _cc(np.asarray(inp["L0_ffn_wqkv"])[l])
        sh[f"f_wo{l}"] = _cc(np.asarray(inp["L0_ffn_wo"])[l])
        sh[f"f_w1{l}"] = _cc(np.asarray(inp["L0_ffn_w1"])[l])
        w2 = np.asarray(inp["L0_ffn_w2"])[l]
        b2 = np.asarray(inp["L0_ffn_b2"])[l]
        sh[f"f_w2a{l}"] = _cc(np.concatenate([w2, b2[None, :]], axis=0))
        b1 = np.asarray(inp["L0_ffn_b1"])[l]
        b1a = np.zeros((22 * 128,), np.float32); b1a[:HID] = b1[:HID]
        b1g = np.zeros((22 * 128,), np.float32); b1g[:HID] = b1[HID:]
        sh[f"f_b1a{l}"] = _cc(b1a.reshape(22, 128).T)
        sh[f"f_b1g{l}"] = _cc(b1g.reshape(22, 128).T)
        sh[f"f_lng_row{l}"] = _cc(np.asarray(inp["L0_ffn_ln_g"])[l][None, :])
        sh[f"f_lnb_row{l}"] = _cc(np.asarray(inp["L0_ffn_ln_b"])[l][None, :])
        qs = np.asarray(inp["L0_ffn_q_scale"])[l]; ks = np.asarray(inp["L0_ffn_k_scale"])[l]
        sh[f"f_qs{l}_bc"] = _cc(np.broadcast_to(qs, (128, 64)))
        sh[f"f_ks{l}_bc"] = _cc(np.broadcast_to(ks, (128, 64)))
        sh[f"f_qs{l}_sw"] = _cc(np.broadcast_to(qs[sw], (128, 64)))
        sh[f"f_ks{l}_sw"] = _cc(np.broadcast_to(ks[sw], (128, 64)))
    sh["b_WqT"] = _cc(np.asarray(inp["L0_attn_b_Wq"]).T)
    sh["b_WkT"] = _cc(np.asarray(inp["L0_attn_b_Wk"]).T)
    sh["b_WvT"] = _cc(np.asarray(inp["L0_attn_b_Wv"]).T)
    sh["b_bq_row"] = _cc(np.asarray(inp["L0_attn_b_bq"])[None, :])
    sh["b_bk_row"] = _cc(np.asarray(inp["L0_attn_b_bk"])[None, :])
    sh["b_bv_row"] = _cc(np.asarray(inp["L0_attn_b_bv"])[None, :])
    sh["b_g_bc"] = _cc(np.broadcast_to(np.asarray(inp["L0_attn_b_gamma"]), (128, D)))
    sh["b_b_bc"] = _cc(np.broadcast_to(np.asarray(inp["L0_attn_b_beta"]), (128, D)))
    for k, v in CONSTS.items():
        sh[k] = v
    x = np.asarray(inp["L0_attn_a_x"])
    in_maps = [dict(sh, x=_cc(x[b])) for b in range(8)]
    res = run_bass_kernel_spmd(nc, in_maps, core_ids=list(range(8)))
    return np.stack([r["out"] for r in res.results], axis=0)
